# version 10
# baseline (speedup 1.0000x reference)
import numpy as np
import concourse.bass as bass
import concourse.mybir as mybir
from concourse.bass_utils import run_bass_kernel_spmd
from contextlib import ExitStack

F32 = mybir.dt.float32
BF16 = mybir.dt.bfloat16
AX = mybir.AxisListType.X
AF = mybir.ActivationFunctionType
OP = mybir.AluOpType

NCORES = 8
NB, S, EMB = 2, 4096, 2048
T = 1024
H, D = 16, 128
CH = 4
HCN = 16
NEXP = 32
EPS = 1e-5

C_ID = 0
C_RT = 128
C_M0 = 256
C_M1 = 512
C_ONE = 768
C_MT0 = 769
C_MT1 = 1025
C_SEL = 1281
CBF_W = C_SEL + NEXP * 128
CF_ID = 0
CF_ONE = 128
CF_ROW = 129
CF_EPS = 257
CF_W = 258


class Sem:
    def __init__(self, st, nc, name):
        self.h = st.enter_context(nc.semaphore(name))
        self.n = 0


class Prog:
    ENGS = ("t", "v", "s", "g", "y")

    def __init__(self):
        self.q = {e: [] for e in self.ENGS}

    def op(self, eng, build, inc=None, amt=1):
        if inc is None:
            self.q[eng].append(build)
            return None
        inc.n += amt
        self.q[eng].append(lambda E, b=build, s=inc.h, a=amt: b(E).then_inc(s, a))
        return inc.n

    def wait(self, eng, sem, val):
        if val is None:
            return
        self.q[eng].append(lambda E, s=sem.h, v=val: E.wait_ge(s, v))


def build_program(dbg=()):
    nc = bass.Bass("TRN2", target_bir_lowering=False,
                   detect_race_conditions=False)
    P = Prog()

    d_x = nc.dram_tensor("x", [EMB, T], F32, kind="ExternalInput").ap()
    d_cos = nc.dram_tensor("cos", [D, T], F32, kind="ExternalInput").ap()
    d_sin = nc.dram_tensor("sin", [D, T], F32, kind="ExternalInput").ap()
    d_pm = nc.dram_tensor("pmask", [128, 8], F32, kind="ExternalInput").ap()
    d_cbf = nc.dram_tensor("cbf", [128, CBF_W], F32, kind="ExternalInput").ap()
    d_cf = nc.dram_tensor("cf", [128, CF_W], F32, kind="ExternalInput").ap()
    d_gate = nc.dram_tensor("gate", [128, 512], F32, kind="ExternalInput").ap()
    d_wkvq = nc.dram_tensor("wkvq", [48, 128, 2048], F32, kind="ExternalInput").ap()
    d_wout = nc.dram_tensor("wout", [16, 128, 2048], F32, kind="ExternalInput").ap()
    d_wgu = nc.dram_tensor("wgu", [NEXP, 6, 128, 4096], F32, kind="ExternalInput").ap()
    d_wd = nc.dram_tensor("wd", [NEXP, 128, 12288], F32, kind="ExternalInput").ap()
    d_out = nc.dram_tensor("out", [EMB, T], F32, kind="ExternalOutput").ap()
    x_src = bass.AP(d_x.tensor, 0, [[T, 128], [128 * T, HCN], [1, T]])

    dbg_out = {}
    for name, shape in (("dbg_qf", [128, H * T]), ("dbg_kf", [128, H * T]),
                        ("dbg_vt", [128, H * T]), ("dbg_attn", [128, H * T])):
        if name in dbg:
            dbg_out[name] = nc.dram_tensor(name, shape, BF16, kind="ExternalOutput").ap()
    for name, shape, dt in (("dbg_scur", [128, H * D], F32),
                            ("dbg_x2", [EMB, T], F32),
                            ("dbg_ce", [NEXP, T], BF16),
                            ("dbg_den", [128, 128], F32),
                            ("dbg_am", [128, 640], BF16),
                            ("dbg_num", [128, 256], F32)):
        if name in dbg:
            dbg_out[name] = nc.dram_tensor(name, shape, dt, kind="ExternalOutput").ap()

    d_ccin = nc.dram_tensor("cc_in", [128, 2064], F32, kind="Internal")
    d_ccout = nc.dram_tensor("cc_out", [128 * NCORES, 2064], F32,
                             kind="Internal", addr_space="Shared")

    st = ExitStack()
    with st:
        def sb(name, shape, dt):
            return st.enter_context(nc.sbuf_tensor(name, shape, dt))

        p = [st.enter_context(nc.psum_tensor(f"p{i}", [128, 512], F32))
             for i in range(6)]
        pt = [st.enter_context(nc.psum_tensor(f"pt{i}", [128, 1024], BF16))
              for i in range(2)]

        sW = Sem(st, nc, "sW")
        sW3 = [Sem(st, nc, f"sW3_{i}") for i in range(3)]
        sGU2 = [Sem(st, nc, f"sGU2_{i}") for i in range(2)]
        sWD2 = [Sem(st, nc, f"sWD2_{i}") for i in range(2)]
        sPRE2 = [Sem(st, nc, f"sPRE2_{i}") for i in range(2)]
        sGU = Sem(st, nc, "sGU")
        sWD = Sem(st, nc, "sWD")
        sPRE = Sem(st, nc, "sPRE")
        sSTG = Sem(st, nc, "sSTG")
        sCC = Sem(st, nc, "sCC")
        sF = Sem(st, nc, "sF")
        sRES = Sem(st, nc, "sRES")
        sOUT = Sem(st, nc, "sOUT")
        cP = Sem(st, nc, "cP")
        cV = Sem(st, nc, "cV")
        cA = Sem(st, nc, "cA")

        cbf = sb("cbf_s", [128, CBF_W], BF16)
        cf = sb("cf_s", [128, CF_W], F32)
        gate_bf = sb("gate_s", [128, 512], BF16)
        pmask = sb("pmask_s", [128, 8], F32)
        wbufA = sb("wbufA", [128, 3 * 2048], BF16)
        qf = sb("qf", [128, H * T], BF16)
        fdv = sb("fdv", [128, 1], F32)
        fds = sb("fds", [128, 1], F32)
        S_cur = sb("S_cur", [128, H * D], F32)
        Z_cur = sb("Z_cur", [128, H], F32)

        def pefence(out_ap_fn):
            return P.op("t", lambda E, f=out_ap_fn: E.matmul(
                f(), cbf[:, C_ONE:C_ONE + 1], cbf[:, C_ONE:C_ONE + 1],
                start=True, stop=True), cP)

        def vfence():
            return P.op("v", lambda E: E.drain(), cV)

        def sfence():
            return P.op("s", lambda E: E.drain(), cA)

        P.op("y", lambda E: E.dma_start(out=cf[:, :], in_=d_cf), sF, 16)
        P.op("y", lambda E: E.dma_start(out=pmask[:, :], in_=d_pm), sF, 16)

        def g_load(dst_ap, src_ap, sem):
            return P.op("g", lambda E, d=dst_ap, s=src_ap: E.dma_start(out=d, in_=s),
                        sem, 16)

        m_wslot = {}
        m_qfin = {}
        nW_holder = [0]

        with ExitStack() as st2:
            def sb2(name, shape, dt):
                return st2.enter_context(nc.sbuf_tensor(name, shape, dt))

            cos_bf = sb2("cos_bf", [128, T], BF16)
            sin_bf = sb2("sin_bf", [128, T], BF16)
            kf = sb2("kf", [128, H * T], BF16)
            vt = sb2("vt", [128, H * T], BF16)
            S_run = sb2("S_run", [128, H * D], F32)
            Z_run = sb2("Z_run", [128, H], F32)
            kraw = [sb2(f"kraw{i}", [128, T], BF16) for i in range(2)]
            vraw = [sb2(f"vraw{i}", [128, 512], BF16) for i in range(2)]
            r1 = [sb2(f"r1_{i}", [128, 512], BF16) for i in range(2)]
            r2 = [sb2(f"r2_{i}", [128, 512], BF16) for i in range(2)]
            r3 = [sb2(f"r3_{i}", [128, 512], BF16) for i in range(2)]
            ktm = [sb2(f"ktm{i}", [128, 256], BF16) for i in range(2)]
            am_sb = [sb2(f"am{i}", [128, 640], BF16) for i in range(2)]
            ot = [sb2(f"ot{i}", [128, 256], BF16) for i in range(2)]
            sbf = [sb2(f"sbf{i}", [128, 128], BF16) for i in range(2)]
            zbf = [sb2(f"zbf{i}", [128, 1], BF16) for i in range(2)]
            den_c = sb2("den_c", [128, 32], F32)
            jnk = sb2("jnk", [128, 256], BF16)
            ztmp = sb2("ztmp", [128, 2], F32)
            pre_buf = [sb2("pre0", [128, 2064], F32)]
            ktmp = sb2("ktmp", [128, 2048], BF16)
            dbg_den_sb = sb2("dbg_den_sb", [128, 128], F32) if "dbg_den" in dbg_out else None
            dbg_am_sb = sb2("dbg_am_sb", [128, 640], BF16) if "dbg_am" in dbg_out else None
            dbg_num_sb = sb2("dbg_num_sb", [128, 256], F32) if "dbg_num" in dbg_out else None

            g_load(cbf[:, :], d_cbf, sW)
            g_load(cos_bf[:, :], d_cos, sW)
            g_load(sin_bf[:, :], d_sin, sW)
            g_load(gate_bf[:, :], d_gate, sW)

            P.op("v", lambda E: E.memset(S_run[:, :], 0.0))
            P.op("v", lambda E: E.memset(Z_run[:, :], 0.0))
            P.op("v", lambda E: E.memset(S_cur[:, :], 0.0))
            P.op("v", lambda E: E.memset(Z_cur[:, :], 0.0))
            m_memset = vfence()

            m_proj = {}
            m_kfin = {}
            m_vtev = {}
            m_Sadd = {}
            m_kact = {}

            with ExitStack() as st3:
                x_bf = st3.enter_context(
                    nc.sbuf_tensor("x_bf", [128, HCN * T], BF16))
                g_load(x_bf[:, :], x_src, sW)
                nW = 5

                def proj_group(ptile, wslot, half):
                    m = None
                    for hc in range(HCN):
                        def bld(E, hc=hc, ptile=ptile, wslot=wslot, half=half):
                            return E.matmul(
                                p[ptile][:, :],
                                wbufA[:, wslot * 2048 + hc * 128:
                                      wslot * 2048 + hc * 128 + 128],
                                x_bf[:, hc * T + half * 512:hc * T + half * 512 + 512],
                                start=(hc == 0), stop=(hc == HCN - 1))
                        if hc == HCN - 1:
                            m = P.op("t", bld, cP)
                        else:
                            P.op("t", bld)
                    return m

                def rope_elu(h, half, dst, me_prev_needed):
                    sl = half
                    g2 = half
                    mrh = P.op("t", lambda E, g2=g2, h=h, half=half:
                               E.matmul(p[2 + g2][:, :], cbf[:, C_RT:C_RT + 128],
                                        kraw[h % 2][:, half * 512:half * 512 + 512],
                                        start=True, stop=True), cP)
                    P.wait("v", cP, mrh)
                    P.op("v", lambda E, sl=sl, h=h, half=half:
                         E.tensor_mul(r1[sl][:, :],
                                      kraw[h % 2][:, half * 512:half * 512 + 512],
                                      cos_bf[:, half * 512:half * 512 + 512]))
                    P.op("v", lambda E, sl=sl, g2=g2, half=half:
                         E.tensor_mul(r3[sl][:, :], p[2 + g2][:, :],
                                      sin_bf[:, half * 512:half * 512 + 512]))
                    m_rope = P.op("v", lambda E, sl=sl:
                                  E.tensor_add(r2[sl][:, :], r1[sl][:, :],
                                               r3[sl][:, :]), cV)
                    P.wait("s", cV, m_rope)
                    mex = P.op("s", lambda E, sl=sl:
                               E.activation(r1[sl][:, :], r2[sl][:, :], AF.Exp), cA)
                    P.op("v", lambda E, sl=sl:
                         E.tensor_scalar_max(r3[sl][:, :], r2[sl][:, :], 0.0))
                    P.wait("v", cA, mex)
                    P.op("v", lambda E, sl=sl:
                         E.tensor_scalar_min(r1[sl][:, :], r1[sl][:, :], 1.0))
                    P.op("v", lambda E, sl=sl, dst=dst:
                         E.tensor_add(dst, r1[sl][:, :], r3[sl][:, :]))
                    return vfence(), mrh

                for ob in range(16):
                    h = ob
                    if ob >= 3:
                        P.wait("g", cP, m_wslot[ob - 3])
                    g_load(wbufA[:, (ob % 3) * 2048:(ob % 3) * 2048 + 2048],
                           d_wkvq[ob], sW3[ob % 3])
                    if ob == 0:
                        P.wait("t", sW, 16 * nW)
                    P.wait("t", sW3[ob % 3], sW3[ob % 3].n)
                    for half in range(2):
                        g2 = half
                        m = proj_group(g2, ob % 3, half)
                        m_proj[(ob, half)] = m
                        P.wait("s", cP, m)
                        P.op("s", lambda E, g2=g2:
                             E.copy(vraw[g2][:, :], p[g2][:, :]))
                        mv = sfence()
                        P.wait("t", cA, mv)
                        if ob >= 1:
                            P.wait("t", cV, m_vtev[(ob - 1, half)])
                        for tb in range(4):
                            P.op("t", lambda E, g2=g2, tb=tb:
                                 E.transpose(pt[g2][:, tb * 128:tb * 128 + 128],
                                             vraw[g2][:, tb * 128:tb * 128 + 128],
                                             cbf[:, C_ID:C_ID + 128]))
                        mtr = pefence(lambda: p[2 + g2][0:1, 0:1])
                        P.wait("v", cP, mtr)
                        P.op("v", lambda E, g2=g2, h=h, half=half:
                             E.tensor_copy(
                                 vt[:, h * T + half * 512:h * T + half * 512 + 512],
                                 pt[g2][:, 0:512]))
                        mvt = vfence()
                        m_vtev[(h, half)] = mvt
                    m_wslot[ob] = m_proj[(ob, 1)]
                m_vt_all = max(m_vtev.values())

                for ob in range(16, 32):
                    h = ob - 16
                    P.wait("g", cP, m_wslot[ob - 3])
                    g_load(wbufA[:, (ob % 3) * 2048:(ob % 3) * 2048 + 2048],
                           d_wkvq[ob], sW3[ob % 3])
                    P.wait("t", sW3[ob % 3], sW3[ob % 3].n)
                    for half in range(2):
                        g2 = half
                        m = proj_group(g2, ob % 3, half)
                        m_proj[(ob, half)] = m
                        P.wait("s", cP, m)
                        if h >= 2:
                            P.wait("s", cV, m_kfin[(h - 2, half)])
                        P.op("s", lambda E, h=h, g2=g2, half=half:
                             E.copy(kraw[h % 2][:, half * 512:half * 512 + 512],
                                    p[g2][:, :]))
                        me = sfence()
                        P.wait("t", cA, me)
                        mk, _ = rope_elu(
                            h, half,
                            kf[:, h * T + half * 512:h * T + half * 512 + 512], None)
                        m_kfin[(h, half)] = mk
                        mzr = P.op("v", lambda E, h=h, half=half:
                                   E.reduce_sum(ztmp[:, 0:1],
                                                kf[:, h * T + half * 512:
                                                   h * T + half * 512 + 512],
                                                AX), cV)
                        P.wait("v", cV, mzr)
                        P.op("v", lambda E, h=h:
                             E.tensor_add(Z_run[:, h:h + 1], Z_run[:, h:h + 1],
                                          ztmp[:, 0:1]), cV)
                        for ci in range(2):
                            c = half * 2 + ci
                            cpar = ci
                            P.wait("t", cV, mk)
                            for mb in range(2):
                                P.op("t", lambda E, h=h, c=c, mb=mb, cpar=cpar:
                                     E.transpose(
                                         pt[cpar][:, mb * 128:mb * 128 + 128],
                                         kf[:, h * T + c * 256 + mb * 128:
                                            h * T + c * 256 + mb * 128 + 128],
                                         cbf[:, C_ID:C_ID + 128]))
                            mtr = pefence(lambda cpar=cpar: p[4 + cpar][0:1, 384:385])
                            P.wait("v", cP, mtr)
                            P.op("v", lambda E, cpar=cpar:
                                 E.tensor_copy(ktm[cpar][:, :],
                                               pt[cpar][:, 0:256]))
                            mke = vfence()
                            P.wait("t", cV, mke)
                            P.wait("t", cV, m_vt_all)
                            msm = None
                            for mb in range(2):
                                def bmm(E, h=h, c=c, mb=mb, cpar=cpar):
                                    return E.matmul(
                                        p[4 + cpar][:, 0:128],
                                        ktm[cpar][:, mb * 128:mb * 128 + 128],
                                        vt[:, h * T + (c * 2 + mb) * 128:
                                           h * T + (c * 2 + mb) * 128 + 128],
                                        start=(mb == 0), stop=(mb == 1))
                                if mb == 1:
                                    msm = P.op("t", bmm, cP)
                                else:
                                    P.op("t", bmm)
                            P.wait("v", cP, msm)
                            msa = P.op("v", lambda E, h=h, cpar=cpar:
                                       E.tensor_add(S_run[:, h * D:h * D + 128],
                                                    S_run[:, h * D:h * D + 128],
                                                    p[4 + cpar][:, 0:128]), cV)
                            m_Sadd[(h, c)] = msa
                    m_wslot[ob] = m_proj[(ob, 1)]
                m_states_done = vfence()

                P.wait("g", cV, m_states_done)
                P.op("g", lambda E: E.dma_start(out=d_ccin[:, 0:2048],
                                                in_=S_run[:, :]), sSTG, 16)
                P.op("g", lambda E: E.dma_start(out=d_ccin[:, 2048:2064],
                                                in_=Z_run[:, :]), sSTG, 16)
                P.wait("g", sSTG, 32)
                P.op("g", lambda E: E.collective_compute(
                    "AllGather", OP.bypass,
                    replica_groups=[list(range(NCORES))],
                    ins=[d_ccin.ap().opt()],
                    outs=[d_ccout.ap().opt()]), sCC, 1)

                def q_pair(pi):
                    for i2 in range(2):
                        ob = 32 + 2 * pi + i2
                        h = ob - 32
                        P.wait("g", cP, m_wslot[ob - 3])
                        g_load(wbufA[:, (ob % 3) * 2048:(ob % 3) * 2048 + 2048],
                               d_wkvq[ob], sW3[ob % 3])
                        P.wait("t", sW3[ob % 3], sW3[ob % 3].n)
                        for half in range(2):
                            g2 = half
                            m = proj_group(g2, ob % 3, half)
                            m_proj[(ob, half)] = m
                            P.wait("s", cP, m)
                            if h >= 2:
                                P.wait("s", cV, m_qfin[(h - 2, half)])
                            else:
                                P.wait("s", cV, m_kfin[(15, half)])
                            P.op("s", lambda E, h=h, g2=g2, half=half:
                                 E.copy(kraw[h % 2][:, half * 512:half * 512 + 512],
                                        p[g2][:, :]))
                            me = sfence()
                            m_kact[(h, half)] = me
                            P.wait("t", cA, me)
                            mq, _ = rope_elu(
                                h, half,
                                qf[:, h * T + half * 512:h * T + half * 512 + 512],
                                None)
                            m_qfin[(h, half)] = mq
                        m_wslot[ob] = m_proj[(ob, 1)]

                q_pair(0)
                q_pair(1)
                q_pair(2)

                P.wait("g", sCC, 1)
                m_pre = {}
                for r in range(NCORES):
                    if r >= 1:
                        P.wait("g", cV, m_pre[r - 1])
                    P.op("g", lambda E, r=r: E.dma_start(
                        out=pre_buf[0][:, :],
                        in_=d_ccout.ap()[r * 128:(r + 1) * 128, :]), sPRE2[r % 2], 16)
                    P.wait("v", sPRE2[r % 2], 16 * (r // 2 + 1))
                    if r == 0:
                        P.wait("v", sF, 32)
                    P.op("v", lambda E, r=r: E.scalar_tensor_tensor(
                        S_cur[:, :], pre_buf[0][:, 0:2048], pmask[:, r:r + 1],
                        S_cur[:, :], OP.mult, OP.add))
                    m_pre[r] = P.op("v", lambda E, r=r: E.scalar_tensor_tensor(
                        Z_cur[:, :], pre_buf[0][:, 2048:2064], pmask[:, r:r + 1],
                        Z_cur[:, :], OP.mult, OP.add), cV)
                m_prefix_done = vfence()

                if "dbg_scur" in dbg_out:
                    P.wait("y", cV, m_prefix_done)
                    P.op("y", lambda E: E.dma_start(out=dbg_out["dbg_scur"],
                                                    in_=S_cur[:, :]), sOUT, 16)

                m_den = {}
                m_N = {}
                m_ot = {}
                m_fm = {}
                m_ke = {}
                m_kv = {}
                m_SC = {}
                seq = [(c, pi) for pi in range(8) for c in range(CH)]
                pos_of = {k: n for n, k in enumerate(seq)}

                def prev(key, k):
                    n = pos_of[key] - k
                    return seq[n] if n >= 0 else None

                def get(dic, key):
                    return dic[key] if key is not None and key in dic else None

                m_ktmp = {}

                def ktm_hoist(pi):
                    hs = (2 * pi, 2 * pi + 1)
                    P.wait("t", cA, get(m_fm, (3, pi - 1) if pi else None))
                    for i2, h in enumerate(hs):
                        for c4 in range(CH):
                            for mb in range(2):
                                P.op("t", lambda E, h=h, c4=c4, mb=mb, i2=i2:
                                     E.transpose(
                                         pt[i2][:, (c4 * 2 + mb) * 128:
                                                (c4 * 2 + mb) * 128 + 128],
                                         kf[:, h * T + c4 * 256 + mb * 128:
                                            h * T + c4 * 256 + mb * 128 + 128],
                                         cbf[:, C_ID:C_ID + 128]))
                    mhtr = pefence(lambda: p[5][0:1, 502:503])
                    P.wait("s", cP, mhtr)
                    for i2 in range(2):
                        P.op("s", lambda E, i2=i2:
                             E.copy(ktmp[:, i2 * 1024:i2 * 1024 + 1024],
                                    pt[i2][:, 0:1024]))
                    m_ktmp[pi] = sfence()

                def attn_pair(c, pi):
                    key = (c, pi)
                    hs = (2 * pi, 2 * pi + 1)
                    P.wait("v", cP, get(m_N, prev(key, 1)))
                    for i2, h in enumerate(hs):
                        P.op("v", lambda E, h=h, i2=i2:
                             E.tensor_copy(sbf[i2][:, :], S_cur[:, h * D:h * D + 128]))
                        P.op("v", lambda E, h=h, i2=i2:
                             E.tensor_copy(zbf[i2][:, :], Z_cur[:, h:h + 1]))
                    mcz = vfence()
                    P.wait("t", cV, mcz)
                    P.wait("t", cV, get(m_den, prev(key, 1)))
                    mA = None
                    for i2, h in enumerate(hs):
                        for lb in range(2):
                            ptile = i2 * 2 + lb
                            P.op("t", lambda E, h=h, c=c, lb=lb, ptile=ptile:
                                 E.matmul(p[ptile][:, 0:256],
                                          qf[:, h * T + c * 256 + lb * 128:
                                             h * T + c * 256 + lb * 128 + 128],
                                          kf[:, h * T + c * 256:h * T + c * 256 + 256],
                                          start=True, stop=True))
                            mA = P.op("t", lambda E, h=h, c=c, lb=lb, ptile=ptile:
                                      E.matmul(p[ptile][:, 256:512],
                                               kf[:, h * T + c * 256 + lb * 128:
                                                  h * T + c * 256 + lb * 128 + 128],
                                               qf[:, h * T + c * 256:
                                                  h * T + c * 256 + 256],
                                               start=True, stop=True), cP)
                    P.wait("v", cP, mA)
                    P.wait("v", cP, get(m_N, prev(key, 1)))
                    for i2, h in enumerate(hs):
                        for lb in range(2):
                            ptile = i2 * 2 + lb
                            col = h * 2 + lb
                            moff = C_M0 if lb == 0 else C_M1
                            P.op("v", lambda E, ptile=ptile, moff=moff, col=col:
                                 E.scalar_tensor_tensor(
                                     jnk[:, :], p[ptile][:, 0:256], 1.0,
                                     cbf[:, moff:moff + 256], OP.bypass, OP.mult,
                                     accum_out=den_c[:, col:col + 1]))
                    m_amv = None
                    for i2, h in enumerate(hs):
                        for mb in range(2):
                            ptile = i2 * 2 + mb
                            mtoff = C_MT0 if mb == 0 else C_MT1
                            P.op("v", lambda E, ptile=ptile, i2=i2, mtoff=mtoff, mb=mb:
                                 E.tensor_mul(am_sb[i2][:, mb * 320:mb * 320 + 256],
                                              p[ptile][:, 256:512],
                                              cbf[:, mtoff:mtoff + 256]))
                    m_amv = vfence()
                    if dbg_am_sb is not None and key == (0, 0):
                        P.op("v", lambda E: E.tensor_copy(dbg_am_sb[:, :],
                                                          am_sb[0][:, :]), cV)
                    P.wait("t", cV, m_amv)
                    P.wait("t", cV, get(m_SC, prev(key, 1)))
                    P.wait("t", cA, get(m_ot, prev(key, 1)))
                    mN = None
                    for i2, h in enumerate(hs):
                        np_ = 4 + i2
                        for lb in range(2):
                            for mb in range(2):
                                P.op("t", lambda E, h=h, c=c, lb=lb, mb=mb, np_=np_,
                                     i2=i2:
                                     E.matmul(p[np_][:, lb * 128:lb * 128 + 128],
                                              am_sb[i2][:, mb * 320 + lb * 128:
                                                        mb * 320 + lb * 128 + 128],
                                              vt[:, h * T + (c * 2 + mb) * 128:
                                                 h * T + (c * 2 + mb) * 128 + 128],
                                              start=(mb == 0), stop=False))
                            P.op("t", lambda E, h=h, c=c, lb=lb, np_=np_, i2=i2:
                                 E.matmul(p[np_][:, lb * 128:lb * 128 + 128],
                                          qf[:, h * T + c * 256 + lb * 128:
                                             h * T + c * 256 + lb * 128 + 128],
                                          sbf[i2][:, :],
                                          start=False, stop=True))
                            mN = P.op("t", lambda E, h=h, c=c, lb=lb, np_=np_, i2=i2:
                                      E.matmul(p[np_][:, 300 + lb:301 + lb],
                                               qf[:, h * T + c * 256 + lb * 128:
                                                  h * T + c * 256 + lb * 128 + 128],
                                               zbf[i2][:, :], start=True, stop=True),
                                      cP)
                    m_N[key] = mN
                    P.wait("v", cP, mN)
                    for i2, h in enumerate(hs):
                        for lb in range(2):
                            col = h * 2 + lb
                            P.op("v", lambda E, col=col, np_=4 + i2, lb=lb:
                                 E.tensor_add(den_c[:, col:col + 1],
                                              den_c[:, col:col + 1],
                                              p[np_][:, 300 + lb:301 + lb]))
                    P.op("v", lambda E: E.drain())
                    cols0 = 4 * pi
                    mmx = P.op("v", lambda E, cols0=cols0:
                               E.tensor_scalar_max(den_c[:, cols0:cols0 + 4],
                                                   den_c[:, cols0:cols0 + 4], 1e-6),
                               cV)
                    P.wait("v", cV, mmx)
                    P.op("v", lambda E, cols0=cols0:
                         E.reciprocal(den_c[:, cols0:cols0 + 4],
                                      den_c[:, cols0:cols0 + 4]))
                    mden = vfence()
                    m_den[key] = mden
                    P.wait("s", cP, mN)
                    P.wait("s", cV, mden)
                    if dbg_num_sb is not None and key == (0, 0):
                        P.op("v", lambda E: E.tensor_copy(dbg_num_sb[:, 0:128],
                                                          p[4][:, 0:128]), cV)
                        P.op("v", lambda E: E.tensor_copy(dbg_num_sb[:, 128:256],
                                                          p[4][:, 128:256]), cV)
                    for i2, h in enumerate(hs):
                        for lb in range(2):
                            col = h * 2 + lb
                            P.op("s", lambda E, np_=4 + i2, col=col, i2=i2, lb=lb:
                                 E.activation(ot[i2][:, lb * 128:lb * 128 + 128],
                                              p[np_][:, lb * 128:lb * 128 + 128],
                                              AF.Copy, scale=den_c[:, col:col + 1]))
                    mot = sfence()
                    m_ot[key] = mot
                    P.wait("t", cA, mot)
                    for i2 in range(2):
                        for lb in range(2):
                            P.op("t", lambda E, i2=i2, lb=lb:
                                 E.transpose(
                                     pt[i2][:, 512 + lb * 128:512 + lb * 128 + 128],
                                     ot[i2][:, lb * 128:lb * 128 + 128],
                                     cbf[:, C_ID:C_ID + 128]))
                    mtr2 = pefence(lambda: p[4][0:1, 500:501])
                    P.wait("s", cP, mtr2)
                    for i2, h in enumerate(hs):
                        P.op("s", lambda E, h=h, c=c, i2=i2:
                             E.copy(qf[:, h * T + c * 256:h * T + c * 256 + 256],
                                    pt[i2][:, 512:768]))
                    mfm = sfence()
                    m_fm[key] = mfm
                    P.wait("t", cA, m_ktmp[pi])
                    P.wait("t", cA, mot)
                    P.wait("t", cV, mden)
                    mkv = None
                    for i2, h in enumerate(hs):
                        for mb in range(2):
                            def bmm(E, h=h, c=c, mb=mb, i2=i2):
                                return E.matmul(
                                    p[4 + i2][:, 352:480],
                                    ktmp[:, i2 * 1024 + (c * 2 + mb) * 128:
                                         i2 * 1024 + (c * 2 + mb) * 128 + 128],
                                    vt[:, h * T + (c * 2 + mb) * 128:
                                       h * T + (c * 2 + mb) * 128 + 128],
                                    start=(mb == 0), stop=(mb == 1))
                            if i2 == 1 and mb == 1:
                                mkv = P.op("t", bmm, cP)
                            else:
                                P.op("t", bmm)
                    m_kv[key] = mkv
                    for i2, h in enumerate(hs):
                        P.op("v", lambda E, h=h, c=c, i2=i2:
                             E.reduce_sum(ztmp[:, i2:i2 + 1],
                                          kf[:, h * T + c * 256:h * T + c * 256 + 256],
                                          AX))
                    P.wait("v", cP, mkv)
                    for i2, h in enumerate(hs):
                        P.op("v", lambda E, h=h, i2=i2:
                             E.tensor_add(S_cur[:, h * D:h * D + 128],
                                          S_cur[:, h * D:h * D + 128],
                                          p[4 + i2][:, 352:480]))
                    for i2, h in enumerate(hs):
                        P.op("v", lambda E, h=h, i2=i2:
                             E.tensor_add(Z_cur[:, h:h + 1], Z_cur[:, h:h + 1],
                                          ztmp[:, i2:i2 + 1]))
                    msz = vfence()
                    m_SC[key] = msz

                P.wait("t", cV, m_prefix_done)
                P.wait("t", cV, max(m_kfin.values()))
                P.wait("t", cV, m_vt_all)
                for pi in range(8):
                    P.wait("t", cV, m_qfin[(2 * pi + 1, 1)])
                    if pi >= 1 and (2 * (pi + 2) + 1, 1) in m_qfin:
                        P.wait("t", cV, m_qfin[(2 * (pi + 2) + 1, 1)])
                        P.wait("t", cA, m_kact[(2 * (pi + 2) + 1, 1)])
                    ktm_hoist(pi)
                    for c in range(CH):
                        attn_pair(c, pi)
                    if pi + 3 < 8:
                        q_pair(pi + 3)

            m_attn_done = m_fm[(CH - 1, 7)]
            m_attn_dve = vfence()
            m_attn_act = sfence()

            for nm, sbv in (("dbg_den", dbg_den_sb), ("dbg_am", dbg_am_sb),
                            ("dbg_num", dbg_num_sb)):
                if nm in dbg_out and sbv is not None:
                    P.wait("y", cV, m_attn_dve)
                    P.op("y", lambda E, nm=nm, sbv=sbv: E.dma_start(
                        out=dbg_out[nm], in_=sbv[:, :]), sOUT, 16)
            if "dbg_attn" in dbg_out:
                P.wait("y", cA, m_attn_done)
                P.op("y", lambda E: E.dma_start(out=dbg_out["dbg_attn"],
                                                in_=qf[:, :]), sOUT, 16)

        with ExitStack() as st4:
            def sb4(name, shape, dt):
                return st4.enter_context(nc.sbuf_tensor(name, shape, dt))

            resid = sb4("resid", [128, HCN * T], F32)
            wgu_buf = sb4("wgu_buf", [128, 2 * 4096], BF16)
            wd_buf = sb4("wd_buf", [128, 2 * 1536], BF16)
            sq_sb = sb4("sq_sb", [128, 512], F32)
            stat1 = sb4("stat1", [1, 1024], F32)
            stat2 = sb4("stat2", [1, 1024], F32)
            nmbB = sb4("nmbB", [128, 1024], F32)
            rbB = sb4("rbB", [128, 1024], F32)
            lg_sb = sb4("lg_sb", [32, 1024], BF16)
            g_tm = sb4("g_tm", [128, 256], F32)
            p_sb = sb4("p_sb", [128, 256], F32)
            w32 = [sb4(f"w32_{i}", [128, 32], F32) for i in range(8)]
            red = sb4("red", [128, 8], F32)
            ce_fm = sb4("ce_fm", [NEXP, T], BF16)
            ce_b = [sb4(f"ce_b{i}", [128, T], BF16) for i in range(2)]
            h_buf = [sb4(f"h_buf{i}", [128, 6 * T], BF16) for i in range(2)]
            sg_sb = [sb4(f"sg{i}", [128, 512], BF16) for i in range(2)]
            usc_sb = [sb4(f"usc{i}", [128, 512], BF16) for i in range(2)]

            EMBf = float(EMB)

            P.wait("y", cV, m_attn_dve)
            P.wait("y", cA, m_attn_act)
            for i in range(4):
                src_i = bass.AP(d_x.tensor, i * 4 * 128 * T,
                                [[T, 128], [128 * T, 4], [1, T]])
                P.op("y", lambda E, i=i, src_i=src_i: E.dma_start(
                    out=resid[:, i * 4 * T:(i + 1) * 4 * T],
                    in_=src_i), sRES, 16)

            m_resid = {}
            m_sqm = {}
            prev_sq_pe = None
            P.wait("t", cV, m_attn_dve)
            P.wait("t", cA, m_attn_act)
            for ob in range(16):
                guard = m_wslot[ob - 3] if ob >= 3 else m_wslot[45 + ob]
                P.wait("g", cP, guard)
                g_load(wbufA[:, (ob % 3) * 2048:(ob % 3) * 2048 + 2048],
                       d_wout[ob], sW3[ob % 3])
                P.wait("t", sW3[ob % 3], sW3[ob % 3].n)
                for half in range(2):
                    g2 = half
                    if ob >= 1:
                        P.wait("t", cV, m_resid[(ob - 1, half)])
                    mwo = None
                    for ec in range(HCN):
                        def bld(E, ec=ec, g2=g2, ob=ob, half=half):
                            return E.matmul(
                                p[g2][:, :],
                                wbufA[:, (ob % 3) * 2048 + ec * 128:
                                      (ob % 3) * 2048 + ec * 128 + 128],
                                qf[:, ec * T + half * 512:ec * T + half * 512 + 512],
                                start=(ec == 0), stop=(ec == HCN - 1))
                        if ec == HCN - 1:
                            mwo = P.op("t", bld, cP)
                        else:
                            P.op("t", bld)
                    m_wslot[("wo", ob)] = mwo
                    P.wait("v", cP, mwo)
                    P.wait("v", sRES, 64)
                    P.op("v", lambda E, ob=ob, half=half, g2=g2:
                         E.tensor_add(
                             resid[:, ob * T + half * 512:ob * T + half * 512 + 512],
                             resid[:, ob * T + half * 512:ob * T + half * 512 + 512],
                             p[g2][:, :]))
                    mr = vfence()
                    m_resid[(ob, half)] = mr
                    P.wait("s", cV, mr)
                    P.wait("s", cP, prev_sq_pe)
                    P.op("s", lambda E, ob=ob, half=half:
                         E.activation(sq_sb[:, :],
                                      resid[:, ob * T + half * 512:
                                            ob * T + half * 512 + 512],
                                      AF.Square))
                    msq = sfence()
                    P.wait("t", cV, mr)
                    P.op("t", lambda E, ob=ob, half=half:
                         E.matmul(p[4 + half][0:1, :], cf[:, CF_ONE:CF_ONE + 1],
                                  resid[:, ob * T + half * 512:ob * T + half * 512 + 512],
                                  start=(ob == 0), stop=(ob == 15)))
                    P.wait("t", cA, msq)
                    msqm = P.op("t", lambda E, ob=ob, half=half:
                                E.matmul(p[2 + half][0:1, :], cf[:, CF_ONE:CF_ONE + 1],
                                         sq_sb[:, :],
                                         start=(ob == 0), stop=(ob == 15)), cP)
                    m_sqm[(ob, half)] = msqm
                    prev_sq_pe = msqm
                m_wslot[ob] = m_wslot[("wo", ob)]
            m_stats = max(m_sqm.values())

            def ln_finalize_apply(m_statmm, write_x2b, sum_b=4, nm_b=0):
                P.wait("v", cP, m_statmm)
                for half in range(2):
                    P.op("v", lambda E, half=half, sum_b=sum_b:
                         E.tensor_copy(stat1[:, half * 512:half * 512 + 512],
                                       p[sum_b + half][0:1, :]))
                    P.op("v", lambda E, half=half:
                         E.tensor_copy(stat2[:, half * 512:half * 512 + 512],
                                       p[2 + half][0:1, :]))
                P.op("v", lambda E: E.tensor_scalar_mul(stat1[:, :], stat1[:, :],
                                                        float(-1.0 / EMBf)))
                P.op("v", lambda E: E.tensor_scalar_mul(stat2[:, :], stat2[:, :],
                                                        float(1.0 / EMBf)))
                mrow = P.op("v", lambda E: E.tensor_mul(nmbB[0:1, 0:1024],
                                                        stat1[:, :], stat1[:, :]),
                            cV)
                P.wait("v", cV, mrow)
                mvar = P.op("v", lambda E: E.tensor_sub(stat2[:, :], stat2[:, :],
                                                        nmbB[0:1, 0:1024]), cV)
                P.wait("v", cV, mvar)
                P.wait("s", cV, mvar)
                P.op("s", lambda E: E.activation(stat2[:, :], stat2[:, :],
                                                 AF.Sqrt,
                                                 bias=cf[0:1, CF_EPS:CF_EPS + 1]))
                msd = sfence()
                P.wait("v", cA, msd)
                P.op("v", lambda E: E.reciprocal(stat2[:, :], stat2[:, :]))
                mrstd = vfence()
                P.wait("t", cV, mrstd)
                mbc = None
                for half in range(2):
                    P.op("t", lambda E, half=half, nm_b=nm_b:
                         E.matmul(p[nm_b + half][:, :], cf[0:1, CF_ROW:CF_ROW + 128],
                                  stat1[:, half * 512:half * 512 + 512],
                                  start=True, stop=True))
                    mbc = P.op("t", lambda E, half=half:
                               E.matmul(p[2 + half][:, :], cf[0:1, CF_ROW:CF_ROW + 128],
                                        stat2[:, half * 512:half * 512 + 512],
                                        start=True, stop=True), cP)
                P.wait("v", cP, mbc)
                for half in range(2):
                    P.op("v", lambda E, half=half, nm_b=nm_b:
                         E.tensor_copy(nmbB[:, half * 512:half * 512 + 512],
                                       p[nm_b + half][:, :]))
                    P.op("v", lambda E, half=half:
                         E.tensor_copy(rbB[:, half * 512:half * 512 + 512],
                                       p[2 + half][:, :]))
                m_apply = {}
                for ec in range(HCN):
                    mxb = None
                    for half in range(2):
                        lo = ec * T + half * 512
                        P.op("v", lambda E, lo=lo, half=half:
                             E.tensor_add(resid[:, lo:lo + 512], resid[:, lo:lo + 512],
                                          nmbB[:, half * 512:half * 512 + 512]))
                        P.op("v", lambda E, lo=lo, half=half:
                             E.tensor_mul(resid[:, lo:lo + 512],
                                          resid[:, lo:lo + 512],
                                          rbB[:, half * 512:half * 512 + 512]))
                        mx2 = vfence()
                        if write_x2b:
                            P.wait("s", cV, mx2)
                            P.op("s", lambda E, lo=lo:
                                 E.copy(qf[:, lo:lo + 512],
                                        resid[:, lo:lo + 512]))
                            mxb = sfence()
                        else:
                            mxb = mx2
                    m_apply[ec] = mxb
                return m_apply

            m_x2b = ln_finalize_apply(m_stats, True)
            m_ln1_done = max(m_x2b.values())

            if "dbg_x2" in dbg_out:
                P.wait("y", cA, m_ln1_done)
                P.op("y", lambda E: E.dma_start(
                    out=bass.AP(dbg_out["dbg_x2"].tensor, 0,
                                [[T, 128], [128 * T, HCN], [1, T]]),
                    in_=resid[:, :]), sOUT, 16)

            m_gu_mm = {}
            m_wd_mm = {}
            m_hband = {}
            m_sg = {}
            m_dn_ev = {}
            m_ceb = {}

            def load_gu(idx, guard):
                e_, j_ = divmod(idx, 6)
                P.wait("g", cP, guard)
                P.op("g", lambda E, e_=e_, j_=j_, idx=idx: E.dma_start(
                    out=wgu_buf[:, (idx % 2) * 4096:(idx % 2) * 4096 + 4096],
                    in_=d_wgu[e_, j_]), sGU2[idx % 2], 16)

            def load_wd(idx, guard):
                e_, q_ = divmod(idx, 8)
                P.wait("g", cP, guard)
                P.op("g", lambda E, e_=e_, q_=q_, idx=idx: E.dma_start(
                    out=wd_buf[:, (idx % 2) * 1536:(idx % 2) * 1536 + 1536],
                    in_=d_wd[e_][:, q_ * 1536:q_ * 1536 + 1536]), sWD2[idx % 2], 16)

            P.wait("g", cV, m_attn_dve)
            P.wait("g", cA, m_attn_act)
            load_gu(0, None)
            load_gu(1, None)
            load_wd(0, None)
            load_wd(1, None)

            def emit_ce_half(e, half, extra_tV=None):
                ep = e % 2
                P.wait("t", cV, extra_tV)
                mcb = P.op("t", lambda E, e=e, half=half:
                           E.matmul(p[4 + half][:, :],
                                    cbf[0:32, C_SEL + e * 128:C_SEL + e * 128 + 128],
                                    ce_fm[0:32, half * 512:half * 512 + 512],
                                    start=True, stop=True), cP)
                P.wait("v", cP, mcb)
                P.op("v", lambda E, ep=ep, half=half:
                     E.tensor_copy(ce_b[ep][:, half * 512:half * 512 + 512],
                                   p[4 + half][:, :]))
                m_ceb[(e, half)] = vfence()

            def emit_gu_band_mms(e, j, half):
                ep = e % 2
                idx = e * 6 + j
                gt_, ut_ = 2 * half, 2 * half + 1
                P.wait("t", sGU2[idx % 2], 16 * (idx // 2 + 1))
                if e >= 2 and j == 0 and half == 0:
                    P.wait("t", cP, m_wd_mm[(e - 2, 7)])
                pkey = (e, j - 1, half) if j >= 1 else                        ((e - 1, 5, half) if e >= 1 else None)
                if pkey is not None:
                    P.wait("t", cV, m_hband[pkey])
                    P.wait("t", cA, m_sg[pkey])
                mgu = None
                for hc in range(HCN):
                    if e == 0 and j == 0:
                        P.wait("t", cA, m_x2b[hc])
                    P.op("t", lambda E, hc=hc, half=half, gt_=gt_, idx=idx:
                         E.matmul(p[gt_][:, :],
                                  wgu_buf[:, (idx % 2) * 4096 + hc * 256:
                                          (idx % 2) * 4096 + hc * 256 + 128],
                                  qf[:, hc * T + half * 512:
                                     hc * T + half * 512 + 512],
                                  start=(hc == 0), stop=(hc == HCN - 1)))
                    def bldu(E, hc=hc, half=half, ut_=ut_, idx=idx):
                        return E.matmul(
                            p[ut_][:, :],
                            wgu_buf[:, (idx % 2) * 4096 + hc * 256 + 128:
                                    (idx % 2) * 4096 + hc * 256 + 256],
                            qf[:, hc * T + half * 512:hc * T + half * 512 + 512],
                            start=(hc == 0), stop=(hc == HCN - 1))
                    if hc == HCN - 1:
                        mgu = P.op("t", bldu, cP)
                    else:
                        P.op("t", bldu)
                m_gu_mm[(e, j)] = mgu
                if half == 1 and e * 6 + j + 2 < NEXP * 6:
                    load_gu(e * 6 + j + 2, mgu)
                return mgu

            def emit_gu_evacs(e, j, half, mgu):
                ep = e % 2
                gt_, ut_ = 2 * half, 2 * half + 1
                P.wait("s", cP, mgu)
                P.op("s", lambda E, half=half, gt_=gt_:
                     E.activation(sg_sb[half][:, :], p[gt_][:, :], AF.Silu))
                msg = sfence()
                m_sg[(e, j, half)] = msg
                P.wait("v", cP, mgu)
                P.wait("v", cV, m_ceb[(e, half)])
                P.op("v", lambda E, half=half, ut_=ut_, ep=ep:
                     E.tensor_mul(usc_sb[half][:, :], p[ut_][:, :],
                                  ce_b[ep][:, half * 512:half * 512 + 512]))
                P.wait("v", cA, msg)
                P.op("v", lambda E, half=half, ep=ep, j=j:
                     E.tensor_mul(
                         h_buf[ep][:, j * T + half * 512:j * T + half * 512 + 512],
                         sg_sb[half][:, :], usc_sb[half][:, :]))
                m_hband[(e, j, half)] = vfence()

            def emit_gu_band(e, j, half):
                emit_gu_evacs(e, j, half, emit_gu_band_mms(e, j, half))

            mlg = None
            for half in range(2):
                for hc in range(HCN):
                    if half == 0:
                        P.wait("t", cA, m_x2b[hc])
                    def bld(E, hc=hc, half=half):
                        return E.matmul(
                            p[4 + half][0:32, :], gate_bf[:, hc * 32:hc * 32 + 32],
                            qf[:, hc * T + half * 512:hc * T + half * 512 + 512],
                            start=(hc == 0), stop=(hc == HCN - 1))
                    if hc == HCN - 1:
                        mlg = P.op("t", bld, cP)
                    else:
                        P.op("t", bld)
            P.wait("v", cP, mlg)
            P.op("v", lambda E: E.tensor_copy(lg_sb[:, 0:512], p[4][0:32, :]))
            P.op("v", lambda E: E.tensor_copy(lg_sb[:, 512:1024], p[5][0:32, :]))
            mlge = vfence()
            P.wait("t", cV, mlge)
            for tb in range(8):
                P.op("t", lambda E, tb=tb:
                     E.transpose(pt[0][0:128, tb * 32:tb * 32 + 32],
                                 lg_sb[0:32, tb * 128:tb * 128 + 128],
                                 cbf[0:32, C_ID:C_ID + 32]))
            mgt = pefence(lambda: p[1][0:1, 0:1])
            P.wait("v", cP, mgt)
            mgtm = P.op("v", lambda E: E.tensor_copy(g_tm[:, :], pt[0][:, 0:256]), cV)
            P.wait("v", cV, mgtm)

            m_cetr = {}
            mcv_prev = None
            for pair in range(4):
                tiles = (2 * pair, 2 * pair + 1)
                mm_ = [w32[0], w32[4]]
                wv_ = [w32[1], w32[5]]
                cnt_ = [w32[2], w32[6]]
                Gg_ = [w32[3], w32[7]]
                rr_ = [0, 4]

                def zip2(fn, inc=None):
                    out = None
                    for i_ in range(2):
                        b_ = fn(i_, tiles[i_])
                        if i_ == 1 and inc is not None:
                            out = P.op("v", b_, inc)
                        else:
                            P.op("v", b_)
                    return out

                zip2(lambda i, tb: (lambda E, i=i, tb=tb:
                     E.reduce_max(red[:, rr_[i]:rr_[i] + 1],
                                  g_tm[:, tb * 32:tb * 32 + 32], AX,
                                  negate=True)))
                mneg = vfence()
                P.wait("s", cV, mneg)
                for i_ in range(2):
                    P.op("s", lambda E, i=i_, tb=tiles[i_]:
                         E.activation(mm_[i][:, :],
                                      g_tm[:, tb * 32:tb * 32 + 32],
                                      AF.Exp, bias=red[:, rr_[i]:rr_[i] + 1]))
                mexp = sfence()
                P.wait("v", cA, mexp)
                zip2(lambda i, tb: (lambda E, i=i:
                     E.reduce_sum(red[:, rr_[i] + 1:rr_[i] + 2], mm_[i][:, :], AX)))
                zip2(lambda i, tb: (lambda E, i=i:
                     E.reciprocal(red[:, rr_[i] + 1:rr_[i] + 2],
                                  red[:, rr_[i] + 1:rr_[i] + 2])))
                zip2(lambda i, tb: (lambda E, i=i, tb=tb:
                     E.tensor_single_scalar(p_sb[:, tb * 32:tb * 32 + 32],
                                            mm_[i][:, :],
                                            red[:, rr_[i] + 1:rr_[i] + 2],
                                            OP.mult)))
                for e32 in range(NEXP):
                    zip2(lambda i, tb: (lambda E, i=i, tb=tb, e32=e32:
                         E.tensor_scalar(
                             mm_[i][:, :], p_sb[:, tb * 32:tb * 32 + 32],
                             p_sb[:, tb * 32 + e32:tb * 32 + e32 + 1], 0.0,
                             OP.is_ge, OP.add,
                             accum_out=cnt_[i][:, e32:e32 + 1])))
                    zip2(lambda i, tb: (lambda E, i=i, tb=tb, e32=e32:
                         E.scalar_tensor_tensor(
                             wv_[i][:, :], p_sb[:, tb * 32:tb * 32 + 32],
                             p_sb[:, tb * 32 + e32:tb * 32 + e32 + 1],
                             p_sb[:, tb * 32:tb * 32 + 32],
                             OP.is_ge, OP.mult,
                             accum_out=Gg_[i][:, e32:e32 + 1])))
                zip2(lambda i, tb: (lambda E, i=i:
                     E.tensor_single_scalar(mm_[i][:, :], cnt_[i][:, :], 8.0,
                                            OP.is_le)))
                zip2(lambda i, tb: (lambda E, i=i, tb=tb:
                     E.tensor_mul(wv_[i][:, :], mm_[i][:, :],
                                  p_sb[:, tb * 32:tb * 32 + 32])))
                zip2(lambda i, tb: (lambda E, i=i:
                     E.reduce_sum(red[:, rr_[i] + 2:rr_[i] + 3], wv_[i][:, :], AX)))
                zip2(lambda i, tb: (lambda E, i=i:
                     E.tensor_scalar_mul(red[:, rr_[i] + 2:rr_[i] + 3],
                                         red[:, rr_[i] + 2:rr_[i] + 3], 0.8)))
                zip2(lambda i, tb: (lambda E, i=i, tb=tb:
                     E.tensor_sub(Gg_[i][:, :], Gg_[i][:, :],
                                  p_sb[:, tb * 32:tb * 32 + 32])))
                zip2(lambda i, tb: (lambda E, i=i:
                     E.tensor_single_scalar(Gg_[i][:, :], Gg_[i][:, :],
                                            red[:, rr_[i] + 2:rr_[i] + 3],
                                            OP.is_lt)))
                zip2(lambda i, tb: (lambda E, i=i:
                     E.tensor_mul(mm_[i][:, :], mm_[i][:, :], Gg_[i][:, :])))
                zip2(lambda i, tb: (lambda E, i=i, tb=tb:
                     E.tensor_mul(wv_[i][:, :], mm_[i][:, :],
                                  p_sb[:, tb * 32:tb * 32 + 32])))
                zip2(lambda i, tb: (lambda E, i=i:
                     E.reduce_sum(red[:, rr_[i] + 3:rr_[i] + 4], wv_[i][:, :], AX)))
                zip2(lambda i, tb: (lambda E, i=i:
                     E.tensor_scalar_max(red[:, rr_[i] + 3:rr_[i] + 4],
                                         red[:, rr_[i] + 3:rr_[i] + 4], 1e-12)))
                zip2(lambda i, tb: (lambda E, i=i:
                     E.reciprocal(red[:, rr_[i] + 3:rr_[i] + 4],
                                  red[:, rr_[i] + 3:rr_[i] + 4])))
                zip2(lambda i, tb: (lambda E, i=i:
                     E.tensor_single_scalar(wv_[i][:, :], wv_[i][:, :],
                                            red[:, rr_[i] + 3:rr_[i] + 4],
                                            OP.mult)))
                mce = vfence()
                P.wait("t", cV, mce)
                for i_ in range(2):
                    tb = tiles[i_]
                    P.wait("t", cV, mcv_prev)
                    P.op("t", lambda E, i=i_:
                         E.transpose(p[4][0:32, 0:128], wv_[i][:, :],
                                     cf[:, CF_ID:CF_ID + 128]))
                    mct = pefence(lambda: p[5][0:1, 400:401])
                    P.wait("v", cP, mct)
                    P.op("v", lambda E, tb=tb:
                         E.tensor_copy(ce_fm[0:32, tb * 128:tb * 128 + 128],
                                       p[4][0:32, 0:128]))
                    mcv = vfence()
                    m_cetr[tb] = mcv
                    mcv_prev = mcv
                if pair == 0:
                    mgu00 = emit_gu_band_mms(0, 0, 0)
                elif pair == 1:
                    emit_ce_half(0, 0, extra_tV=m_cetr[3])
                    emit_gu_evacs(0, 0, 0, mgu00)
                    mcv_prev = m_ceb[(0, 0)]
                elif pair == 2:
                    mgu01 = emit_gu_band_mms(0, 0, 1)
                elif pair == 3:
                    emit_ce_half(0, 1, extra_tV=m_cetr[7])
                    emit_gu_evacs(0, 0, 1, mgu01)
            m_ce_done = m_cetr[7]

            if "dbg_ce" in dbg_out:
                P.wait("y", cV, m_ce_done)
                P.op("y", lambda E: E.dma_start(out=dbg_out["dbg_ce"],
                                                in_=ce_fm[:, :]), sOUT, 16)

            for e in range(NEXP):
                ep = e % 2
                if e >= 1:
                    for half in range(2):
                        emit_ce_half(e, half,
                                     extra_tV=m_dn_ev[(e - 1, 15, 1)]
                                     if half == 0 else None)
                for j in range(6):
                    if e == 0 and j == 0:
                        continue
                    for half in range(2):
                        emit_gu_band(e, j, half)
                m_h_done = max(m_hband[(e, j, hh)] for j in range(6)
                               for hh in range(2))

                for oc in range(16):
                    qq = oc // 2
                    idx = e * 8 + qq
                    for half in range(2):
                        dt_ = 4 + half
                        P.wait("t", sWD2[idx % 2], 16 * (idx // 2 + 1))
                        P.wait("t", cV, m_h_done)
                        pkey = (e, oc - 1, half) if oc >= 1 else \
                               ((e - 1, 15, half) if e >= 1 else None)
                        if pkey is not None and pkey in m_dn_ev:
                            P.wait("t", cV, m_dn_ev[pkey])
                        woff = (idx % 2) * 1536 + (oc % 2) * 768
                        mdn = None
                        for ic in range(6):
                            def bld(E, ic=ic, woff=woff, half=half, dt_=dt_, ep=ep):
                                return E.matmul(
                                    p[dt_][:, :],
                                    wd_buf[:, woff + ic * 128:woff + ic * 128 + 128],
                                    h_buf[ep][:, ic * T + half * 512:
                                              ic * T + half * 512 + 512],
                                    start=(ic == 0), stop=(ic == 5))
                            if ic == 5:
                                mdn = P.op("t", bld, cP)
                            else:
                                P.op("t", bld)
                        if half == 1 and oc % 2 == 1:
                            m_wd_mm[(e, qq)] = mdn
                            if e * 8 + qq + 2 < NEXP * 8:
                                load_wd(e * 8 + qq + 2, mdn)
                        P.wait("v", cP, mdn)
                        if e == NEXP - 1:
                            P.op("v", lambda E, oc=oc, half=half, dt_=dt_:
                                 E.tensor_add(
                                     resid[:, oc * T + half * 512:
                                           oc * T + half * 512 + 512],
                                     resid[:, oc * T + half * 512:
                                           oc * T + half * 512 + 512],
                                     p[dt_][:, :]))
                            mde = vfence()
                        else:
                            mde = P.op("v", lambda E, oc=oc, half=half, dt_=dt_:
                                       E.tensor_add(
                                           resid[:, oc * T + half * 512:
                                                 oc * T + half * 512 + 512],
                                           resid[:, oc * T + half * 512:
                                                 oc * T + half * 512 + 512],
                                           p[dt_][:, :]), cV)
                        m_dn_ev[(e, oc, half)] = mde

            m_dn_all = m_dn_ev[(NEXP - 1, 15, 1)]
            m_sg_all = m_sg[(NEXP - 1, 5, 1)]

            prev_sq_pe = None
            m_sqm2 = {}
            P.wait("t", cV, max(m_hband[(NEXP - 1, 5, hh)] for hh in range(2)))
            P.wait("t", cA, m_sg_all)
            for ec in range(HCN):
                for half in range(2):
                    mlast = m_dn_ev[(NEXP - 1, ec, half)]
                    P.wait("s", cV, mlast)
                    P.wait("s", cP, prev_sq_pe)
                    P.op("s", lambda E, ec=ec, half=half:
                         E.activation(sq_sb[:, :],
                                      resid[:, ec * T + half * 512:
                                            ec * T + half * 512 + 512],
                                      AF.Square))
                    msqa = sfence()
                    P.wait("t", cV, mlast)
                    P.op("t", lambda E, ec=ec, half=half:
                         E.matmul(p[half][0:1, :], cf[:, CF_ONE:CF_ONE + 1],
                                  resid[:, ec * T + half * 512:ec * T + half * 512 + 512],
                                  start=(ec == 0), stop=(ec == 15)))
                    P.wait("t", cA, msqa)
                    msq2 = P.op("t", lambda E, half=half, ec=ec:
                                E.matmul(p[2 + half][0:1, :], cf[:, CF_ONE:CF_ONE + 1],
                                         sq_sb[:, :],
                                         start=(ec == 0), stop=(ec == 15)), cP)
                    m_sqm2[(ec, half)] = msq2
                    prev_sq_pe = msq2
            m_stats2 = max(m_sqm2.values())
            m_out = ln_finalize_apply(m_stats2, False, sum_b=0, nm_b=4)
            for ec in range(HCN):
                P.wait("y", cV, m_out[ec])
                P.op("y", lambda E, ec=ec: E.dma_start(
                    out=d_out[ec * 128:(ec + 1) * 128, :],
                    in_=resid[:, ec * T:(ec + 1) * T]), sOUT, 16)
            P.wait("y", sOUT, sOUT.n)

        with nc.Block() as block:
            @block.tensor
            def _(t):
                for f in P.q["t"]:
                    f(t)

            @block.vector
            def _(v):
                for f in P.q["v"]:
                    f(v)

            @block.scalar
            def _(s_):
                for f in P.q["s"]:
                    f(s_)

            @block.gpsimd
            def _(g):
                for f in P.q["g"]:
                    f(g)

            @block.sync
            def _(y):
                for f in P.q["y"]:
                    f(y)

    return nc



def _tile_T(blk):
    K = blk.shape[1]
    n = K // 128
    return np.ascontiguousarray(
        blk.T.reshape(n, 128, 128).transpose(1, 0, 2).reshape(128, n * 128))


def _host_prep(x, wqkv, wo, gw, wgu, wd):
    x = np.asarray(x, np.float32)
    wqkv = np.asarray(wqkv, np.float32)
    wo = np.asarray(wo, np.float32)
    gw = np.asarray(gw, np.float32)
    wgu = np.asarray(wgu, np.float32)
    wd = np.asarray(wd, np.float32)

    tiles = []
    for h in range(16):
        tiles.append(_tile_T(wqkv[4096 + h * 128:4096 + (h + 1) * 128, :]))
    for h in range(16):
        tiles.append(_tile_T(wqkv[2048 + h * 128:2048 + (h + 1) * 128, :]))
    for h in range(16):
        tiles.append(_tile_T(wqkv[h * 128:(h + 1) * 128, :]))
    wkvq_t = np.stack(tiles)
    wout_t = np.stack([_tile_T(wo[ob * 128:(ob + 1) * 128, :])
                       for ob in range(16)])
    gate_t = np.ascontiguousarray(
        gw.T.reshape(16, 128, 32).transpose(1, 0, 2).reshape(128, 512))
    wgu_t = np.empty((NEXP, 6, 128, 4096), np.float32)
    for e in range(NEXP):
        for j in range(6):
            tg = _tile_T(wgu[e, j * 128:(j + 1) * 128, :]).reshape(128, 16, 128)
            tu = _tile_T(wgu[e, 768 + j * 128:768 + (j + 1) * 128, :]
                         ).reshape(128, 16, 128)
            wgu_t[e, j] = np.concatenate([tg, tu], axis=2).reshape(128, 4096)
    wd_t = np.empty((NEXP, 128, 12288), np.float32)
    for e in range(NEXP):
        cols = [_tile_T(wd[e, oc * 128:(oc + 1) * 128, :]) for oc in range(16)]
        wd_t[e] = np.concatenate(cols, axis=1)

    cbf = np.zeros((128, CBF_W), np.float32)
    cbf[:, C_ID:C_ID + 128] = np.eye(128)
    RT = np.zeros((128, 128), np.float32)
    for i in range(64):
        RT[2 * i + 1, 2 * i] = -1.0
        RT[2 * i, 2 * i + 1] = 1.0
    cbf[:, C_RT:C_RT + 128] = RT
    l_idx = np.arange(128)[:, None]
    m_idx = np.arange(256)[None, :]
    cbf[:, C_M0:C_M0 + 256] = (m_idx <= l_idx).astype(np.float32)
    cbf[:, C_M1:C_M1 + 256] = (m_idx <= l_idx + 128).astype(np.float32)
    cbf[:, C_ONE] = 1.0
    m_idx2 = np.arange(256)[None, :]
    p_idx = np.arange(128)[:, None]
    cbf[:, C_MT0:C_MT0 + 256] = (m_idx2 >= p_idx).astype(np.float32)
    cbf[:, C_MT1:C_MT1 + 256] = (m_idx2 >= p_idx + 128).astype(np.float32)
    for e in range(NEXP):
        cbf[e, C_SEL + e * 128:C_SEL + (e + 1) * 128] = 1.0
    cfc = np.zeros((128, CF_W), np.float32)
    cfc[:, CF_ID:CF_ID + 128] = np.eye(128)
    cfc[:, CF_ONE] = 1.0
    cfc[0, CF_ROW:CF_ROW + 128] = 1.0
    cfc[0, CF_EPS] = EPS

    inv_freq = 1.0 / (10000.0 ** (np.arange(0, 128, 2, dtype=np.float32) / 128))
    pos = np.arange(S, dtype=np.float32)
    freqs = np.outer(pos, inv_freq)
    cos_full = np.cos(freqs).astype(np.float32)
    sin_full = np.sin(freqs).astype(np.float32)

    maps = []
    for core in range(NCORES):
        b, pp = divmod(core, 4)
        t0 = pp * T
        xs = np.ascontiguousarray(x[b, t0:t0 + T, :].T)
        cos_i = np.ascontiguousarray(
            np.concatenate([cos_full[t0:t0 + T], cos_full[t0:t0 + T]], axis=1).T)
        sin_i = np.ascontiguousarray(
            np.concatenate([sin_full[t0:t0 + T], sin_full[t0:t0 + T]], axis=1).T)
        pm = np.zeros((128, 8), np.float32)
        pm[:, b * 4:b * 4 + pp] = 1.0
        maps.append(dict(x=xs, cos=cos_i, sin=sin_i, pmask=pm, cbf=cbf, cf=cfc,
                         gate=gate_t, wkvq=wkvq_t, wout=wout_t, wgu=wgu_t,
                         wd=wd_t))
    return maps


def kernel(L0_attn_x, L0_attn_W_qkv, L0_attn_W_out, L0_ffn_gate_weight,
           L0_ffn_gate_up_proj, L0_ffn_down_proj):
    maps = _host_prep(L0_attn_x, L0_attn_W_qkv, L0_attn_W_out,
                      L0_ffn_gate_weight, L0_ffn_gate_up_proj,
                      L0_ffn_down_proj)
    nc = build_program()
    res = run_bass_kernel_spmd(nc, maps, core_ids=list(range(NCORES)))
    out = np.empty((NB, S, EMB), np.float32)
    for core in range(NCORES):
        b, pp = divmod(core, 4)
        out[b, pp * T:(pp + 1) * T, :] = np.asarray(res.results[core]["out"]).T
    return out


# version 11
# speedup vs baseline: 1.0176x; 1.0176x over previous
import numpy as np
import concourse.bass as bass
import concourse.mybir as mybir
from concourse.bass_utils import run_bass_kernel_spmd
from contextlib import ExitStack

F32 = mybir.dt.float32
BF16 = mybir.dt.bfloat16
AX = mybir.AxisListType.X
AF = mybir.ActivationFunctionType
OP = mybir.AluOpType

NCORES = 8
NB, S, EMB = 2, 4096, 2048
T = 1024
H, D = 16, 128
CH = 4
HCN = 16
NEXP = 32
EPS = 1e-5

C_ID = 0
C_RT = 128
C_M0 = 256
C_M1 = 512
C_ONE = 768
C_MT0 = 769
C_MT1 = 1025
C_SEL = 1281
CBF_W = C_SEL + NEXP * 128
CF_ID = 0
CF_ONE = 128
CF_ROW = 129
CF_EPS = 257
CF_W = 258


class Sem:
    def __init__(self, st, nc, name):
        self.h = st.enter_context(nc.semaphore(name))
        self.n = 0


class Prog:
    ENGS = ("t", "v", "s", "g", "y")

    def __init__(self):
        self.q = {e: [] for e in self.ENGS}

    def op(self, eng, build, inc=None, amt=1):
        if inc is None:
            self.q[eng].append(build)
            return None
        inc.n += amt
        self.q[eng].append(lambda E, b=build, s=inc.h, a=amt: b(E).then_inc(s, a))
        return inc.n

    def wait(self, eng, sem, val):
        if val is None:
            return
        self.q[eng].append(lambda E, s=sem.h, v=val: E.wait_ge(s, v))


def build_program(dbg=()):
    nc = bass.Bass("TRN2", target_bir_lowering=False,
                   detect_race_conditions=False)
    P = Prog()

    d_x = nc.dram_tensor("x", [EMB, T], F32, kind="ExternalInput").ap()
    d_cos = nc.dram_tensor("cos", [D, T], F32, kind="ExternalInput").ap()
    d_sin = nc.dram_tensor("sin", [D, T], F32, kind="ExternalInput").ap()
    d_pm = nc.dram_tensor("pmask", [128, 8], F32, kind="ExternalInput").ap()
    d_cbf = nc.dram_tensor("cbf", [128, CBF_W], F32, kind="ExternalInput").ap()
    d_cf = nc.dram_tensor("cf", [128, CF_W], F32, kind="ExternalInput").ap()
    d_gate = nc.dram_tensor("gate", [128, 512], F32, kind="ExternalInput").ap()
    d_wkvq = nc.dram_tensor("wkvq", [48, 128, 2048], F32, kind="ExternalInput").ap()
    d_wout = nc.dram_tensor("wout", [16, 128, 2048], F32, kind="ExternalInput").ap()
    d_wgu = nc.dram_tensor("wgu", [NEXP, 6, 128, 4096], F32, kind="ExternalInput").ap()
    d_wd = nc.dram_tensor("wd", [NEXP, 128, 12288], F32, kind="ExternalInput").ap()
    d_out = nc.dram_tensor("out", [EMB, T], F32, kind="ExternalOutput").ap()
    x_src = bass.AP(d_x.tensor, 0, [[T, 128], [128 * T, HCN], [1, T]])

    dbg_out = {}
    for name, shape in (("dbg_qf", [128, H * T]), ("dbg_kf", [128, H * T]),
                        ("dbg_vt", [128, H * T]), ("dbg_attn", [128, H * T])):
        if name in dbg:
            dbg_out[name] = nc.dram_tensor(name, shape, BF16, kind="ExternalOutput").ap()
    for name, shape, dt in (("dbg_scur", [128, H * D], F32),
                            ("dbg_x2", [EMB, T], F32),
                            ("dbg_ce", [NEXP, T], BF16),
                            ("dbg_den", [128, 128], F32),
                            ("dbg_am", [128, 640], BF16),
                            ("dbg_num", [128, 256], F32)):
        if name in dbg:
            dbg_out[name] = nc.dram_tensor(name, shape, dt, kind="ExternalOutput").ap()

    d_ccin = nc.dram_tensor("cc_in", [128, 2064], F32, kind="Internal")
    d_ccout = nc.dram_tensor("cc_out", [128 * NCORES, 2064], F32,
                             kind="Internal", addr_space="Shared")

    st = ExitStack()
    with st:
        def sb(name, shape, dt):
            return st.enter_context(nc.sbuf_tensor(name, shape, dt))

        p = [st.enter_context(nc.psum_tensor(f"p{i}", [128, 512], F32))
             for i in range(6)]
        pt = [st.enter_context(nc.psum_tensor(f"pt{i}", [128, 1024], BF16))
              for i in range(2)]

        sW = Sem(st, nc, "sW")
        sW3 = [Sem(st, nc, f"sW3_{i}") for i in range(3)]
        sGU2 = [Sem(st, nc, f"sGU2_{i}") for i in range(2)]
        sWD2 = [Sem(st, nc, f"sWD2_{i}") for i in range(2)]
        sPRE2 = [Sem(st, nc, f"sPRE2_{i}") for i in range(2)]
        sGU = Sem(st, nc, "sGU")
        sWD = Sem(st, nc, "sWD")
        sPRE = Sem(st, nc, "sPRE")
        sSTG = Sem(st, nc, "sSTG")
        sCC = Sem(st, nc, "sCC")
        sF = Sem(st, nc, "sF")
        sRES = Sem(st, nc, "sRES")
        sOUT = Sem(st, nc, "sOUT")
        cP = Sem(st, nc, "cP")
        cV = Sem(st, nc, "cV")
        cA = Sem(st, nc, "cA")

        cbf = sb("cbf_s", [128, CBF_W], BF16)
        cf = sb("cf_s", [128, CF_W], F32)
        gate_bf = sb("gate_s", [128, 512], BF16)
        pmask = sb("pmask_s", [128, 8], F32)
        wbufA = sb("wbufA", [128, 3 * 2048], BF16)
        qf = sb("qf", [128, H * T], BF16)
        fdv = sb("fdv", [128, 1], F32)
        fds = sb("fds", [128, 1], F32)
        S_cur = sb("S_cur", [128, H * D], F32)
        Z_cur = sb("Z_cur", [128, H], F32)

        def pefence(out_ap_fn):
            return P.op("t", lambda E, f=out_ap_fn: E.matmul(
                f(), cbf[:, C_ONE:C_ONE + 1], cbf[:, C_ONE:C_ONE + 1],
                start=True, stop=True), cP)

        def vfence():
            return P.op("v", lambda E: E.drain(), cV)

        def sfence():
            return P.op("s", lambda E: E.drain(), cA)

        P.op("y", lambda E: E.dma_start(out=cf[:, :], in_=d_cf), sF, 16)
        P.op("y", lambda E: E.dma_start(out=pmask[:, :], in_=d_pm), sF, 16)

        def g_load(dst_ap, src_ap, sem):
            return P.op("g", lambda E, d=dst_ap, s=src_ap: E.dma_start(out=d, in_=s),
                        sem, 16)

        m_wslot = {}
        m_qfin = {}
        nW_holder = [0]

        with ExitStack() as st2:
            def sb2(name, shape, dt):
                return st2.enter_context(nc.sbuf_tensor(name, shape, dt))

            cos_bf = sb2("cos_bf", [128, T], BF16)
            sin_bf = sb2("sin_bf", [128, T], BF16)
            kf = sb2("kf", [128, H * T], BF16)
            vt = sb2("vt", [128, H * T], BF16)
            S_run = sb2("S_run", [128, H * D], F32)
            Z_run = sb2("Z_run", [128, H], F32)
            kraw = [sb2(f"kraw{i}", [128, T], BF16) for i in range(2)]
            vraw = [sb2(f"vraw{i}", [128, 512], BF16) for i in range(2)]
            r1 = [sb2(f"r1_{i}", [128, 512], BF16) for i in range(2)]
            r2 = [sb2(f"r2_{i}", [128, 512], BF16) for i in range(2)]
            r3 = [sb2(f"r3_{i}", [128, 512], BF16) for i in range(2)]
            ktm = [sb2(f"ktm{i}", [128, 256], BF16) for i in range(2)]
            am_sb = [sb2(f"am{i}", [128, 640], BF16) for i in range(2)]
            ot = [sb2(f"ot{i}", [128, 256], BF16) for i in range(2)]
            sbf = [sb2(f"sbf{i}", [128, 128], BF16) for i in range(2)]
            zbf = [sb2(f"zbf{i}", [128, 1], BF16) for i in range(2)]
            den_c = sb2("den_c", [128, 32], F32)
            jnk = sb2("jnk", [128, 256], BF16)
            ztmp = sb2("ztmp", [128, 2], F32)
            pre_buf = [sb2("pre0", [128, 2064], F32)]
            ktmp = sb2("ktmp", [128, 2048], BF16)
            dbg_den_sb = sb2("dbg_den_sb", [128, 128], F32) if "dbg_den" in dbg_out else None
            dbg_am_sb = sb2("dbg_am_sb", [128, 640], BF16) if "dbg_am" in dbg_out else None
            dbg_num_sb = sb2("dbg_num_sb", [128, 256], F32) if "dbg_num" in dbg_out else None

            g_load(cbf[:, :], d_cbf, sW)
            g_load(cos_bf[:, :], d_cos, sW)
            g_load(sin_bf[:, :], d_sin, sW)
            g_load(gate_bf[:, :], d_gate, sW)

            P.op("v", lambda E: E.memset(S_run[:, :], 0.0))
            P.op("v", lambda E: E.memset(Z_run[:, :], 0.0))
            P.op("v", lambda E: E.memset(S_cur[:, :], 0.0))
            P.op("v", lambda E: E.memset(Z_cur[:, :], 0.0))
            m_memset = vfence()

            m_proj = {}
            m_kfin = {}
            m_vtev = {}
            m_Sadd = {}
            m_kact = {}

            with ExitStack() as st3:
                x_bf = st3.enter_context(
                    nc.sbuf_tensor("x_bf", [128, HCN * T], BF16))
                g_load(x_bf[:, :], x_src, sW)
                nW = 5

                def proj_group(ptile, wslot, half):
                    m = None
                    for hc in range(HCN):
                        def bld(E, hc=hc, ptile=ptile, wslot=wslot, half=half):
                            return E.matmul(
                                p[ptile][:, :],
                                wbufA[:, wslot * 2048 + hc * 128:
                                      wslot * 2048 + hc * 128 + 128],
                                x_bf[:, hc * T + half * 512:hc * T + half * 512 + 512],
                                start=(hc == 0), stop=(hc == HCN - 1))
                        if hc == HCN - 1:
                            m = P.op("t", bld, cP)
                        else:
                            P.op("t", bld)
                    return m

                def rope_elu(h, half, dst, me_prev_needed):
                    sl = half
                    g2 = half
                    mrh = P.op("t", lambda E, g2=g2, h=h, half=half:
                               E.matmul(p[2 + g2][:, :], cbf[:, C_RT:C_RT + 128],
                                        kraw[h % 2][:, half * 512:half * 512 + 512],
                                        start=True, stop=True), cP)
                    P.wait("v", cP, mrh)
                    P.op("v", lambda E, sl=sl, h=h, half=half:
                         E.tensor_mul(r1[sl][:, :],
                                      kraw[h % 2][:, half * 512:half * 512 + 512],
                                      cos_bf[:, half * 512:half * 512 + 512]))
                    P.op("v", lambda E, sl=sl, g2=g2, half=half:
                         E.tensor_mul(r3[sl][:, :], p[2 + g2][:, :],
                                      sin_bf[:, half * 512:half * 512 + 512]))
                    m_rope = P.op("v", lambda E, sl=sl:
                                  E.tensor_add(r2[sl][:, :], r1[sl][:, :],
                                               r3[sl][:, :]), cV)
                    P.wait("s", cV, m_rope)
                    mex = P.op("s", lambda E, sl=sl:
                               E.activation(r1[sl][:, :], r2[sl][:, :], AF.Exp), cA)
                    P.op("v", lambda E, sl=sl:
                         E.tensor_scalar_max(r3[sl][:, :], r2[sl][:, :], 0.0))
                    P.wait("v", cA, mex)
                    P.op("v", lambda E, sl=sl:
                         E.tensor_scalar_min(r1[sl][:, :], r1[sl][:, :], 1.0))
                    P.op("v", lambda E, sl=sl, dst=dst:
                         E.tensor_add(dst, r1[sl][:, :], r3[sl][:, :]))
                    return vfence(), mrh

                for ob in range(16):
                    h = ob
                    if ob >= 3:
                        P.wait("g", cP, m_wslot[ob - 3])
                    g_load(wbufA[:, (ob % 3) * 2048:(ob % 3) * 2048 + 2048],
                           d_wkvq[ob], sW3[ob % 3])
                    if ob == 0:
                        P.wait("t", sW, 16 * nW)
                    P.wait("t", sW3[ob % 3], sW3[ob % 3].n)
                    for half in range(2):
                        g2 = half
                        m = proj_group(g2, ob % 3, half)
                        m_proj[(ob, half)] = m
                        P.wait("s", cP, m)
                        P.op("s", lambda E, g2=g2:
                             E.copy(vraw[g2][:, :], p[g2][:, :]))
                        mv = sfence()
                        P.wait("t", cA, mv)
                        if ob >= 1:
                            P.wait("t", cV, m_vtev[(ob - 1, half)])
                        for tb in range(4):
                            P.op("t", lambda E, g2=g2, tb=tb:
                                 E.transpose(pt[g2][:, tb * 128:tb * 128 + 128],
                                             vraw[g2][:, tb * 128:tb * 128 + 128],
                                             cbf[:, C_ID:C_ID + 128]))
                        mtr = pefence(lambda: p[2 + g2][0:1, 0:1])
                        P.wait("v", cP, mtr)
                        P.op("v", lambda E, g2=g2, h=h, half=half:
                             E.tensor_copy(
                                 vt[:, h * T + half * 512:h * T + half * 512 + 512],
                                 pt[g2][:, 0:512]))
                        mvt = vfence()
                        m_vtev[(h, half)] = mvt
                    m_wslot[ob] = m_proj[(ob, 1)]
                m_vt_all = max(m_vtev.values())

                for ob in range(16, 32):
                    h = ob - 16
                    P.wait("g", cP, m_wslot[ob - 3])
                    g_load(wbufA[:, (ob % 3) * 2048:(ob % 3) * 2048 + 2048],
                           d_wkvq[ob], sW3[ob % 3])
                    P.wait("t", sW3[ob % 3], sW3[ob % 3].n)
                    for half in range(2):
                        g2 = half
                        m = proj_group(g2, ob % 3, half)
                        m_proj[(ob, half)] = m
                        P.wait("s", cP, m)
                        if h >= 2:
                            P.wait("s", cV, m_kfin[(h - 2, half)])
                        P.op("s", lambda E, h=h, g2=g2, half=half:
                             E.copy(kraw[h % 2][:, half * 512:half * 512 + 512],
                                    p[g2][:, :]))
                        me = sfence()
                        P.wait("t", cA, me)
                        mk, _ = rope_elu(
                            h, half,
                            kf[:, h * T + half * 512:h * T + half * 512 + 512], None)
                        m_kfin[(h, half)] = mk
                        mzr = P.op("v", lambda E, h=h, half=half:
                                   E.reduce_sum(ztmp[:, 0:1],
                                                kf[:, h * T + half * 512:
                                                   h * T + half * 512 + 512],
                                                AX), cV)
                        P.wait("v", cV, mzr)
                        P.op("v", lambda E, h=h:
                             E.tensor_add(Z_run[:, h:h + 1], Z_run[:, h:h + 1],
                                          ztmp[:, 0:1]), cV)
                        for ci in range(2):
                            c = half * 2 + ci
                            cpar = ci
                            P.wait("t", cV, mk)
                            for mb in range(2):
                                P.op("t", lambda E, h=h, c=c, mb=mb, cpar=cpar:
                                     E.transpose(
                                         pt[cpar][:, mb * 128:mb * 128 + 128],
                                         kf[:, h * T + c * 256 + mb * 128:
                                            h * T + c * 256 + mb * 128 + 128],
                                         cbf[:, C_ID:C_ID + 128]))
                            mtr = pefence(lambda cpar=cpar: p[4 + cpar][0:1, 384:385])
                            P.wait("v", cP, mtr)
                            P.op("v", lambda E, cpar=cpar:
                                 E.tensor_copy(ktm[cpar][:, :],
                                               pt[cpar][:, 0:256]))
                            mke = vfence()
                            P.wait("t", cV, mke)
                            P.wait("t", cV, m_vt_all)
                            msm = None
                            for mb in range(2):
                                def bmm(E, h=h, c=c, mb=mb, cpar=cpar):
                                    return E.matmul(
                                        p[4 + cpar][:, 0:128],
                                        ktm[cpar][:, mb * 128:mb * 128 + 128],
                                        vt[:, h * T + (c * 2 + mb) * 128:
                                           h * T + (c * 2 + mb) * 128 + 128],
                                        start=(mb == 0), stop=(mb == 1))
                                if mb == 1:
                                    msm = P.op("t", bmm, cP)
                                else:
                                    P.op("t", bmm)
                            P.wait("v", cP, msm)
                            msa = P.op("v", lambda E, h=h, cpar=cpar:
                                       E.tensor_add(S_run[:, h * D:h * D + 128],
                                                    S_run[:, h * D:h * D + 128],
                                                    p[4 + cpar][:, 0:128]), cV)
                            m_Sadd[(h, c)] = msa
                    m_wslot[ob] = m_proj[(ob, 1)]
                m_states_done = vfence()

                P.wait("g", cV, m_states_done)
                P.op("g", lambda E: E.dma_start(out=d_ccin[:, 0:2048],
                                                in_=S_run[:, :]), sSTG, 16)
                P.op("g", lambda E: E.dma_start(out=d_ccin[:, 2048:2064],
                                                in_=Z_run[:, :]), sSTG, 16)
                P.wait("g", sSTG, 32)
                P.op("g", lambda E: E.collective_compute(
                    "AllGather", OP.bypass,
                    replica_groups=[list(range(NCORES))],
                    ins=[d_ccin.ap().opt()],
                    outs=[d_ccout.ap().opt()]), sCC, 1)

                def q_pair(pi):
                    for i2 in range(2):
                        ob = 32 + 2 * pi + i2
                        h = ob - 32
                        P.wait("g", cP, m_wslot[ob - 3])
                        g_load(wbufA[:, (ob % 3) * 2048:(ob % 3) * 2048 + 2048],
                               d_wkvq[ob], sW3[ob % 3])
                        P.wait("t", sW3[ob % 3], sW3[ob % 3].n)
                        for half in range(2):
                            g2 = half
                            m = proj_group(g2, ob % 3, half)
                            m_proj[(ob, half)] = m
                            P.wait("s", cP, m)
                            if h >= 2:
                                P.wait("s", cV, m_qfin[(h - 2, half)])
                            else:
                                P.wait("s", cV, m_kfin[(15, half)])
                            P.op("s", lambda E, h=h, g2=g2, half=half:
                                 E.copy(kraw[h % 2][:, half * 512:half * 512 + 512],
                                        p[g2][:, :]))
                            me = sfence()
                            m_kact[(h, half)] = me
                            P.wait("t", cA, me)
                            mq, _ = rope_elu(
                                h, half,
                                qf[:, h * T + half * 512:h * T + half * 512 + 512],
                                None)
                            m_qfin[(h, half)] = mq
                        m_wslot[ob] = m_proj[(ob, 1)]

                q_pair(0)
                q_pair(1)
                q_pair(2)
                q_pair(3)

                P.wait("g", sCC, 1)
                m_pre = {}
                for r in range(NCORES):
                    if r >= 1:
                        P.wait("g", cV, m_pre[r - 1])
                    P.op("g", lambda E, r=r: E.dma_start(
                        out=pre_buf[0][:, :],
                        in_=d_ccout.ap()[r * 128:(r + 1) * 128, :]), sPRE2[r % 2], 16)
                    P.wait("v", sPRE2[r % 2], 16 * (r // 2 + 1))
                    if r == 0:
                        P.wait("v", sF, 32)
                    P.op("v", lambda E, r=r: E.scalar_tensor_tensor(
                        S_cur[:, :], pre_buf[0][:, 0:2048], pmask[:, r:r + 1],
                        S_cur[:, :], OP.mult, OP.add))
                    m_pre[r] = P.op("v", lambda E, r=r: E.scalar_tensor_tensor(
                        Z_cur[:, :], pre_buf[0][:, 2048:2064], pmask[:, r:r + 1],
                        Z_cur[:, :], OP.mult, OP.add), cV)
                m_prefix_done = vfence()

                if "dbg_scur" in dbg_out:
                    P.wait("y", cV, m_prefix_done)
                    P.op("y", lambda E: E.dma_start(out=dbg_out["dbg_scur"],
                                                    in_=S_cur[:, :]), sOUT, 16)

                m_den = {}
                m_N = {}
                m_ot = {}
                m_fm = {}
                m_ke = {}
                m_kv = {}
                m_SC = {}
                seq = [(c, pi) for pi in range(8) for c in range(CH)]
                pos_of = {k: n for n, k in enumerate(seq)}

                def prev(key, k):
                    n = pos_of[key] - k
                    return seq[n] if n >= 0 else None

                def get(dic, key):
                    return dic[key] if key is not None and key in dic else None

                m_ktmp = {}

                def ktm_hoist(pi):
                    hs = (2 * pi, 2 * pi + 1)
                    P.wait("t", cA, get(m_fm, (3, pi - 1) if pi else None))
                    for i2, h in enumerate(hs):
                        for c4 in range(CH):
                            for mb in range(2):
                                P.op("t", lambda E, h=h, c4=c4, mb=mb, i2=i2:
                                     E.transpose(
                                         pt[i2][:, (c4 * 2 + mb) * 128:
                                                (c4 * 2 + mb) * 128 + 128],
                                         kf[:, h * T + c4 * 256 + mb * 128:
                                            h * T + c4 * 256 + mb * 128 + 128],
                                         cbf[:, C_ID:C_ID + 128]))
                    mhtr = pefence(lambda: p[5][0:1, 502:503])
                    P.wait("s", cP, mhtr)
                    for i2 in range(2):
                        P.op("s", lambda E, i2=i2:
                             E.copy(ktmp[:, i2 * 1024:i2 * 1024 + 1024],
                                    pt[i2][:, 0:1024]))
                    m_ktmp[pi] = sfence()

                def attn_pair(c, pi):
                    key = (c, pi)
                    hs = (2 * pi, 2 * pi + 1)
                    P.wait("v", cP, get(m_N, prev(key, 1)))
                    for i2, h in enumerate(hs):
                        P.op("v", lambda E, h=h, i2=i2:
                             E.tensor_copy(sbf[i2][:, :], S_cur[:, h * D:h * D + 128]))
                        P.op("v", lambda E, h=h, i2=i2:
                             E.tensor_copy(zbf[i2][:, :], Z_cur[:, h:h + 1]))
                    mcz = vfence()
                    P.wait("t", cV, mcz)
                    P.wait("t", cV, get(m_den, prev(key, 1)))
                    mA = None
                    for i2, h in enumerate(hs):
                        for lb in range(2):
                            ptile = i2 * 2 + lb
                            P.op("t", lambda E, h=h, c=c, lb=lb, ptile=ptile:
                                 E.matmul(p[ptile][:, 0:256],
                                          qf[:, h * T + c * 256 + lb * 128:
                                             h * T + c * 256 + lb * 128 + 128],
                                          kf[:, h * T + c * 256:h * T + c * 256 + 256],
                                          start=True, stop=True))
                            mA = P.op("t", lambda E, h=h, c=c, lb=lb, ptile=ptile:
                                      E.matmul(p[ptile][:, 256:512],
                                               kf[:, h * T + c * 256 + lb * 128:
                                                  h * T + c * 256 + lb * 128 + 128],
                                               qf[:, h * T + c * 256:
                                                  h * T + c * 256 + 256],
                                               start=True, stop=True), cP)
                    P.wait("v", cP, mA)
                    P.wait("v", cP, get(m_N, prev(key, 1)))
                    for i2, h in enumerate(hs):
                        for lb in range(2):
                            ptile = i2 * 2 + lb
                            col = h * 2 + lb
                            moff = C_M0 if lb == 0 else C_M1
                            P.op("v", lambda E, ptile=ptile, moff=moff, col=col:
                                 E.scalar_tensor_tensor(
                                     jnk[:, :], p[ptile][:, 0:256], 1.0,
                                     cbf[:, moff:moff + 256], OP.bypass, OP.mult,
                                     accum_out=den_c[:, col:col + 1]))
                    m_amv = None
                    for i2, h in enumerate(hs):
                        for mb in range(2):
                            ptile = i2 * 2 + mb
                            mtoff = C_MT0 if mb == 0 else C_MT1
                            P.op("v", lambda E, ptile=ptile, i2=i2, mtoff=mtoff, mb=mb:
                                 E.tensor_mul(am_sb[i2][:, mb * 320:mb * 320 + 256],
                                              p[ptile][:, 256:512],
                                              cbf[:, mtoff:mtoff + 256]))
                    m_amv = vfence()
                    if dbg_am_sb is not None and key == (0, 0):
                        P.op("v", lambda E: E.tensor_copy(dbg_am_sb[:, :],
                                                          am_sb[0][:, :]), cV)
                    P.wait("t", cV, m_amv)
                    P.wait("t", cV, get(m_SC, prev(key, 1)))
                    P.wait("t", cA, get(m_ot, prev(key, 1)))
                    mN = None
                    for i2, h in enumerate(hs):
                        np_ = 4 + i2
                        for lb in range(2):
                            for mb in range(2):
                                P.op("t", lambda E, h=h, c=c, lb=lb, mb=mb, np_=np_,
                                     i2=i2:
                                     E.matmul(p[np_][:, lb * 128:lb * 128 + 128],
                                              am_sb[i2][:, mb * 320 + lb * 128:
                                                        mb * 320 + lb * 128 + 128],
                                              vt[:, h * T + (c * 2 + mb) * 128:
                                                 h * T + (c * 2 + mb) * 128 + 128],
                                              start=(mb == 0), stop=False))
                            P.op("t", lambda E, h=h, c=c, lb=lb, np_=np_, i2=i2:
                                 E.matmul(p[np_][:, lb * 128:lb * 128 + 128],
                                          qf[:, h * T + c * 256 + lb * 128:
                                             h * T + c * 256 + lb * 128 + 128],
                                          sbf[i2][:, :],
                                          start=False, stop=True))
                            mN = P.op("t", lambda E, h=h, c=c, lb=lb, np_=np_, i2=i2:
                                      E.matmul(p[np_][:, 300 + lb:301 + lb],
                                               qf[:, h * T + c * 256 + lb * 128:
                                                  h * T + c * 256 + lb * 128 + 128],
                                               zbf[i2][:, :], start=True, stop=True),
                                      cP)
                    m_N[key] = mN
                    P.wait("v", cP, mN)
                    for i2, h in enumerate(hs):
                        for lb in range(2):
                            col = h * 2 + lb
                            P.op("v", lambda E, col=col, np_=4 + i2, lb=lb:
                                 E.tensor_add(den_c[:, col:col + 1],
                                              den_c[:, col:col + 1],
                                              p[np_][:, 300 + lb:301 + lb]))
                    P.op("v", lambda E: E.drain())
                    cols0 = 4 * pi
                    mmx = P.op("v", lambda E, cols0=cols0:
                               E.tensor_scalar_max(den_c[:, cols0:cols0 + 4],
                                                   den_c[:, cols0:cols0 + 4], 1e-6),
                               cV)
                    P.wait("v", cV, mmx)
                    P.op("v", lambda E, cols0=cols0:
                         E.reciprocal(den_c[:, cols0:cols0 + 4],
                                      den_c[:, cols0:cols0 + 4]))
                    mden = vfence()
                    m_den[key] = mden
                    P.wait("s", cP, mN)
                    P.wait("s", cV, mden)
                    if dbg_num_sb is not None and key == (0, 0):
                        P.op("v", lambda E: E.tensor_copy(dbg_num_sb[:, 0:128],
                                                          p[4][:, 0:128]), cV)
                        P.op("v", lambda E: E.tensor_copy(dbg_num_sb[:, 128:256],
                                                          p[4][:, 128:256]), cV)
                    for i2, h in enumerate(hs):
                        for lb in range(2):
                            col = h * 2 + lb
                            P.op("s", lambda E, np_=4 + i2, col=col, i2=i2, lb=lb:
                                 E.activation(ot[i2][:, lb * 128:lb * 128 + 128],
                                              p[np_][:, lb * 128:lb * 128 + 128],
                                              AF.Copy, scale=den_c[:, col:col + 1]))
                    mot = sfence()
                    m_ot[key] = mot
                    P.wait("t", cA, mot)
                    for i2 in range(2):
                        for lb in range(2):
                            P.op("t", lambda E, i2=i2, lb=lb:
                                 E.transpose(
                                     pt[i2][:, 512 + lb * 128:512 + lb * 128 + 128],
                                     ot[i2][:, lb * 128:lb * 128 + 128],
                                     cbf[:, C_ID:C_ID + 128]))
                    mtr2 = pefence(lambda: p[4][0:1, 500:501])
                    P.wait("s", cP, mtr2)
                    for i2, h in enumerate(hs):
                        P.op("s", lambda E, h=h, c=c, i2=i2:
                             E.copy(qf[:, h * T + c * 256:h * T + c * 256 + 256],
                                    pt[i2][:, 512:768]))
                    mfm = sfence()
                    m_fm[key] = mfm
                    P.wait("t", cA, m_ktmp[pi])
                    P.wait("t", cA, mot)
                    P.wait("t", cV, mden)
                    mkv = None
                    for i2, h in enumerate(hs):
                        for mb in range(2):
                            def bmm(E, h=h, c=c, mb=mb, i2=i2):
                                return E.matmul(
                                    p[4 + i2][:, 352:480],
                                    ktmp[:, i2 * 1024 + (c * 2 + mb) * 128:
                                         i2 * 1024 + (c * 2 + mb) * 128 + 128],
                                    vt[:, h * T + (c * 2 + mb) * 128:
                                       h * T + (c * 2 + mb) * 128 + 128],
                                    start=(mb == 0), stop=(mb == 1))
                            if i2 == 1 and mb == 1:
                                mkv = P.op("t", bmm, cP)
                            else:
                                P.op("t", bmm)
                    m_kv[key] = mkv
                    for i2, h in enumerate(hs):
                        P.op("v", lambda E, h=h, c=c, i2=i2:
                             E.reduce_sum(ztmp[:, i2:i2 + 1],
                                          kf[:, h * T + c * 256:h * T + c * 256 + 256],
                                          AX))
                    P.wait("v", cP, mkv)
                    for i2, h in enumerate(hs):
                        P.op("v", lambda E, h=h, i2=i2:
                             E.tensor_add(S_cur[:, h * D:h * D + 128],
                                          S_cur[:, h * D:h * D + 128],
                                          p[4 + i2][:, 352:480]))
                    for i2, h in enumerate(hs):
                        P.op("v", lambda E, h=h, i2=i2:
                             E.tensor_add(Z_cur[:, h:h + 1], Z_cur[:, h:h + 1],
                                          ztmp[:, i2:i2 + 1]))
                    msz = vfence()
                    m_SC[key] = msz

                P.wait("t", cV, m_prefix_done)
                P.wait("t", cV, max(m_kfin.values()))
                P.wait("t", cV, m_vt_all)
                for pi in range(8):
                    P.wait("t", cV, m_qfin[(2 * pi + 1, 1)])
                    if pi >= 1 and (2 * (pi + 3) + 1, 1) in m_qfin:
                        P.wait("t", cV, m_qfin[(2 * (pi + 3) + 1, 1)])
                        P.wait("t", cA, m_kact[(2 * (pi + 3) + 1, 1)])
                    ktm_hoist(pi)
                    for c in range(CH):
                        attn_pair(c, pi)
                    if pi + 4 < 8:
                        q_pair(pi + 4)

            m_attn_done = m_fm[(CH - 1, 7)]
            m_attn_dve = vfence()
            m_attn_act = sfence()

            for nm, sbv in (("dbg_den", dbg_den_sb), ("dbg_am", dbg_am_sb),
                            ("dbg_num", dbg_num_sb)):
                if nm in dbg_out and sbv is not None:
                    P.wait("y", cV, m_attn_dve)
                    P.op("y", lambda E, nm=nm, sbv=sbv: E.dma_start(
                        out=dbg_out[nm], in_=sbv[:, :]), sOUT, 16)
            if "dbg_attn" in dbg_out:
                P.wait("y", cA, m_attn_done)
                P.op("y", lambda E: E.dma_start(out=dbg_out["dbg_attn"],
                                                in_=qf[:, :]), sOUT, 16)

        with ExitStack() as st4:
            def sb4(name, shape, dt):
                return st4.enter_context(nc.sbuf_tensor(name, shape, dt))

            resid = sb4("resid", [128, HCN * T], F32)
            wgu_buf = sb4("wgu_buf", [128, 2 * 4096], BF16)
            wd_buf = sb4("wd_buf", [128, 2 * 1536], BF16)
            sq_sb = sb4("sq_sb", [128, 512], F32)
            stat1 = sb4("stat1", [1, 1024], F32)
            stat2 = sb4("stat2", [1, 1024], F32)
            nmbB = sb4("nmbB", [128, 1024], F32)
            rbB = sb4("rbB", [128, 1024], F32)
            lg_sb = sb4("lg_sb", [32, 1024], BF16)
            g_tm = sb4("g_tm", [128, 256], F32)
            p_sb = sb4("p_sb", [128, 256], F32)
            w32 = [sb4(f"w32_{i}", [128, 32], F32) for i in range(8)]
            red = sb4("red", [128, 8], F32)
            ce_fm = sb4("ce_fm", [NEXP, T], BF16)
            ce_b = [sb4(f"ce_b{i}", [128, T], BF16) for i in range(2)]
            h_buf = [sb4(f"h_buf{i}", [128, 6 * T], BF16) for i in range(2)]
            sg_sb = [sb4(f"sg{i}", [128, 512], BF16) for i in range(2)]
            usc_sb = [sb4(f"usc{i}", [128, 512], BF16) for i in range(2)]

            EMBf = float(EMB)

            P.wait("y", cV, m_attn_dve)
            P.wait("y", cA, m_attn_act)
            for i in range(4):
                src_i = bass.AP(d_x.tensor, i * 4 * 128 * T,
                                [[T, 128], [128 * T, 4], [1, T]])
                P.op("y", lambda E, i=i, src_i=src_i: E.dma_start(
                    out=resid[:, i * 4 * T:(i + 1) * 4 * T],
                    in_=src_i), sRES, 16)

            m_resid = {}
            m_sqm = {}
            prev_sq_pe = None
            P.wait("t", cV, m_attn_dve)
            P.wait("t", cA, m_attn_act)
            for ob in range(16):
                guard = m_wslot[ob - 3] if ob >= 3 else m_wslot[45 + ob]
                P.wait("g", cP, guard)
                g_load(wbufA[:, (ob % 3) * 2048:(ob % 3) * 2048 + 2048],
                       d_wout[ob], sW3[ob % 3])
                P.wait("t", sW3[ob % 3], sW3[ob % 3].n)
                for half in range(2):
                    g2 = half
                    if ob >= 1:
                        P.wait("t", cV, m_resid[(ob - 1, half)])
                    mwo = None
                    for ec in range(HCN):
                        def bld(E, ec=ec, g2=g2, ob=ob, half=half):
                            return E.matmul(
                                p[g2][:, :],
                                wbufA[:, (ob % 3) * 2048 + ec * 128:
                                      (ob % 3) * 2048 + ec * 128 + 128],
                                qf[:, ec * T + half * 512:ec * T + half * 512 + 512],
                                start=(ec == 0), stop=(ec == HCN - 1))
                        if ec == HCN - 1:
                            mwo = P.op("t", bld, cP)
                        else:
                            P.op("t", bld)
                    m_wslot[("wo", ob)] = mwo
                    P.wait("v", cP, mwo)
                    P.wait("v", sRES, 64)
                    P.op("v", lambda E, ob=ob, half=half, g2=g2:
                         E.tensor_add(
                             resid[:, ob * T + half * 512:ob * T + half * 512 + 512],
                             resid[:, ob * T + half * 512:ob * T + half * 512 + 512],
                             p[g2][:, :]))
                    mr = vfence()
                    m_resid[(ob, half)] = mr
                    P.wait("s", cV, mr)
                    P.wait("s", cP, prev_sq_pe)
                    P.op("s", lambda E, ob=ob, half=half:
                         E.activation(sq_sb[:, :],
                                      resid[:, ob * T + half * 512:
                                            ob * T + half * 512 + 512],
                                      AF.Square))
                    msq = sfence()
                    P.wait("t", cV, mr)
                    P.op("t", lambda E, ob=ob, half=half:
                         E.matmul(p[4 + half][0:1, :], cf[:, CF_ONE:CF_ONE + 1],
                                  resid[:, ob * T + half * 512:ob * T + half * 512 + 512],
                                  start=(ob == 0), stop=(ob == 15)))
                    P.wait("t", cA, msq)
                    msqm = P.op("t", lambda E, ob=ob, half=half:
                                E.matmul(p[2 + half][0:1, :], cf[:, CF_ONE:CF_ONE + 1],
                                         sq_sb[:, :],
                                         start=(ob == 0), stop=(ob == 15)), cP)
                    m_sqm[(ob, half)] = msqm
                    prev_sq_pe = msqm
                m_wslot[ob] = m_wslot[("wo", ob)]
            m_stats = max(m_sqm.values())

            def ln_finalize_apply(m_statmm, write_x2b, sum_b=4, nm_b=0):
                P.wait("v", cP, m_statmm)
                for half in range(2):
                    P.op("v", lambda E, half=half, sum_b=sum_b:
                         E.tensor_copy(stat1[:, half * 512:half * 512 + 512],
                                       p[sum_b + half][0:1, :]))
                    P.op("v", lambda E, half=half:
                         E.tensor_copy(stat2[:, half * 512:half * 512 + 512],
                                       p[2 + half][0:1, :]))
                P.op("v", lambda E: E.tensor_scalar_mul(stat1[:, :], stat1[:, :],
                                                        float(-1.0 / EMBf)))
                P.op("v", lambda E: E.tensor_scalar_mul(stat2[:, :], stat2[:, :],
                                                        float(1.0 / EMBf)))
                mrow = P.op("v", lambda E: E.tensor_mul(nmbB[0:1, 0:1024],
                                                        stat1[:, :], stat1[:, :]),
                            cV)
                P.wait("v", cV, mrow)
                mvar = P.op("v", lambda E: E.tensor_sub(stat2[:, :], stat2[:, :],
                                                        nmbB[0:1, 0:1024]), cV)
                P.wait("v", cV, mvar)
                P.wait("s", cV, mvar)
                P.op("s", lambda E: E.activation(stat2[:, :], stat2[:, :],
                                                 AF.Sqrt,
                                                 bias=cf[0:1, CF_EPS:CF_EPS + 1]))
                msd = sfence()
                P.wait("v", cA, msd)
                P.op("v", lambda E: E.reciprocal(stat2[:, :], stat2[:, :]))
                mrstd = vfence()
                P.wait("t", cV, mrstd)
                mbc = None
                for half in range(2):
                    P.op("t", lambda E, half=half, nm_b=nm_b:
                         E.matmul(p[nm_b + half][:, :], cf[0:1, CF_ROW:CF_ROW + 128],
                                  stat1[:, half * 512:half * 512 + 512],
                                  start=True, stop=True))
                    mbc = P.op("t", lambda E, half=half:
                               E.matmul(p[2 + half][:, :], cf[0:1, CF_ROW:CF_ROW + 128],
                                        stat2[:, half * 512:half * 512 + 512],
                                        start=True, stop=True), cP)
                P.wait("v", cP, mbc)
                for half in range(2):
                    P.op("v", lambda E, half=half, nm_b=nm_b:
                         E.tensor_copy(nmbB[:, half * 512:half * 512 + 512],
                                       p[nm_b + half][:, :]))
                    P.op("v", lambda E, half=half:
                         E.tensor_copy(rbB[:, half * 512:half * 512 + 512],
                                       p[2 + half][:, :]))
                m_apply = {}
                for ec in range(HCN):
                    mxb = None
                    for half in range(2):
                        lo = ec * T + half * 512
                        P.op("v", lambda E, lo=lo, half=half:
                             E.tensor_add(resid[:, lo:lo + 512], resid[:, lo:lo + 512],
                                          nmbB[:, half * 512:half * 512 + 512]))
                        P.op("v", lambda E, lo=lo, half=half:
                             E.tensor_mul(resid[:, lo:lo + 512],
                                          resid[:, lo:lo + 512],
                                          rbB[:, half * 512:half * 512 + 512]))
                        mx2 = vfence()
                        if write_x2b:
                            P.wait("s", cV, mx2)
                            P.op("s", lambda E, lo=lo:
                                 E.copy(qf[:, lo:lo + 512],
                                        resid[:, lo:lo + 512]))
                            mxb = sfence()
                        else:
                            mxb = mx2
                    m_apply[ec] = mxb
                return m_apply

            m_x2b = ln_finalize_apply(m_stats, True)
            m_ln1_done = max(m_x2b.values())

            if "dbg_x2" in dbg_out:
                P.wait("y", cA, m_ln1_done)
                P.op("y", lambda E: E.dma_start(
                    out=bass.AP(dbg_out["dbg_x2"].tensor, 0,
                                [[T, 128], [128 * T, HCN], [1, T]]),
                    in_=resid[:, :]), sOUT, 16)

            m_gu_mm = {}
            m_wd_mm = {}
            m_hband = {}
            m_sg = {}
            m_dn_ev = {}
            m_ceb = {}

            def load_gu(idx, guard):
                e_, j_ = divmod(idx, 6)
                P.wait("g", cP, guard)
                P.op("g", lambda E, e_=e_, j_=j_, idx=idx: E.dma_start(
                    out=wgu_buf[:, (idx % 2) * 4096:(idx % 2) * 4096 + 4096],
                    in_=d_wgu[e_, j_]), sGU2[idx % 2], 16)

            def load_wd(idx, guard):
                e_, q_ = divmod(idx, 8)
                P.wait("g", cP, guard)
                P.op("g", lambda E, e_=e_, q_=q_, idx=idx: E.dma_start(
                    out=wd_buf[:, (idx % 2) * 1536:(idx % 2) * 1536 + 1536],
                    in_=d_wd[e_][:, q_ * 1536:q_ * 1536 + 1536]), sWD2[idx % 2], 16)

            P.wait("g", cV, m_attn_dve)
            P.wait("g", cA, m_attn_act)
            load_gu(0, None)
            load_gu(1, None)
            load_wd(0, None)
            load_wd(1, None)

            def emit_ce_half(e, half, extra_tV=None):
                ep = e % 2
                P.wait("t", cV, extra_tV)
                mcb = P.op("t", lambda E, e=e, half=half:
                           E.matmul(p[4 + half][:, :],
                                    cbf[0:32, C_SEL + e * 128:C_SEL + e * 128 + 128],
                                    ce_fm[0:32, half * 512:half * 512 + 512],
                                    start=True, stop=True), cP)
                P.wait("v", cP, mcb)
                P.op("v", lambda E, ep=ep, half=half:
                     E.tensor_copy(ce_b[ep][:, half * 512:half * 512 + 512],
                                   p[4 + half][:, :]))
                m_ceb[(e, half)] = vfence()

            def emit_gu_band_mms(e, j, half):
                ep = e % 2
                idx = e * 6 + j
                gt_, ut_ = 2 * half, 2 * half + 1
                P.wait("t", sGU2[idx % 2], 16 * (idx // 2 + 1))
                if e >= 2 and j == 0 and half == 0:
                    P.wait("t", cP, m_wd_mm[(e - 2, 7)])
                pkey = (e, j - 1, half) if j >= 1 else                        ((e - 1, 5, half) if e >= 1 else None)
                if pkey is not None:
                    P.wait("t", cV, m_hband[pkey])
                    P.wait("t", cA, m_sg[pkey])
                mgu = None
                for hc in range(HCN):
                    if e == 0 and j == 0:
                        P.wait("t", cA, m_x2b[hc])
                    P.op("t", lambda E, hc=hc, half=half, gt_=gt_, idx=idx:
                         E.matmul(p[gt_][:, :],
                                  wgu_buf[:, (idx % 2) * 4096 + hc * 256:
                                          (idx % 2) * 4096 + hc * 256 + 128],
                                  qf[:, hc * T + half * 512:
                                     hc * T + half * 512 + 512],
                                  start=(hc == 0), stop=(hc == HCN - 1)))
                    def bldu(E, hc=hc, half=half, ut_=ut_, idx=idx):
                        return E.matmul(
                            p[ut_][:, :],
                            wgu_buf[:, (idx % 2) * 4096 + hc * 256 + 128:
                                    (idx % 2) * 4096 + hc * 256 + 256],
                            qf[:, hc * T + half * 512:hc * T + half * 512 + 512],
                            start=(hc == 0), stop=(hc == HCN - 1))
                    if hc == HCN - 1:
                        mgu = P.op("t", bldu, cP)
                    else:
                        P.op("t", bldu)
                m_gu_mm[(e, j)] = mgu
                if half == 1 and e * 6 + j + 2 < NEXP * 6:
                    load_gu(e * 6 + j + 2, mgu)
                return mgu

            def emit_gu_evacs(e, j, half, mgu):
                ep = e % 2
                gt_, ut_ = 2 * half, 2 * half + 1
                P.wait("s", cP, mgu)
                P.op("s", lambda E, half=half, gt_=gt_:
                     E.activation(sg_sb[half][:, :], p[gt_][:, :], AF.Silu))
                msg = sfence()
                m_sg[(e, j, half)] = msg
                P.wait("v", cP, mgu)
                P.wait("v", cV, m_ceb[(e, half)])
                P.op("v", lambda E, half=half, ut_=ut_, ep=ep:
                     E.tensor_mul(usc_sb[half][:, :], p[ut_][:, :],
                                  ce_b[ep][:, half * 512:half * 512 + 512]))
                P.wait("v", cA, msg)
                P.op("v", lambda E, half=half, ep=ep, j=j:
                     E.tensor_mul(
                         h_buf[ep][:, j * T + half * 512:j * T + half * 512 + 512],
                         sg_sb[half][:, :], usc_sb[half][:, :]))
                m_hband[(e, j, half)] = vfence()

            def emit_gu_band(e, j, half):
                emit_gu_evacs(e, j, half, emit_gu_band_mms(e, j, half))

            mlg = None
            for half in range(2):
                for hc in range(HCN):
                    if half == 0:
                        P.wait("t", cA, m_x2b[hc])
                    def bld(E, hc=hc, half=half):
                        return E.matmul(
                            p[4 + half][0:32, :], gate_bf[:, hc * 32:hc * 32 + 32],
                            qf[:, hc * T + half * 512:hc * T + half * 512 + 512],
                            start=(hc == 0), stop=(hc == HCN - 1))
                    if hc == HCN - 1:
                        mlg = P.op("t", bld, cP)
                    else:
                        P.op("t", bld)
            P.wait("v", cP, mlg)
            P.op("v", lambda E: E.tensor_copy(lg_sb[:, 0:512], p[4][0:32, :]))
            P.op("v", lambda E: E.tensor_copy(lg_sb[:, 512:1024], p[5][0:32, :]))
            mlge = vfence()
            P.wait("t", cV, mlge)
            for tb in range(8):
                P.op("t", lambda E, tb=tb:
                     E.transpose(pt[0][0:128, tb * 32:tb * 32 + 32],
                                 lg_sb[0:32, tb * 128:tb * 128 + 128],
                                 cbf[0:32, C_ID:C_ID + 32]))
            mgt = pefence(lambda: p[1][0:1, 0:1])
            P.wait("v", cP, mgt)
            mgtm = P.op("v", lambda E: E.tensor_copy(g_tm[:, :], pt[0][:, 0:256]), cV)
            P.wait("v", cV, mgtm)

            m_cetr = {}
            mcv_prev = None
            for pair in range(4):
                tiles = (2 * pair, 2 * pair + 1)
                mm_ = [w32[0], w32[4]]
                wv_ = [w32[1], w32[5]]
                cnt_ = [w32[2], w32[6]]
                Gg_ = [w32[3], w32[7]]
                rr_ = [0, 4]

                def zip2(fn, inc=None):
                    out = None
                    for i_ in range(2):
                        b_ = fn(i_, tiles[i_])
                        if i_ == 1 and inc is not None:
                            out = P.op("v", b_, inc)
                        else:
                            P.op("v", b_)
                    return out

                zip2(lambda i, tb: (lambda E, i=i, tb=tb:
                     E.reduce_max(red[:, rr_[i]:rr_[i] + 1],
                                  g_tm[:, tb * 32:tb * 32 + 32], AX,
                                  negate=True)))
                mneg = vfence()
                P.wait("s", cV, mneg)
                for i_ in range(2):
                    P.op("s", lambda E, i=i_, tb=tiles[i_]:
                         E.activation(mm_[i][:, :],
                                      g_tm[:, tb * 32:tb * 32 + 32],
                                      AF.Exp, bias=red[:, rr_[i]:rr_[i] + 1]))
                mexp = sfence()
                P.wait("v", cA, mexp)
                zip2(lambda i, tb: (lambda E, i=i:
                     E.reduce_sum(red[:, rr_[i] + 1:rr_[i] + 2], mm_[i][:, :], AX)))
                zip2(lambda i, tb: (lambda E, i=i:
                     E.reciprocal(red[:, rr_[i] + 1:rr_[i] + 2],
                                  red[:, rr_[i] + 1:rr_[i] + 2])))
                zip2(lambda i, tb: (lambda E, i=i, tb=tb:
                     E.tensor_single_scalar(p_sb[:, tb * 32:tb * 32 + 32],
                                            mm_[i][:, :],
                                            red[:, rr_[i] + 1:rr_[i] + 2],
                                            OP.mult)))
                for e32 in range(NEXP):
                    zip2(lambda i, tb: (lambda E, i=i, tb=tb, e32=e32:
                         E.tensor_scalar(
                             mm_[i][:, :], p_sb[:, tb * 32:tb * 32 + 32],
                             p_sb[:, tb * 32 + e32:tb * 32 + e32 + 1], 0.0,
                             OP.is_ge, OP.add,
                             accum_out=cnt_[i][:, e32:e32 + 1])))
                    zip2(lambda i, tb: (lambda E, i=i, tb=tb, e32=e32:
                         E.scalar_tensor_tensor(
                             wv_[i][:, :], p_sb[:, tb * 32:tb * 32 + 32],
                             p_sb[:, tb * 32 + e32:tb * 32 + e32 + 1],
                             p_sb[:, tb * 32:tb * 32 + 32],
                             OP.is_ge, OP.mult,
                             accum_out=Gg_[i][:, e32:e32 + 1])))
                zip2(lambda i, tb: (lambda E, i=i:
                     E.tensor_single_scalar(mm_[i][:, :], cnt_[i][:, :], 8.0,
                                            OP.is_le)))
                zip2(lambda i, tb: (lambda E, i=i, tb=tb:
                     E.tensor_mul(wv_[i][:, :], mm_[i][:, :],
                                  p_sb[:, tb * 32:tb * 32 + 32])))
                zip2(lambda i, tb: (lambda E, i=i:
                     E.reduce_sum(red[:, rr_[i] + 2:rr_[i] + 3], wv_[i][:, :], AX)))
                zip2(lambda i, tb: (lambda E, i=i:
                     E.tensor_scalar_mul(red[:, rr_[i] + 2:rr_[i] + 3],
                                         red[:, rr_[i] + 2:rr_[i] + 3], 0.8)))
                zip2(lambda i, tb: (lambda E, i=i, tb=tb:
                     E.tensor_sub(Gg_[i][:, :], Gg_[i][:, :],
                                  p_sb[:, tb * 32:tb * 32 + 32])))
                zip2(lambda i, tb: (lambda E, i=i:
                     E.tensor_single_scalar(Gg_[i][:, :], Gg_[i][:, :],
                                            red[:, rr_[i] + 2:rr_[i] + 3],
                                            OP.is_lt)))
                zip2(lambda i, tb: (lambda E, i=i:
                     E.tensor_mul(mm_[i][:, :], mm_[i][:, :], Gg_[i][:, :])))
                zip2(lambda i, tb: (lambda E, i=i, tb=tb:
                     E.tensor_mul(wv_[i][:, :], mm_[i][:, :],
                                  p_sb[:, tb * 32:tb * 32 + 32])))
                zip2(lambda i, tb: (lambda E, i=i:
                     E.reduce_sum(red[:, rr_[i] + 3:rr_[i] + 4], wv_[i][:, :], AX)))
                zip2(lambda i, tb: (lambda E, i=i:
                     E.tensor_scalar_max(red[:, rr_[i] + 3:rr_[i] + 4],
                                         red[:, rr_[i] + 3:rr_[i] + 4], 1e-12)))
                zip2(lambda i, tb: (lambda E, i=i:
                     E.reciprocal(red[:, rr_[i] + 3:rr_[i] + 4],
                                  red[:, rr_[i] + 3:rr_[i] + 4])))
                zip2(lambda i, tb: (lambda E, i=i:
                     E.tensor_single_scalar(wv_[i][:, :], wv_[i][:, :],
                                            red[:, rr_[i] + 3:rr_[i] + 4],
                                            OP.mult)))
                if pair == 0:
                    mgu00 = emit_gu_band_mms(0, 0, 0)
                elif pair == 1:
                    mgu01 = emit_gu_band_mms(0, 0, 1)
                elif pair == 2:
                    emit_gu_evacs(0, 0, 0, mgu00)
                    mgu10 = emit_gu_band_mms(0, 1, 0)
                elif pair == 3:
                    mgu11 = None
                mce = vfence()
                P.wait("t", cV, mce)
                for i_ in range(2):
                    tb = tiles[i_]
                    P.wait("t", cV, mcv_prev)
                    P.op("t", lambda E, i=i_:
                         E.transpose(p[4][0:32, 0:128], wv_[i][:, :],
                                     cf[:, CF_ID:CF_ID + 128]))
                    mct = pefence(lambda: p[5][0:1, 400:401])
                    P.wait("v", cP, mct)
                    P.op("v", lambda E, tb=tb:
                         E.tensor_copy(ce_fm[0:32, tb * 128:tb * 128 + 128],
                                       p[4][0:32, 0:128]))
                    mcv = vfence()
                    m_cetr[tb] = mcv
                    mcv_prev = mcv
                if pair == 1:
                    emit_ce_half(0, 0, extra_tV=m_cetr[3])
                    mcv_prev = m_ceb[(0, 0)]
                elif pair == 3:
                    emit_ce_half(0, 1, extra_tV=m_cetr[7])
                    emit_gu_evacs(0, 0, 1, mgu01)
                    emit_gu_evacs(0, 1, 0, mgu10)
            m_ce_done = m_cetr[7]

            if "dbg_ce" in dbg_out:
                P.wait("y", cV, m_ce_done)
                P.op("y", lambda E: E.dma_start(out=dbg_out["dbg_ce"],
                                                in_=ce_fm[:, :]), sOUT, 16)

            for e in range(NEXP):
                ep = e % 2
                if e >= 1:
                    for half in range(2):
                        emit_ce_half(e, half,
                                     extra_tV=m_dn_ev[(e - 1, 15, 1)]
                                     if half == 0 else None)
                for j in range(6):
                    for half in range(2):
                        if e == 0 and (j == 0 or (j == 1 and half == 0)):
                            continue
                        emit_gu_band(e, j, half)
                m_h_done = max(m_hband[(e, j, hh)] for j in range(6)
                               for hh in range(2))

                for oc in range(16):
                    qq = oc // 2
                    idx = e * 8 + qq
                    for half in range(2):
                        dt_ = 4 + half
                        P.wait("t", sWD2[idx % 2], 16 * (idx // 2 + 1))
                        P.wait("t", cV, m_h_done)
                        pkey = (e, oc - 1, half) if oc >= 1 else \
                               ((e - 1, 15, half) if e >= 1 else None)
                        if pkey is not None and pkey in m_dn_ev:
                            P.wait("t", cV, m_dn_ev[pkey])
                        woff = (idx % 2) * 1536 + (oc % 2) * 768
                        mdn = None
                        for ic in range(6):
                            def bld(E, ic=ic, woff=woff, half=half, dt_=dt_, ep=ep):
                                return E.matmul(
                                    p[dt_][:, :],
                                    wd_buf[:, woff + ic * 128:woff + ic * 128 + 128],
                                    h_buf[ep][:, ic * T + half * 512:
                                              ic * T + half * 512 + 512],
                                    start=(ic == 0), stop=(ic == 5))
                            if ic == 5:
                                mdn = P.op("t", bld, cP)
                            else:
                                P.op("t", bld)
                        if half == 1 and oc % 2 == 1:
                            m_wd_mm[(e, qq)] = mdn
                            if e * 8 + qq + 2 < NEXP * 8:
                                load_wd(e * 8 + qq + 2, mdn)
                        P.wait("v", cP, mdn)
                        if e == NEXP - 1:
                            P.op("v", lambda E, oc=oc, half=half, dt_=dt_:
                                 E.tensor_add(
                                     resid[:, oc * T + half * 512:
                                           oc * T + half * 512 + 512],
                                     resid[:, oc * T + half * 512:
                                           oc * T + half * 512 + 512],
                                     p[dt_][:, :]))
                            mde = vfence()
                        else:
                            mde = P.op("v", lambda E, oc=oc, half=half, dt_=dt_:
                                       E.tensor_add(
                                           resid[:, oc * T + half * 512:
                                                 oc * T + half * 512 + 512],
                                           resid[:, oc * T + half * 512:
                                                 oc * T + half * 512 + 512],
                                           p[dt_][:, :]), cV)
                        m_dn_ev[(e, oc, half)] = mde

            m_dn_all = m_dn_ev[(NEXP - 1, 15, 1)]
            m_sg_all = m_sg[(NEXP - 1, 5, 1)]

            prev_sq_pe = None
            m_sqm2 = {}
            P.wait("t", cV, max(m_hband[(NEXP - 1, 5, hh)] for hh in range(2)))
            P.wait("t", cA, m_sg_all)
            for ec in range(HCN):
                for half in range(2):
                    mlast = m_dn_ev[(NEXP - 1, ec, half)]
                    P.wait("s", cV, mlast)
                    P.wait("s", cP, prev_sq_pe)
                    P.op("s", lambda E, ec=ec, half=half:
                         E.activation(sq_sb[:, :],
                                      resid[:, ec * T + half * 512:
                                            ec * T + half * 512 + 512],
                                      AF.Square))
                    msqa = sfence()
                    P.wait("t", cV, mlast)
                    P.op("t", lambda E, ec=ec, half=half:
                         E.matmul(p[half][0:1, :], cf[:, CF_ONE:CF_ONE + 1],
                                  resid[:, ec * T + half * 512:ec * T + half * 512 + 512],
                                  start=(ec == 0), stop=(ec == 15)))
                    P.wait("t", cA, msqa)
                    msq2 = P.op("t", lambda E, half=half, ec=ec:
                                E.matmul(p[2 + half][0:1, :], cf[:, CF_ONE:CF_ONE + 1],
                                         sq_sb[:, :],
                                         start=(ec == 0), stop=(ec == 15)), cP)
                    m_sqm2[(ec, half)] = msq2
                    prev_sq_pe = msq2
            m_stats2 = max(m_sqm2.values())
            m_out = ln_finalize_apply(m_stats2, False, sum_b=0, nm_b=4)
            for ec in range(HCN):
                P.wait("y", cV, m_out[ec])
                P.op("y", lambda E, ec=ec: E.dma_start(
                    out=d_out[ec * 128:(ec + 1) * 128, :],
                    in_=resid[:, ec * T:(ec + 1) * T]), sOUT, 16)
            P.wait("y", sOUT, sOUT.n)

        with nc.Block() as block:
            @block.tensor
            def _(t):
                for f in P.q["t"]:
                    f(t)

            @block.vector
            def _(v):
                for f in P.q["v"]:
                    f(v)

            @block.scalar
            def _(s_):
                for f in P.q["s"]:
                    f(s_)

            @block.gpsimd
            def _(g):
                for f in P.q["g"]:
                    f(g)

            @block.sync
            def _(y):
                for f in P.q["y"]:
                    f(y)

    return nc



def _tile_T(blk):
    K = blk.shape[1]
    n = K // 128
    return np.ascontiguousarray(
        blk.T.reshape(n, 128, 128).transpose(1, 0, 2).reshape(128, n * 128))


def _host_prep(x, wqkv, wo, gw, wgu, wd):
    x = np.asarray(x, np.float32)
    wqkv = np.asarray(wqkv, np.float32)
    wo = np.asarray(wo, np.float32)
    gw = np.asarray(gw, np.float32)
    wgu = np.asarray(wgu, np.float32)
    wd = np.asarray(wd, np.float32)

    tiles = []
    for h in range(16):
        tiles.append(_tile_T(wqkv[4096 + h * 128:4096 + (h + 1) * 128, :]))
    for h in range(16):
        tiles.append(_tile_T(wqkv[2048 + h * 128:2048 + (h + 1) * 128, :]))
    for h in range(16):
        tiles.append(_tile_T(wqkv[h * 128:(h + 1) * 128, :]))
    wkvq_t = np.stack(tiles)
    wout_t = np.stack([_tile_T(wo[ob * 128:(ob + 1) * 128, :])
                       for ob in range(16)])
    gate_t = np.ascontiguousarray(
        gw.T.reshape(16, 128, 32).transpose(1, 0, 2).reshape(128, 512))
    wgu_t = np.empty((NEXP, 6, 128, 4096), np.float32)
    for e in range(NEXP):
        for j in range(6):
            tg = _tile_T(wgu[e, j * 128:(j + 1) * 128, :]).reshape(128, 16, 128)
            tu = _tile_T(wgu[e, 768 + j * 128:768 + (j + 1) * 128, :]
                         ).reshape(128, 16, 128)
            wgu_t[e, j] = np.concatenate([tg, tu], axis=2).reshape(128, 4096)
    wd_t = np.empty((NEXP, 128, 12288), np.float32)
    for e in range(NEXP):
        cols = [_tile_T(wd[e, oc * 128:(oc + 1) * 128, :]) for oc in range(16)]
        wd_t[e] = np.concatenate(cols, axis=1)

    cbf = np.zeros((128, CBF_W), np.float32)
    cbf[:, C_ID:C_ID + 128] = np.eye(128)
    RT = np.zeros((128, 128), np.float32)
    for i in range(64):
        RT[2 * i + 1, 2 * i] = -1.0
        RT[2 * i, 2 * i + 1] = 1.0
    cbf[:, C_RT:C_RT + 128] = RT
    l_idx = np.arange(128)[:, None]
    m_idx = np.arange(256)[None, :]
    cbf[:, C_M0:C_M0 + 256] = (m_idx <= l_idx).astype(np.float32)
    cbf[:, C_M1:C_M1 + 256] = (m_idx <= l_idx + 128).astype(np.float32)
    cbf[:, C_ONE] = 1.0
    m_idx2 = np.arange(256)[None, :]
    p_idx = np.arange(128)[:, None]
    cbf[:, C_MT0:C_MT0 + 256] = (m_idx2 >= p_idx).astype(np.float32)
    cbf[:, C_MT1:C_MT1 + 256] = (m_idx2 >= p_idx + 128).astype(np.float32)
    for e in range(NEXP):
        cbf[e, C_SEL + e * 128:C_SEL + (e + 1) * 128] = 1.0
    cfc = np.zeros((128, CF_W), np.float32)
    cfc[:, CF_ID:CF_ID + 128] = np.eye(128)
    cfc[:, CF_ONE] = 1.0
    cfc[0, CF_ROW:CF_ROW + 128] = 1.0
    cfc[0, CF_EPS] = EPS

    inv_freq = 1.0 / (10000.0 ** (np.arange(0, 128, 2, dtype=np.float32) / 128))
    pos = np.arange(S, dtype=np.float32)
    freqs = np.outer(pos, inv_freq)
    cos_full = np.cos(freqs).astype(np.float32)
    sin_full = np.sin(freqs).astype(np.float32)

    maps = []
    for core in range(NCORES):
        b, pp = divmod(core, 4)
        t0 = pp * T
        xs = np.ascontiguousarray(x[b, t0:t0 + T, :].T)
        cos_i = np.ascontiguousarray(
            np.concatenate([cos_full[t0:t0 + T], cos_full[t0:t0 + T]], axis=1).T)
        sin_i = np.ascontiguousarray(
            np.concatenate([sin_full[t0:t0 + T], sin_full[t0:t0 + T]], axis=1).T)
        pm = np.zeros((128, 8), np.float32)
        pm[:, b * 4:b * 4 + pp] = 1.0
        maps.append(dict(x=xs, cos=cos_i, sin=sin_i, pmask=pm, cbf=cbf, cf=cfc,
                         gate=gate_t, wkvq=wkvq_t, wout=wout_t, wgu=wgu_t,
                         wd=wd_t))
    return maps


def kernel(L0_attn_x, L0_attn_W_qkv, L0_attn_W_out, L0_ffn_gate_weight,
           L0_ffn_gate_up_proj, L0_ffn_down_proj):
    maps = _host_prep(L0_attn_x, L0_attn_W_qkv, L0_attn_W_out,
                      L0_ffn_gate_weight, L0_ffn_gate_up_proj,
                      L0_ffn_down_proj)
    nc = build_program()
    res = run_bass_kernel_spmd(nc, maps, core_ids=list(range(NCORES)))
    out = np.empty((NB, S, EMB), np.float32)
    for core in range(NCORES):
        b, pp = divmod(core, 4)
        out[b, pp * T:(pp + 1) * T, :] = np.asarray(res.results[core]["out"]).T
    return out


# version 12
# speedup vs baseline: 1.0228x; 1.0052x over previous
import numpy as np
import concourse.bass as bass
import concourse.mybir as mybir
from concourse.bass_utils import run_bass_kernel_spmd
from contextlib import ExitStack

F32 = mybir.dt.float32
BF16 = mybir.dt.bfloat16
AX = mybir.AxisListType.X
AF = mybir.ActivationFunctionType
OP = mybir.AluOpType

NCORES = 8
NB, S, EMB = 2, 4096, 2048
T = 1024
H, D = 16, 128
CH = 4
HCN = 16
NEXP = 32
EPS = 1e-5

C_ID = 0
C_RT = 128
C_M0 = 256
C_M1 = 512
C_ONE = 768
C_MT0 = 769
C_MT1 = 1025
C_SEL = 1281
CBF_W = C_SEL + NEXP * 128
CF_ID = 0
CF_ONE = 128
CF_ROW = 129
CF_EPS = 257
CF_W = 258


class Sem:
    def __init__(self, st, nc, name):
        self.h = st.enter_context(nc.semaphore(name))
        self.n = 0


class Prog:
    ENGS = ("t", "v", "s", "g", "y")

    def __init__(self):
        self.q = {e: [] for e in self.ENGS}

    def op(self, eng, build, inc=None, amt=1):
        if inc is None:
            self.q[eng].append(build)
            return None
        inc.n += amt
        self.q[eng].append(lambda E, b=build, s=inc.h, a=amt: b(E).then_inc(s, a))
        return inc.n

    def wait(self, eng, sem, val):
        if val is None:
            return
        self.q[eng].append(lambda E, s=sem.h, v=val: E.wait_ge(s, v))


def build_program(dbg=()):
    nc = bass.Bass("TRN2", target_bir_lowering=False,
                   detect_race_conditions=False)
    P = Prog()

    d_x = nc.dram_tensor("x", [EMB, T], F32, kind="ExternalInput").ap()
    d_cos = nc.dram_tensor("cos", [D, T], F32, kind="ExternalInput").ap()
    d_sin = nc.dram_tensor("sin", [D, T], F32, kind="ExternalInput").ap()
    d_pm = nc.dram_tensor("pmask", [128, 8], F32, kind="ExternalInput").ap()
    d_cbf = nc.dram_tensor("cbf", [128, CBF_W], F32, kind="ExternalInput").ap()
    d_cf = nc.dram_tensor("cf", [128, CF_W], F32, kind="ExternalInput").ap()
    d_gate = nc.dram_tensor("gate", [128, 512], F32, kind="ExternalInput").ap()
    d_wkvq = nc.dram_tensor("wkvq", [48, 128, 2048], F32, kind="ExternalInput").ap()
    d_wout = nc.dram_tensor("wout", [16, 128, 2048], F32, kind="ExternalInput").ap()
    d_wgu = nc.dram_tensor("wgu", [NEXP, 6, 128, 4096], F32, kind="ExternalInput").ap()
    d_wd = nc.dram_tensor("wd", [NEXP, 128, 12288], F32, kind="ExternalInput").ap()
    d_out = nc.dram_tensor("out", [EMB, T], F32, kind="ExternalOutput").ap()
    x_src = bass.AP(d_x.tensor, 0, [[T, 128], [128 * T, HCN], [1, T]])

    dbg_out = {}
    for name, shape in (("dbg_qf", [128, H * T]), ("dbg_kf", [128, H * T]),
                        ("dbg_vt", [128, H * T]), ("dbg_attn", [128, H * T])):
        if name in dbg:
            dbg_out[name] = nc.dram_tensor(name, shape, BF16, kind="ExternalOutput").ap()
    for name, shape, dt in (("dbg_scur", [128, H * D], F32),
                            ("dbg_x2", [EMB, T], F32),
                            ("dbg_ce", [NEXP, T], BF16),
                            ("dbg_den", [128, 128], F32),
                            ("dbg_am", [128, 640], BF16),
                            ("dbg_num", [128, 256], F32)):
        if name in dbg:
            dbg_out[name] = nc.dram_tensor(name, shape, dt, kind="ExternalOutput").ap()

    d_ccin = nc.dram_tensor("cc_in", [128, 2064], F32, kind="Internal")
    d_ccout = nc.dram_tensor("cc_out", [128 * NCORES, 2064], F32,
                             kind="Internal", addr_space="Shared")

    st = ExitStack()
    with st:
        def sb(name, shape, dt):
            return st.enter_context(nc.sbuf_tensor(name, shape, dt))

        p = [st.enter_context(nc.psum_tensor(f"p{i}", [128, 512], F32))
             for i in range(6)]
        pt = [st.enter_context(nc.psum_tensor(f"pt{i}", [128, 1024], BF16))
              for i in range(2)]

        sW = Sem(st, nc, "sW")
        sW3 = [Sem(st, nc, f"sW3_{i}") for i in range(3)]
        sGU2 = [Sem(st, nc, f"sGU2_{i}") for i in range(2)]
        sWD2 = [Sem(st, nc, f"sWD2_{i}") for i in range(2)]
        sPRE2 = [Sem(st, nc, f"sPRE2_{i}") for i in range(2)]
        sGU = Sem(st, nc, "sGU")
        sWD = Sem(st, nc, "sWD")
        sPRE = Sem(st, nc, "sPRE")
        sSTG = Sem(st, nc, "sSTG")
        sCC = Sem(st, nc, "sCC")
        sF = Sem(st, nc, "sF")
        sRES = Sem(st, nc, "sRES")
        sOUT = Sem(st, nc, "sOUT")
        cP = Sem(st, nc, "cP")
        cV = Sem(st, nc, "cV")
        cA = Sem(st, nc, "cA")

        cbf = sb("cbf_s", [128, CBF_W], BF16)
        cf = sb("cf_s", [128, CF_W], F32)
        gate_bf = sb("gate_s", [128, 512], BF16)
        pmask = sb("pmask_s", [128, 8], F32)
        wbufA = sb("wbufA", [128, 3 * 2048], BF16)
        qf = sb("qf", [128, H * T], BF16)
        fdv = sb("fdv", [128, 1], F32)
        fds = sb("fds", [128, 1], F32)
        S_cur = sb("S_cur", [128, H * D], F32)
        Z_cur = sb("Z_cur", [128, H], F32)

        def pefence(out_ap_fn):
            return P.op("t", lambda E, f=out_ap_fn: E.matmul(
                f(), cbf[:, C_ONE:C_ONE + 1], cbf[:, C_ONE:C_ONE + 1],
                start=True, stop=True), cP)

        def vfence():
            return P.op("v", lambda E: E.drain(), cV)

        def sfence():
            return P.op("s", lambda E: E.drain(), cA)

        P.op("y", lambda E: E.dma_start(out=cf[:, :], in_=d_cf), sF, 16)
        P.op("y", lambda E: E.dma_start(out=pmask[:, :], in_=d_pm), sF, 16)

        def g_load(dst_ap, src_ap, sem):
            return P.op("g", lambda E, d=dst_ap, s=src_ap: E.dma_start(out=d, in_=s),
                        sem, 16)

        m_wslot = {}
        m_qfin = {}
        nW_holder = [0]

        with ExitStack() as st2:
            def sb2(name, shape, dt):
                return st2.enter_context(nc.sbuf_tensor(name, shape, dt))

            cos_bf = sb2("cos_bf", [128, T], BF16)
            sin_bf = sb2("sin_bf", [128, T], BF16)
            kf = sb2("kf", [128, H * T], BF16)
            vt = sb2("vt", [128, H * T], BF16)
            S_run = sb2("S_run", [128, H * D], F32)
            Z_run = sb2("Z_run", [128, H], F32)
            kraw = [sb2(f"kraw{i}", [128, T], BF16) for i in range(2)]
            vraw = [sb2(f"vraw{i}", [128, 512], BF16) for i in range(2)]
            r1 = [sb2(f"r1_{i}", [128, 512], BF16) for i in range(2)]
            r2 = [sb2(f"r2_{i}", [128, 512], BF16) for i in range(2)]
            r3 = [sb2(f"r3_{i}", [128, 512], BF16) for i in range(2)]
            ktm = [sb2(f"ktm{i}", [128, 256], BF16) for i in range(2)]
            am_sb = [sb2(f"am{i}", [128, 640], BF16) for i in range(2)]
            ot = [sb2(f"ot{i}", [128, 256], BF16) for i in range(2)]
            sbf = [sb2(f"sbf{i}", [128, 128], BF16) for i in range(2)]
            zbf = [sb2(f"zbf{i}", [128, 1], BF16) for i in range(2)]
            den_c = sb2("den_c", [128, 32], F32)
            jnk = sb2("jnk", [128, 256], BF16)
            ztmp = sb2("ztmp", [128, 2], F32)
            pre_buf = [sb2("pre0", [128, 2064], F32)]
            ktmp = sb2("ktmp", [128, 2048], BF16)
            dbg_den_sb = sb2("dbg_den_sb", [128, 128], F32) if "dbg_den" in dbg_out else None
            dbg_am_sb = sb2("dbg_am_sb", [128, 640], BF16) if "dbg_am" in dbg_out else None
            dbg_num_sb = sb2("dbg_num_sb", [128, 256], F32) if "dbg_num" in dbg_out else None

            g_load(cbf[:, :], d_cbf, sW)
            g_load(cos_bf[:, :], d_cos, sW)
            g_load(sin_bf[:, :], d_sin, sW)
            g_load(gate_bf[:, :], d_gate, sW)

            P.op("v", lambda E: E.memset(S_run[:, :], 0.0))
            P.op("v", lambda E: E.memset(Z_run[:, :], 0.0))
            P.op("v", lambda E: E.memset(S_cur[:, :], 0.0))
            P.op("v", lambda E: E.memset(Z_cur[:, :], 0.0))
            m_memset = vfence()

            m_proj = {}
            m_kfin = {}
            m_vtev = {}
            m_Sadd = {}
            m_kact = {}

            with ExitStack() as st3:
                x_bf = st3.enter_context(
                    nc.sbuf_tensor("x_bf", [128, HCN * T], BF16))
                g_load(x_bf[:, :], x_src, sW)
                nW = 5

                def proj_group(ptile, wslot, half):
                    m = None
                    for hc in range(HCN):
                        def bld(E, hc=hc, ptile=ptile, wslot=wslot, half=half):
                            return E.matmul(
                                p[ptile][:, :],
                                wbufA[:, wslot * 2048 + hc * 128:
                                      wslot * 2048 + hc * 128 + 128],
                                x_bf[:, hc * T + half * 512:hc * T + half * 512 + 512],
                                start=(hc == 0), stop=(hc == HCN - 1))
                        if hc == HCN - 1:
                            m = P.op("t", bld, cP)
                        else:
                            P.op("t", bld)
                    return m

                def rope_elu(h, half, dst, me_prev_needed):
                    sl = half
                    g2 = half
                    mrh = P.op("t", lambda E, g2=g2, h=h, half=half:
                               E.matmul(p[2 + g2][:, :], cbf[:, C_RT:C_RT + 128],
                                        kraw[h % 2][:, half * 512:half * 512 + 512],
                                        start=True, stop=True), cP)
                    P.wait("v", cP, mrh)
                    P.op("v", lambda E, sl=sl, h=h, half=half:
                         E.tensor_mul(r1[sl][:, :],
                                      kraw[h % 2][:, half * 512:half * 512 + 512],
                                      cos_bf[:, half * 512:half * 512 + 512]))
                    P.op("v", lambda E, sl=sl, g2=g2, half=half:
                         E.tensor_mul(r3[sl][:, :], p[2 + g2][:, :],
                                      sin_bf[:, half * 512:half * 512 + 512]))
                    m_rope = P.op("v", lambda E, sl=sl:
                                  E.tensor_add(r2[sl][:, :], r1[sl][:, :],
                                               r3[sl][:, :]), cV)
                    P.wait("s", cV, m_rope)
                    mex = P.op("s", lambda E, sl=sl:
                               E.activation(r1[sl][:, :], r2[sl][:, :], AF.Exp), cA)
                    P.op("v", lambda E, sl=sl:
                         E.tensor_scalar_max(r3[sl][:, :], r2[sl][:, :], 0.0))
                    P.wait("v", cA, mex)
                    P.op("v", lambda E, sl=sl:
                         E.tensor_scalar_min(r1[sl][:, :], r1[sl][:, :], 1.0))
                    P.op("v", lambda E, sl=sl, dst=dst:
                         E.tensor_add(dst, r1[sl][:, :], r3[sl][:, :]))
                    return vfence(), mrh

                for ob in range(16):
                    h = ob
                    if ob >= 3:
                        P.wait("g", cP, m_wslot[ob - 3])
                    g_load(wbufA[:, (ob % 3) * 2048:(ob % 3) * 2048 + 2048],
                           d_wkvq[ob], sW3[ob % 3])
                    if ob == 0:
                        P.wait("t", sW, 16 * nW)
                    P.wait("t", sW3[ob % 3], sW3[ob % 3].n)
                    proj_group(0, ob % 3, 0)
                    m1 = proj_group(1, ob % 3, 1)
                    m_proj[(ob, 1)] = m1
                    P.wait("s", cP, m1)
                    for g2 in range(2):
                        P.op("s", lambda E, g2=g2:
                             E.copy(vraw[g2][:, :], p[g2][:, :]))
                    mv = sfence()
                    P.wait("t", cA, mv)
                    if ob >= 1:
                        P.wait("t", cV, m_vtev[ob - 1])
                    for g2 in range(2):
                        for tb in range(4):
                            P.op("t", lambda E, g2=g2, tb=tb:
                                 E.transpose(pt[g2][:, tb * 128:tb * 128 + 128],
                                             vraw[g2][:, tb * 128:tb * 128 + 128],
                                             cbf[:, C_ID:C_ID + 128]))
                    mtr = pefence(lambda: p[2][0:1, 0:1])
                    P.wait("v", cP, mtr)
                    for g2 in range(2):
                        P.op("v", lambda E, g2=g2, h=h:
                             E.tensor_copy(
                                 vt[:, h * T + g2 * 512:h * T + g2 * 512 + 512],
                                 pt[g2][:, 0:512]))
                    mvt = vfence()
                    m_vtev[ob] = mvt
                    m_wslot[ob] = m1
                m_vt_all = max(m_vtev.values())

                for ob in range(16, 32):
                    h = ob - 16
                    P.wait("g", cP, m_wslot[ob - 3])
                    g_load(wbufA[:, (ob % 3) * 2048:(ob % 3) * 2048 + 2048],
                           d_wkvq[ob], sW3[ob % 3])
                    P.wait("t", sW3[ob % 3], sW3[ob % 3].n)
                    proj_group(0, ob % 3, 0)
                    m1 = proj_group(1, ob % 3, 1)
                    m_proj[(ob, 1)] = m1
                    P.wait("s", cP, m1)
                    if h >= 2:
                        P.wait("s", cV, m_kfin[h - 2])
                    for g2 in range(2):
                        P.op("s", lambda E, h=h, g2=g2:
                             E.copy(kraw[h % 2][:, g2 * 512:g2 * 512 + 512],
                                    p[g2][:, :]))
                    me = sfence()
                    P.wait("t", cA, me)
                    mrh = None
                    for g2 in range(2):
                        mrh = P.op("t", lambda E, g2=g2, h=h:
                                   E.matmul(p[2 + g2][:, :], cbf[:, C_RT:C_RT + 128],
                                            kraw[h % 2][:, g2 * 512:g2 * 512 + 512],
                                            start=True, stop=True), cP)
                    P.wait("v", cP, mrh)
                    for g2 in range(2):
                        P.op("v", lambda E, g2=g2, h=h:
                             E.tensor_mul(r1[g2][:, :],
                                          kraw[h % 2][:, g2 * 512:g2 * 512 + 512],
                                          cos_bf[:, g2 * 512:g2 * 512 + 512]))
                    for g2 in range(2):
                        P.op("v", lambda E, g2=g2:
                             E.tensor_mul(r3[g2][:, :], p[2 + g2][:, :],
                                          sin_bf[:, g2 * 512:g2 * 512 + 512]))
                    for g2 in range(2):
                        P.op("v", lambda E, g2=g2:
                             E.tensor_add(r2[g2][:, :], r1[g2][:, :], r3[g2][:, :]))
                    mrope = vfence()
                    P.wait("s", cV, mrope)
                    for g2 in range(2):
                        P.op("s", lambda E, g2=g2:
                             E.activation(r1[g2][:, :], r2[g2][:, :], AF.Exp))
                    mex = sfence()
                    for g2 in range(2):
                        P.op("v", lambda E, g2=g2:
                             E.tensor_scalar_max(r3[g2][:, :], r2[g2][:, :], 0.0))
                    P.wait("v", cA, mex)
                    for g2 in range(2):
                        P.op("v", lambda E, g2=g2:
                             E.tensor_scalar_min(r1[g2][:, :], r1[g2][:, :], 1.0))
                    for g2 in range(2):
                        P.op("v", lambda E, g2=g2, h=h:
                             E.tensor_add(kf[:, h * T + g2 * 512:h * T + g2 * 512 + 512],
                                          r1[g2][:, :], r3[g2][:, :]))
                    mk = vfence()
                    m_kfin[h] = mk
                    mzr = P.op("v", lambda E, h=h:
                               E.reduce_sum(ztmp[:, 0:1],
                                            kf[:, h * T:h * T + T], AX), cV)
                    P.wait("v", cV, mzr)
                    P.op("v", lambda E, h=h:
                         E.tensor_add(Z_run[:, h:h + 1], Z_run[:, h:h + 1],
                                      ztmp[:, 0:1]), cV)
                    P.wait("t", cV, mk)
                    for c in range(CH):
                        tile = c // 2
                        coff = (c % 2) * 256
                        for mb in range(2):
                            P.op("t", lambda E, h=h, c=c, mb=mb, tile=tile, coff=coff:
                                 E.transpose(
                                     pt[tile][:, coff + mb * 128:coff + mb * 128 + 128],
                                     kf[:, h * T + c * 256 + mb * 128:
                                        h * T + c * 256 + mb * 128 + 128],
                                     cbf[:, C_ID:C_ID + 128]))
                    mtr = pefence(lambda: p[2][0:1, 0:1])
                    P.wait("s", cP, mtr)
                    for tile in range(2):
                        P.op("s", lambda E, tile=tile:
                             E.copy(ktmp[:, tile * 512:tile * 512 + 512],
                                    pt[tile][:, 0:512]))
                    mke = sfence()
                    P.wait("t", cA, mke)
                    P.wait("t", cV, m_vt_all)
                    for group in range(2):
                        msm = None
                        for cc in range(2):
                            c = group * 2 + cc
                            for mb in range(2):
                                def bmm(E, h=h, c=c, mb=mb, cc=cc):
                                    return E.matmul(
                                        p[4 + cc][:, 0:128],
                                        ktmp[:, c * 256 + mb * 128:
                                             c * 256 + mb * 128 + 128],
                                        vt[:, h * T + (c * 2 + mb) * 128:
                                           h * T + (c * 2 + mb) * 128 + 128],
                                        start=(mb == 0), stop=(mb == 1))
                                if cc == 1 and mb == 1:
                                    msm = P.op("t", bmm, cP)
                                else:
                                    P.op("t", bmm)
                        P.wait("v", cP, msm)
                        for cc in range(2):
                            P.op("v", lambda E, h=h, cc=cc:
                                 E.tensor_add(S_run[:, h * D:h * D + 128],
                                              S_run[:, h * D:h * D + 128],
                                              p[4 + cc][:, 0:128]))
                        msa = vfence()
                        if group == 0:
                            P.wait("t", cV, msa)
                        m_Sadd[(h, group)] = msa
                    m_wslot[ob] = m1
                m_states_done = vfence()

                P.wait("g", cV, m_states_done)
                P.op("g", lambda E: E.dma_start(out=d_ccin[:, 0:2048],
                                                in_=S_run[:, :]), sSTG, 16)
                P.op("g", lambda E: E.dma_start(out=d_ccin[:, 2048:2064],
                                                in_=Z_run[:, :]), sSTG, 16)
                P.wait("g", sSTG, 32)
                P.op("g", lambda E: E.collective_compute(
                    "AllGather", OP.bypass,
                    replica_groups=[list(range(NCORES))],
                    ins=[d_ccin.ap().opt()],
                    outs=[d_ccout.ap().opt()]), sCC, 1)

                def q_pair(pi):
                    for i2 in range(2):
                        ob = 32 + 2 * pi + i2
                        h = ob - 32
                        P.wait("g", cP, m_wslot[ob - 3])
                        g_load(wbufA[:, (ob % 3) * 2048:(ob % 3) * 2048 + 2048],
                               d_wkvq[ob], sW3[ob % 3])
                        P.wait("t", sW3[ob % 3], sW3[ob % 3].n)
                        for half in range(2):
                            g2 = half
                            m = proj_group(g2, ob % 3, half)
                            m_proj[(ob, half)] = m
                            P.wait("s", cP, m)
                            if h >= 2:
                                P.wait("s", cV, m_qfin[(h - 2, half)])
                            else:
                                P.wait("s", cV, m_kfin[15])
                            P.op("s", lambda E, h=h, g2=g2, half=half:
                                 E.copy(kraw[h % 2][:, half * 512:half * 512 + 512],
                                        p[g2][:, :]))
                            me = sfence()
                            m_kact[(h, half)] = me
                            P.wait("t", cA, me)
                            mq, _ = rope_elu(
                                h, half,
                                qf[:, h * T + half * 512:h * T + half * 512 + 512],
                                None)
                            m_qfin[(h, half)] = mq
                        m_wslot[ob] = m_proj[(ob, 1)]

                q_pair(0)
                q_pair(1)
                q_pair(2)
                q_pair(3)

                P.wait("g", sCC, 1)
                m_pre = {}
                for r in range(NCORES):
                    if r >= 1:
                        P.wait("g", cV, m_pre[r - 1])
                    P.op("g", lambda E, r=r: E.dma_start(
                        out=pre_buf[0][:, :],
                        in_=d_ccout.ap()[r * 128:(r + 1) * 128, :]), sPRE2[r % 2], 16)
                    P.wait("v", sPRE2[r % 2], 16 * (r // 2 + 1))
                    if r == 0:
                        P.wait("v", sF, 32)
                    P.op("v", lambda E, r=r: E.scalar_tensor_tensor(
                        S_cur[:, :], pre_buf[0][:, 0:2048], pmask[:, r:r + 1],
                        S_cur[:, :], OP.mult, OP.add))
                    m_pre[r] = P.op("v", lambda E, r=r: E.scalar_tensor_tensor(
                        Z_cur[:, :], pre_buf[0][:, 2048:2064], pmask[:, r:r + 1],
                        Z_cur[:, :], OP.mult, OP.add), cV)
                m_prefix_done = vfence()

                if "dbg_scur" in dbg_out:
                    P.wait("y", cV, m_prefix_done)
                    P.op("y", lambda E: E.dma_start(out=dbg_out["dbg_scur"],
                                                    in_=S_cur[:, :]), sOUT, 16)

                m_den = {}
                m_N = {}
                m_ot = {}
                m_fm = {}
                m_ke = {}
                m_kv = {}
                m_SC = {}
                seq = [(c, pi) for pi in range(8) for c in range(CH)]
                pos_of = {k: n for n, k in enumerate(seq)}

                def prev(key, k):
                    n = pos_of[key] - k
                    return seq[n] if n >= 0 else None

                def get(dic, key):
                    return dic[key] if key is not None and key in dic else None

                m_ktmp = {}

                def ktm_hoist(pi):
                    hs = (2 * pi, 2 * pi + 1)
                    P.wait("t", cA, get(m_fm, (3, pi - 1) if pi else None))
                    for i2, h in enumerate(hs):
                        for c4 in range(CH):
                            for mb in range(2):
                                P.op("t", lambda E, h=h, c4=c4, mb=mb, i2=i2:
                                     E.transpose(
                                         pt[i2][:, (c4 * 2 + mb) * 128:
                                                (c4 * 2 + mb) * 128 + 128],
                                         kf[:, h * T + c4 * 256 + mb * 128:
                                            h * T + c4 * 256 + mb * 128 + 128],
                                         cbf[:, C_ID:C_ID + 128]))
                    mhtr = pefence(lambda: p[5][0:1, 502:503])
                    P.wait("s", cP, mhtr)
                    for i2 in range(2):
                        P.op("s", lambda E, i2=i2:
                             E.copy(ktmp[:, i2 * 1024:i2 * 1024 + 1024],
                                    pt[i2][:, 0:1024]))
                    m_ktmp[pi] = sfence()

                def attn_pair(c, pi):
                    key = (c, pi)
                    hs = (2 * pi, 2 * pi + 1)
                    P.wait("v", cP, get(m_N, prev(key, 1)))
                    for i2, h in enumerate(hs):
                        P.op("v", lambda E, h=h, i2=i2:
                             E.tensor_copy(sbf[i2][:, :], S_cur[:, h * D:h * D + 128]))
                        P.op("v", lambda E, h=h, i2=i2:
                             E.tensor_copy(zbf[i2][:, :], Z_cur[:, h:h + 1]))
                    mcz = vfence()
                    P.wait("t", cV, mcz)
                    P.wait("t", cV, get(m_den, prev(key, 1)))
                    mA = None
                    for i2, h in enumerate(hs):
                        for lb in range(2):
                            ptile = i2 * 2 + lb
                            P.op("t", lambda E, h=h, c=c, lb=lb, ptile=ptile:
                                 E.matmul(p[ptile][:, 0:256],
                                          qf[:, h * T + c * 256 + lb * 128:
                                             h * T + c * 256 + lb * 128 + 128],
                                          kf[:, h * T + c * 256:h * T + c * 256 + 256],
                                          start=True, stop=True))
                            mA = P.op("t", lambda E, h=h, c=c, lb=lb, ptile=ptile:
                                      E.matmul(p[ptile][:, 256:512],
                                               kf[:, h * T + c * 256 + lb * 128:
                                                  h * T + c * 256 + lb * 128 + 128],
                                               qf[:, h * T + c * 256:
                                                  h * T + c * 256 + 256],
                                               start=True, stop=True), cP)
                    P.wait("v", cP, mA)
                    P.wait("v", cP, get(m_N, prev(key, 1)))
                    for i2, h in enumerate(hs):
                        for lb in range(2):
                            ptile = i2 * 2 + lb
                            col = h * 2 + lb
                            moff = C_M0 if lb == 0 else C_M1
                            P.op("v", lambda E, ptile=ptile, moff=moff, col=col:
                                 E.scalar_tensor_tensor(
                                     jnk[:, :], p[ptile][:, 0:256], 1.0,
                                     cbf[:, moff:moff + 256], OP.bypass, OP.mult,
                                     accum_out=den_c[:, col:col + 1]))
                    m_amv = None
                    for i2, h in enumerate(hs):
                        for mb in range(2):
                            ptile = i2 * 2 + mb
                            mtoff = C_MT0 if mb == 0 else C_MT1
                            P.op("v", lambda E, ptile=ptile, i2=i2, mtoff=mtoff, mb=mb:
                                 E.tensor_mul(am_sb[i2][:, mb * 320:mb * 320 + 256],
                                              p[ptile][:, 256:512],
                                              cbf[:, mtoff:mtoff + 256]))
                    m_amv = vfence()
                    if dbg_am_sb is not None and key == (0, 0):
                        P.op("v", lambda E: E.tensor_copy(dbg_am_sb[:, :],
                                                          am_sb[0][:, :]), cV)
                    P.wait("t", cV, m_amv)
                    P.wait("t", cV, get(m_SC, prev(key, 1)))
                    P.wait("t", cA, get(m_ot, prev(key, 1)))
                    mN = None
                    for i2, h in enumerate(hs):
                        np_ = 4 + i2
                        for lb in range(2):
                            for mb in range(2):
                                P.op("t", lambda E, h=h, c=c, lb=lb, mb=mb, np_=np_,
                                     i2=i2:
                                     E.matmul(p[np_][:, lb * 128:lb * 128 + 128],
                                              am_sb[i2][:, mb * 320 + lb * 128:
                                                        mb * 320 + lb * 128 + 128],
                                              vt[:, h * T + (c * 2 + mb) * 128:
                                                 h * T + (c * 2 + mb) * 128 + 128],
                                              start=(mb == 0), stop=False))
                            P.op("t", lambda E, h=h, c=c, lb=lb, np_=np_, i2=i2:
                                 E.matmul(p[np_][:, lb * 128:lb * 128 + 128],
                                          qf[:, h * T + c * 256 + lb * 128:
                                             h * T + c * 256 + lb * 128 + 128],
                                          sbf[i2][:, :],
                                          start=False, stop=True))
                            mN = P.op("t", lambda E, h=h, c=c, lb=lb, np_=np_, i2=i2:
                                      E.matmul(p[np_][:, 300 + lb:301 + lb],
                                               qf[:, h * T + c * 256 + lb * 128:
                                                  h * T + c * 256 + lb * 128 + 128],
                                               zbf[i2][:, :], start=True, stop=True),
                                      cP)
                    m_N[key] = mN
                    P.wait("v", cP, mN)
                    for i2, h in enumerate(hs):
                        for lb in range(2):
                            col = h * 2 + lb
                            P.op("v", lambda E, col=col, np_=4 + i2, lb=lb:
                                 E.tensor_add(den_c[:, col:col + 1],
                                              den_c[:, col:col + 1],
                                              p[np_][:, 300 + lb:301 + lb]))
                    P.op("v", lambda E: E.drain())
                    cols0 = 4 * pi
                    mmx = P.op("v", lambda E, cols0=cols0:
                               E.tensor_scalar_max(den_c[:, cols0:cols0 + 4],
                                                   den_c[:, cols0:cols0 + 4], 1e-6),
                               cV)
                    P.wait("v", cV, mmx)
                    P.op("v", lambda E, cols0=cols0:
                         E.reciprocal(den_c[:, cols0:cols0 + 4],
                                      den_c[:, cols0:cols0 + 4]))
                    mden = vfence()
                    m_den[key] = mden
                    P.wait("s", cP, mN)
                    P.wait("s", cV, mden)
                    if dbg_num_sb is not None and key == (0, 0):
                        P.op("v", lambda E: E.tensor_copy(dbg_num_sb[:, 0:128],
                                                          p[4][:, 0:128]), cV)
                        P.op("v", lambda E: E.tensor_copy(dbg_num_sb[:, 128:256],
                                                          p[4][:, 128:256]), cV)
                    for i2, h in enumerate(hs):
                        for lb in range(2):
                            col = h * 2 + lb
                            P.op("s", lambda E, np_=4 + i2, col=col, i2=i2, lb=lb:
                                 E.activation(ot[i2][:, lb * 128:lb * 128 + 128],
                                              p[np_][:, lb * 128:lb * 128 + 128],
                                              AF.Copy, scale=den_c[:, col:col + 1]))
                    mot = sfence()
                    m_ot[key] = mot
                    P.wait("t", cA, mot)
                    for i2 in range(2):
                        for lb in range(2):
                            P.op("t", lambda E, i2=i2, lb=lb:
                                 E.transpose(
                                     pt[i2][:, 512 + lb * 128:512 + lb * 128 + 128],
                                     ot[i2][:, lb * 128:lb * 128 + 128],
                                     cbf[:, C_ID:C_ID + 128]))
                    mtr2 = pefence(lambda: p[4][0:1, 500:501])
                    P.wait("s", cP, mtr2)
                    for i2, h in enumerate(hs):
                        P.op("s", lambda E, h=h, c=c, i2=i2:
                             E.copy(qf[:, h * T + c * 256:h * T + c * 256 + 256],
                                    pt[i2][:, 512:768]))
                    mfm = sfence()
                    m_fm[key] = mfm
                    P.wait("t", cA, m_ktmp[pi])
                    P.wait("t", cA, mot)
                    P.wait("t", cV, mden)
                    mkv = None
                    for i2, h in enumerate(hs):
                        for mb in range(2):
                            def bmm(E, h=h, c=c, mb=mb, i2=i2):
                                return E.matmul(
                                    p[4 + i2][:, 352:480],
                                    ktmp[:, i2 * 1024 + (c * 2 + mb) * 128:
                                         i2 * 1024 + (c * 2 + mb) * 128 + 128],
                                    vt[:, h * T + (c * 2 + mb) * 128:
                                       h * T + (c * 2 + mb) * 128 + 128],
                                    start=(mb == 0), stop=(mb == 1))
                            if i2 == 1 and mb == 1:
                                mkv = P.op("t", bmm, cP)
                            else:
                                P.op("t", bmm)
                    m_kv[key] = mkv
                    for i2, h in enumerate(hs):
                        P.op("v", lambda E, h=h, c=c, i2=i2:
                             E.reduce_sum(ztmp[:, i2:i2 + 1],
                                          kf[:, h * T + c * 256:h * T + c * 256 + 256],
                                          AX))
                    P.wait("v", cP, mkv)
                    for i2, h in enumerate(hs):
                        P.op("v", lambda E, h=h, i2=i2:
                             E.tensor_add(S_cur[:, h * D:h * D + 128],
                                          S_cur[:, h * D:h * D + 128],
                                          p[4 + i2][:, 352:480]))
                    for i2, h in enumerate(hs):
                        P.op("v", lambda E, h=h, i2=i2:
                             E.tensor_add(Z_cur[:, h:h + 1], Z_cur[:, h:h + 1],
                                          ztmp[:, i2:i2 + 1]))
                    msz = vfence()
                    m_SC[key] = msz

                P.wait("t", cV, m_prefix_done)
                P.wait("t", cV, max(m_kfin.values()))
                P.wait("t", cV, m_vt_all)
                for pi in range(8):
                    P.wait("t", cV, m_qfin[(2 * pi + 1, 1)])
                    if pi >= 1 and (2 * (pi + 3) + 1, 1) in m_qfin:
                        P.wait("t", cV, m_qfin[(2 * (pi + 3) + 1, 1)])
                        P.wait("t", cA, m_kact[(2 * (pi + 3) + 1, 1)])
                    ktm_hoist(pi)
                    for c in range(CH):
                        attn_pair(c, pi)
                    if pi + 4 < 8:
                        q_pair(pi + 4)

            m_attn_done = m_fm[(CH - 1, 7)]
            m_attn_dve = vfence()
            m_attn_act = sfence()

            for nm, sbv in (("dbg_den", dbg_den_sb), ("dbg_am", dbg_am_sb),
                            ("dbg_num", dbg_num_sb)):
                if nm in dbg_out and sbv is not None:
                    P.wait("y", cV, m_attn_dve)
                    P.op("y", lambda E, nm=nm, sbv=sbv: E.dma_start(
                        out=dbg_out[nm], in_=sbv[:, :]), sOUT, 16)
            if "dbg_attn" in dbg_out:
                P.wait("y", cA, m_attn_done)
                P.op("y", lambda E: E.dma_start(out=dbg_out["dbg_attn"],
                                                in_=qf[:, :]), sOUT, 16)

        with ExitStack() as st4:
            def sb4(name, shape, dt):
                return st4.enter_context(nc.sbuf_tensor(name, shape, dt))

            resid = sb4("resid", [128, HCN * T], F32)
            wgu_buf = sb4("wgu_buf", [128, 2 * 4096], BF16)
            wd_buf = sb4("wd_buf", [128, 2 * 1536], BF16)
            sq_sb = sb4("sq_sb", [128, 512], F32)
            stat1 = sb4("stat1", [1, 1024], F32)
            stat2 = sb4("stat2", [1, 1024], F32)
            nmbB = sb4("nmbB", [128, 1024], F32)
            rbB = sb4("rbB", [128, 1024], F32)
            lg_sb = sb4("lg_sb", [32, 1024], BF16)
            g_tm = sb4("g_tm", [128, 256], F32)
            p_sb = sb4("p_sb", [128, 256], F32)
            w32 = [sb4(f"w32_{i}", [128, 32], F32) for i in range(8)]
            red = sb4("red", [128, 8], F32)
            ce_fm = sb4("ce_fm", [NEXP, T], BF16)
            ce_b = [sb4(f"ce_b{i}", [128, T], BF16) for i in range(2)]
            h_buf = [sb4(f"h_buf{i}", [128, 6 * T], BF16) for i in range(2)]
            sg_sb = [sb4(f"sg{i}", [128, 512], BF16) for i in range(2)]
            usc_sb = [sb4(f"usc{i}", [128, 512], BF16) for i in range(2)]

            EMBf = float(EMB)

            P.wait("y", cV, m_attn_dve)
            P.wait("y", cA, m_attn_act)
            for i in range(4):
                src_i = bass.AP(d_x.tensor, i * 4 * 128 * T,
                                [[T, 128], [128 * T, 4], [1, T]])
                P.op("y", lambda E, i=i, src_i=src_i: E.dma_start(
                    out=resid[:, i * 4 * T:(i + 1) * 4 * T],
                    in_=src_i), sRES, 16)

            m_resid = {}
            m_sqm = {}
            prev_sq_pe = None
            P.wait("t", cV, m_attn_dve)
            P.wait("t", cA, m_attn_act)
            for ob in range(16):
                guard = m_wslot[ob - 3] if ob >= 3 else m_wslot[45 + ob]
                P.wait("g", cP, guard)
                g_load(wbufA[:, (ob % 3) * 2048:(ob % 3) * 2048 + 2048],
                       d_wout[ob], sW3[ob % 3])
                P.wait("t", sW3[ob % 3], sW3[ob % 3].n)
                for half in range(2):
                    g2 = half
                    if ob >= 1:
                        P.wait("t", cV, m_resid[(ob - 1, half)])
                    mwo = None
                    for ec in range(HCN):
                        def bld(E, ec=ec, g2=g2, ob=ob, half=half):
                            return E.matmul(
                                p[g2][:, :],
                                wbufA[:, (ob % 3) * 2048 + ec * 128:
                                      (ob % 3) * 2048 + ec * 128 + 128],
                                qf[:, ec * T + half * 512:ec * T + half * 512 + 512],
                                start=(ec == 0), stop=(ec == HCN - 1))
                        if ec == HCN - 1:
                            mwo = P.op("t", bld, cP)
                        else:
                            P.op("t", bld)
                    m_wslot[("wo", ob)] = mwo
                    P.wait("v", cP, mwo)
                    P.wait("v", sRES, 64)
                    P.op("v", lambda E, ob=ob, half=half, g2=g2:
                         E.tensor_add(
                             resid[:, ob * T + half * 512:ob * T + half * 512 + 512],
                             resid[:, ob * T + half * 512:ob * T + half * 512 + 512],
                             p[g2][:, :]))
                    mr = vfence()
                    m_resid[(ob, half)] = mr
                    P.wait("s", cV, mr)
                    P.wait("s", cP, prev_sq_pe)
                    P.op("s", lambda E, ob=ob, half=half:
                         E.activation(sq_sb[:, :],
                                      resid[:, ob * T + half * 512:
                                            ob * T + half * 512 + 512],
                                      AF.Square))
                    msq = sfence()
                    P.wait("t", cV, mr)
                    P.op("t", lambda E, ob=ob, half=half:
                         E.matmul(p[4 + half][0:1, :], cf[:, CF_ONE:CF_ONE + 1],
                                  resid[:, ob * T + half * 512:ob * T + half * 512 + 512],
                                  start=(ob == 0), stop=(ob == 15)))
                    P.wait("t", cA, msq)
                    msqm = P.op("t", lambda E, ob=ob, half=half:
                                E.matmul(p[2 + half][0:1, :], cf[:, CF_ONE:CF_ONE + 1],
                                         sq_sb[:, :],
                                         start=(ob == 0), stop=(ob == 15)), cP)
                    m_sqm[(ob, half)] = msqm
                    prev_sq_pe = msqm
                m_wslot[ob] = m_wslot[("wo", ob)]
            m_stats = max(m_sqm.values())

            def ln_finalize_apply(m_statmm, write_x2b, sum_b=4, nm_b=0):
                P.wait("v", cP, m_statmm)
                for half in range(2):
                    P.op("v", lambda E, half=half, sum_b=sum_b:
                         E.tensor_copy(stat1[:, half * 512:half * 512 + 512],
                                       p[sum_b + half][0:1, :]))
                    P.op("v", lambda E, half=half:
                         E.tensor_copy(stat2[:, half * 512:half * 512 + 512],
                                       p[2 + half][0:1, :]))
                P.op("v", lambda E: E.tensor_scalar_mul(stat1[:, :], stat1[:, :],
                                                        float(-1.0 / EMBf)))
                P.op("v", lambda E: E.tensor_scalar_mul(stat2[:, :], stat2[:, :],
                                                        float(1.0 / EMBf)))
                mrow = P.op("v", lambda E: E.tensor_mul(nmbB[0:1, 0:1024],
                                                        stat1[:, :], stat1[:, :]),
                            cV)
                P.wait("v", cV, mrow)
                mvar = P.op("v", lambda E: E.tensor_sub(stat2[:, :], stat2[:, :],
                                                        nmbB[0:1, 0:1024]), cV)
                P.wait("v", cV, mvar)
                P.wait("s", cV, mvar)
                P.op("s", lambda E: E.activation(stat2[:, :], stat2[:, :],
                                                 AF.Sqrt,
                                                 bias=cf[0:1, CF_EPS:CF_EPS + 1]))
                msd = sfence()
                P.wait("v", cA, msd)
                P.op("v", lambda E: E.reciprocal(stat2[:, :], stat2[:, :]))
                mrstd = vfence()
                P.wait("t", cV, mrstd)
                mbc = None
                for half in range(2):
                    P.op("t", lambda E, half=half, nm_b=nm_b:
                         E.matmul(p[nm_b + half][:, :], cf[0:1, CF_ROW:CF_ROW + 128],
                                  stat1[:, half * 512:half * 512 + 512],
                                  start=True, stop=True))
                    mbc = P.op("t", lambda E, half=half:
                               E.matmul(p[2 + half][:, :], cf[0:1, CF_ROW:CF_ROW + 128],
                                        stat2[:, half * 512:half * 512 + 512],
                                        start=True, stop=True), cP)
                P.wait("v", cP, mbc)
                for half in range(2):
                    P.op("v", lambda E, half=half, nm_b=nm_b:
                         E.tensor_copy(nmbB[:, half * 512:half * 512 + 512],
                                       p[nm_b + half][:, :]))
                    P.op("v", lambda E, half=half:
                         E.tensor_copy(rbB[:, half * 512:half * 512 + 512],
                                       p[2 + half][:, :]))
                m_apply = {}
                for ec in range(HCN):
                    mxb = None
                    for half in range(2):
                        lo = ec * T + half * 512
                        P.op("v", lambda E, lo=lo, half=half:
                             E.tensor_add(resid[:, lo:lo + 512], resid[:, lo:lo + 512],
                                          nmbB[:, half * 512:half * 512 + 512]))
                        P.op("v", lambda E, lo=lo, half=half:
                             E.tensor_mul(resid[:, lo:lo + 512],
                                          resid[:, lo:lo + 512],
                                          rbB[:, half * 512:half * 512 + 512]))
                        mx2 = vfence()
                        if write_x2b:
                            P.wait("s", cV, mx2)
                            P.op("s", lambda E, lo=lo:
                                 E.copy(qf[:, lo:lo + 512],
                                        resid[:, lo:lo + 512]))
                            mxb = sfence()
                        else:
                            mxb = mx2
                    m_apply[ec] = mxb
                return m_apply

            m_x2b = ln_finalize_apply(m_stats, True)
            m_ln1_done = max(m_x2b.values())

            if "dbg_x2" in dbg_out:
                P.wait("y", cA, m_ln1_done)
                P.op("y", lambda E: E.dma_start(
                    out=bass.AP(dbg_out["dbg_x2"].tensor, 0,
                                [[T, 128], [128 * T, HCN], [1, T]]),
                    in_=resid[:, :]), sOUT, 16)

            m_gu_mm = {}
            m_wd_mm = {}
            m_hband = {}
            m_sg = {}
            m_dn_ev = {}
            m_ceb = {}

            def load_gu(idx, guard):
                e_, j_ = divmod(idx, 6)
                P.wait("g", cP, guard)
                P.op("g", lambda E, e_=e_, j_=j_, idx=idx: E.dma_start(
                    out=wgu_buf[:, (idx % 2) * 4096:(idx % 2) * 4096 + 4096],
                    in_=d_wgu[e_, j_]), sGU2[idx % 2], 16)

            def load_wd(idx, guard):
                e_, q_ = divmod(idx, 8)
                P.wait("g", cP, guard)
                P.op("g", lambda E, e_=e_, q_=q_, idx=idx: E.dma_start(
                    out=wd_buf[:, (idx % 2) * 1536:(idx % 2) * 1536 + 1536],
                    in_=d_wd[e_][:, q_ * 1536:q_ * 1536 + 1536]), sWD2[idx % 2], 16)

            P.wait("g", cV, m_attn_dve)
            P.wait("g", cA, m_attn_act)
            load_gu(0, None)
            load_gu(1, None)
            load_wd(0, None)
            load_wd(1, None)

            def emit_ce_half(e, half, extra_tV=None):
                ep = e % 2
                P.wait("t", cV, extra_tV)
                mcb = P.op("t", lambda E, e=e, half=half:
                           E.matmul(p[4 + half][:, :],
                                    cbf[0:32, C_SEL + e * 128:C_SEL + e * 128 + 128],
                                    ce_fm[0:32, half * 512:half * 512 + 512],
                                    start=True, stop=True), cP)
                P.wait("v", cP, mcb)
                P.op("v", lambda E, ep=ep, half=half:
                     E.tensor_copy(ce_b[ep][:, half * 512:half * 512 + 512],
                                   p[4 + half][:, :]))
                m_ceb[(e, half)] = vfence()

            def emit_gu_band_mms(e, j, half):
                ep = e % 2
                idx = e * 6 + j
                gt_, ut_ = 2 * half, 2 * half + 1
                P.wait("t", sGU2[idx % 2], 16 * (idx // 2 + 1))
                if e >= 2 and j == 0 and half == 0:
                    P.wait("t", cP, m_wd_mm[(e - 2, 7)])
                pkey = (e, j - 1, half) if j >= 1 else                        ((e - 1, 5, half) if e >= 1 else None)
                if pkey is not None:
                    P.wait("t", cV, m_hband[pkey])
                    P.wait("t", cA, m_sg[pkey])
                mgu = None
                for hc in range(HCN):
                    if e == 0 and j == 0:
                        P.wait("t", cA, m_x2b[hc])
                    P.op("t", lambda E, hc=hc, half=half, gt_=gt_, idx=idx:
                         E.matmul(p[gt_][:, :],
                                  wgu_buf[:, (idx % 2) * 4096 + hc * 256:
                                          (idx % 2) * 4096 + hc * 256 + 128],
                                  qf[:, hc * T + half * 512:
                                     hc * T + half * 512 + 512],
                                  start=(hc == 0), stop=(hc == HCN - 1)))
                    def bldu(E, hc=hc, half=half, ut_=ut_, idx=idx):
                        return E.matmul(
                            p[ut_][:, :],
                            wgu_buf[:, (idx % 2) * 4096 + hc * 256 + 128:
                                    (idx % 2) * 4096 + hc * 256 + 256],
                            qf[:, hc * T + half * 512:hc * T + half * 512 + 512],
                            start=(hc == 0), stop=(hc == HCN - 1))
                    if hc == HCN - 1:
                        mgu = P.op("t", bldu, cP)
                    else:
                        P.op("t", bldu)
                m_gu_mm[(e, j)] = mgu
                if half == 1 and e * 6 + j + 2 < NEXP * 6:
                    load_gu(e * 6 + j + 2, mgu)
                return mgu

            def emit_gu_evacs(e, j, half, mgu):
                ep = e % 2
                gt_, ut_ = 2 * half, 2 * half + 1
                P.wait("s", cP, mgu)
                P.op("s", lambda E, half=half, gt_=gt_:
                     E.activation(sg_sb[half][:, :], p[gt_][:, :], AF.Silu))
                msg = sfence()
                m_sg[(e, j, half)] = msg
                P.wait("v", cP, mgu)
                P.wait("v", cV, m_ceb[(e, half)])
                P.op("v", lambda E, half=half, ut_=ut_, ep=ep:
                     E.tensor_mul(usc_sb[half][:, :], p[ut_][:, :],
                                  ce_b[ep][:, half * 512:half * 512 + 512]))
                P.wait("v", cA, msg)
                P.op("v", lambda E, half=half, ep=ep, j=j:
                     E.tensor_mul(
                         h_buf[ep][:, j * T + half * 512:j * T + half * 512 + 512],
                         sg_sb[half][:, :], usc_sb[half][:, :]))
                m_hband[(e, j, half)] = vfence()

            def emit_gu_band(e, j, half):
                emit_gu_evacs(e, j, half, emit_gu_band_mms(e, j, half))

            mlg = None
            for half in range(2):
                for hc in range(HCN):
                    if half == 0:
                        P.wait("t", cA, m_x2b[hc])
                    def bld(E, hc=hc, half=half):
                        return E.matmul(
                            p[4 + half][0:32, :], gate_bf[:, hc * 32:hc * 32 + 32],
                            qf[:, hc * T + half * 512:hc * T + half * 512 + 512],
                            start=(hc == 0), stop=(hc == HCN - 1))
                    if hc == HCN - 1:
                        mlg = P.op("t", bld, cP)
                    else:
                        P.op("t", bld)
            P.wait("v", cP, mlg)
            P.op("v", lambda E: E.tensor_copy(lg_sb[:, 0:512], p[4][0:32, :]))
            P.op("v", lambda E: E.tensor_copy(lg_sb[:, 512:1024], p[5][0:32, :]))
            mlge = vfence()
            P.wait("t", cV, mlge)
            for tb in range(8):
                P.op("t", lambda E, tb=tb:
                     E.transpose(pt[0][0:128, tb * 32:tb * 32 + 32],
                                 lg_sb[0:32, tb * 128:tb * 128 + 128],
                                 cbf[0:32, C_ID:C_ID + 32]))
            mgt = pefence(lambda: p[1][0:1, 0:1])
            P.wait("v", cP, mgt)
            mgtm = P.op("v", lambda E: E.tensor_copy(g_tm[:, :], pt[0][:, 0:256]), cV)
            P.wait("v", cV, mgtm)

            m_cetr = {}
            mcv_prev = None
            for pair in range(4):
                tiles = (2 * pair, 2 * pair + 1)
                mm_ = [w32[0], w32[4]]
                wv_ = [w32[1], w32[5]]
                cnt_ = [w32[2], w32[6]]
                Gg_ = [w32[3], w32[7]]
                rr_ = [0, 4]

                def zip2(fn, inc=None):
                    out = None
                    for i_ in range(2):
                        b_ = fn(i_, tiles[i_])
                        if i_ == 1 and inc is not None:
                            out = P.op("v", b_, inc)
                        else:
                            P.op("v", b_)
                    return out

                zip2(lambda i, tb: (lambda E, i=i, tb=tb:
                     E.reduce_max(red[:, rr_[i]:rr_[i] + 1],
                                  g_tm[:, tb * 32:tb * 32 + 32], AX,
                                  negate=True)))
                mneg = vfence()
                P.wait("s", cV, mneg)
                for i_ in range(2):
                    P.op("s", lambda E, i=i_, tb=tiles[i_]:
                         E.activation(mm_[i][:, :],
                                      g_tm[:, tb * 32:tb * 32 + 32],
                                      AF.Exp, bias=red[:, rr_[i]:rr_[i] + 1]))
                mexp = sfence()
                P.wait("v", cA, mexp)
                zip2(lambda i, tb: (lambda E, i=i:
                     E.reduce_sum(red[:, rr_[i] + 1:rr_[i] + 2], mm_[i][:, :], AX)))
                zip2(lambda i, tb: (lambda E, i=i:
                     E.reciprocal(red[:, rr_[i] + 1:rr_[i] + 2],
                                  red[:, rr_[i] + 1:rr_[i] + 2])))
                zip2(lambda i, tb: (lambda E, i=i, tb=tb:
                     E.tensor_single_scalar(p_sb[:, tb * 32:tb * 32 + 32],
                                            mm_[i][:, :],
                                            red[:, rr_[i] + 1:rr_[i] + 2],
                                            OP.mult)))
                for e32 in range(NEXP):
                    zip2(lambda i, tb: (lambda E, i=i, tb=tb, e32=e32:
                         E.tensor_scalar(
                             mm_[i][:, :], p_sb[:, tb * 32:tb * 32 + 32],
                             p_sb[:, tb * 32 + e32:tb * 32 + e32 + 1], 0.0,
                             OP.is_ge, OP.add,
                             accum_out=cnt_[i][:, e32:e32 + 1])))
                    zip2(lambda i, tb: (lambda E, i=i, tb=tb, e32=e32:
                         E.scalar_tensor_tensor(
                             wv_[i][:, :], p_sb[:, tb * 32:tb * 32 + 32],
                             p_sb[:, tb * 32 + e32:tb * 32 + e32 + 1],
                             p_sb[:, tb * 32:tb * 32 + 32],
                             OP.is_ge, OP.mult,
                             accum_out=Gg_[i][:, e32:e32 + 1])))
                zip2(lambda i, tb: (lambda E, i=i:
                     E.tensor_single_scalar(mm_[i][:, :], cnt_[i][:, :], 8.0,
                                            OP.is_le)))
                zip2(lambda i, tb: (lambda E, i=i, tb=tb:
                     E.tensor_mul(wv_[i][:, :], mm_[i][:, :],
                                  p_sb[:, tb * 32:tb * 32 + 32])))
                zip2(lambda i, tb: (lambda E, i=i:
                     E.reduce_sum(red[:, rr_[i] + 2:rr_[i] + 3], wv_[i][:, :], AX)))
                zip2(lambda i, tb: (lambda E, i=i:
                     E.tensor_scalar_mul(red[:, rr_[i] + 2:rr_[i] + 3],
                                         red[:, rr_[i] + 2:rr_[i] + 3], 0.8)))
                zip2(lambda i, tb: (lambda E, i=i, tb=tb:
                     E.tensor_sub(Gg_[i][:, :], Gg_[i][:, :],
                                  p_sb[:, tb * 32:tb * 32 + 32])))
                zip2(lambda i, tb: (lambda E, i=i:
                     E.tensor_single_scalar(Gg_[i][:, :], Gg_[i][:, :],
                                            red[:, rr_[i] + 2:rr_[i] + 3],
                                            OP.is_lt)))
                zip2(lambda i, tb: (lambda E, i=i:
                     E.tensor_mul(mm_[i][:, :], mm_[i][:, :], Gg_[i][:, :])))
                zip2(lambda i, tb: (lambda E, i=i, tb=tb:
                     E.tensor_mul(wv_[i][:, :], mm_[i][:, :],
                                  p_sb[:, tb * 32:tb * 32 + 32])))
                zip2(lambda i, tb: (lambda E, i=i:
                     E.reduce_sum(red[:, rr_[i] + 3:rr_[i] + 4], wv_[i][:, :], AX)))
                zip2(lambda i, tb: (lambda E, i=i:
                     E.tensor_scalar_max(red[:, rr_[i] + 3:rr_[i] + 4],
                                         red[:, rr_[i] + 3:rr_[i] + 4], 1e-12)))
                zip2(lambda i, tb: (lambda E, i=i:
                     E.reciprocal(red[:, rr_[i] + 3:rr_[i] + 4],
                                  red[:, rr_[i] + 3:rr_[i] + 4])))
                zip2(lambda i, tb: (lambda E, i=i:
                     E.tensor_single_scalar(wv_[i][:, :], wv_[i][:, :],
                                            red[:, rr_[i] + 3:rr_[i] + 4],
                                            OP.mult)))
                if pair == 0:
                    mgu00 = emit_gu_band_mms(0, 0, 0)
                elif pair == 1:
                    mgu01 = emit_gu_band_mms(0, 0, 1)
                elif pair == 2:
                    emit_gu_evacs(0, 0, 0, mgu00)
                    mgu10 = emit_gu_band_mms(0, 1, 0)
                elif pair == 3:
                    mgu11 = None
                mce = vfence()
                P.wait("t", cV, mce)
                for i_ in range(2):
                    tb = tiles[i_]
                    P.wait("t", cV, mcv_prev)
                    P.op("t", lambda E, i=i_:
                         E.transpose(p[4][0:32, 0:128], wv_[i][:, :],
                                     cf[:, CF_ID:CF_ID + 128]))
                    mct = pefence(lambda: p[5][0:1, 400:401])
                    P.wait("v", cP, mct)
                    P.op("v", lambda E, tb=tb:
                         E.tensor_copy(ce_fm[0:32, tb * 128:tb * 128 + 128],
                                       p[4][0:32, 0:128]))
                    mcv = vfence()
                    m_cetr[tb] = mcv
                    mcv_prev = mcv
                if pair == 1:
                    emit_ce_half(0, 0, extra_tV=m_cetr[3])
                    mcv_prev = m_ceb[(0, 0)]
                elif pair == 3:
                    emit_ce_half(0, 1, extra_tV=m_cetr[7])
                    emit_gu_evacs(0, 0, 1, mgu01)
                    emit_gu_evacs(0, 1, 0, mgu10)
            m_ce_done = m_cetr[7]

            if "dbg_ce" in dbg_out:
                P.wait("y", cV, m_ce_done)
                P.op("y", lambda E: E.dma_start(out=dbg_out["dbg_ce"],
                                                in_=ce_fm[:, :]), sOUT, 16)

            for e in range(NEXP):
                ep = e % 2
                if e >= 1:
                    for half in range(2):
                        emit_ce_half(e, half,
                                     extra_tV=m_dn_ev[(e - 1, 15, 1)]
                                     if half == 0 else None)
                for j in range(6):
                    for half in range(2):
                        if e == 0 and (j == 0 or (j == 1 and half == 0)):
                            continue
                        emit_gu_band(e, j, half)
                m_h_done = max(m_hband[(e, j, hh)] for j in range(6)
                               for hh in range(2))

                for oc in range(16):
                    qq = oc // 2
                    idx = e * 8 + qq
                    for half in range(2):
                        dt_ = 4 + half
                        P.wait("t", sWD2[idx % 2], 16 * (idx // 2 + 1))
                        P.wait("t", cV, m_h_done)
                        pkey = (e, oc - 1, half) if oc >= 1 else \
                               ((e - 1, 15, half) if e >= 1 else None)
                        if pkey is not None and pkey in m_dn_ev:
                            P.wait("t", cV, m_dn_ev[pkey])
                        woff = (idx % 2) * 1536 + (oc % 2) * 768
                        mdn = None
                        for ic in range(6):
                            def bld(E, ic=ic, woff=woff, half=half, dt_=dt_, ep=ep):
                                return E.matmul(
                                    p[dt_][:, :],
                                    wd_buf[:, woff + ic * 128:woff + ic * 128 + 128],
                                    h_buf[ep][:, ic * T + half * 512:
                                              ic * T + half * 512 + 512],
                                    start=(ic == 0), stop=(ic == 5))
                            if ic == 5:
                                mdn = P.op("t", bld, cP)
                            else:
                                P.op("t", bld)
                        if half == 1 and oc % 2 == 1:
                            m_wd_mm[(e, qq)] = mdn
                            if e * 8 + qq + 2 < NEXP * 8:
                                load_wd(e * 8 + qq + 2, mdn)
                        P.wait("v", cP, mdn)
                        if e == NEXP - 1:
                            P.op("v", lambda E, oc=oc, half=half, dt_=dt_:
                                 E.tensor_add(
                                     resid[:, oc * T + half * 512:
                                           oc * T + half * 512 + 512],
                                     resid[:, oc * T + half * 512:
                                           oc * T + half * 512 + 512],
                                     p[dt_][:, :]))
                            mde = vfence()
                        else:
                            mde = P.op("v", lambda E, oc=oc, half=half, dt_=dt_:
                                       E.tensor_add(
                                           resid[:, oc * T + half * 512:
                                                 oc * T + half * 512 + 512],
                                           resid[:, oc * T + half * 512:
                                                 oc * T + half * 512 + 512],
                                           p[dt_][:, :]), cV)
                        m_dn_ev[(e, oc, half)] = mde

            m_dn_all = m_dn_ev[(NEXP - 1, 15, 1)]
            m_sg_all = m_sg[(NEXP - 1, 5, 1)]

            prev_sq_pe = None
            m_sqm2 = {}
            P.wait("t", cV, max(m_hband[(NEXP - 1, 5, hh)] for hh in range(2)))
            P.wait("t", cA, m_sg_all)
            for ec in range(HCN):
                for half in range(2):
                    mlast = m_dn_ev[(NEXP - 1, ec, half)]
                    P.wait("s", cV, mlast)
                    P.wait("s", cP, prev_sq_pe)
                    P.op("s", lambda E, ec=ec, half=half:
                         E.activation(sq_sb[:, :],
                                      resid[:, ec * T + half * 512:
                                            ec * T + half * 512 + 512],
                                      AF.Square))
                    msqa = sfence()
                    P.wait("t", cV, mlast)
                    P.op("t", lambda E, ec=ec, half=half:
                         E.matmul(p[half][0:1, :], cf[:, CF_ONE:CF_ONE + 1],
                                  resid[:, ec * T + half * 512:ec * T + half * 512 + 512],
                                  start=(ec == 0), stop=(ec == 15)))
                    P.wait("t", cA, msqa)
                    msq2 = P.op("t", lambda E, half=half, ec=ec:
                                E.matmul(p[2 + half][0:1, :], cf[:, CF_ONE:CF_ONE + 1],
                                         sq_sb[:, :],
                                         start=(ec == 0), stop=(ec == 15)), cP)
                    m_sqm2[(ec, half)] = msq2
                    prev_sq_pe = msq2
            m_stats2 = max(m_sqm2.values())
            m_out = ln_finalize_apply(m_stats2, False, sum_b=0, nm_b=4)
            for ec in range(HCN):
                P.wait("y", cV, m_out[ec])
                P.op("y", lambda E, ec=ec: E.dma_start(
                    out=d_out[ec * 128:(ec + 1) * 128, :],
                    in_=resid[:, ec * T:(ec + 1) * T]), sOUT, 16)
            P.wait("y", sOUT, sOUT.n)

        with nc.Block() as block:
            @block.tensor
            def _(t):
                for f in P.q["t"]:
                    f(t)

            @block.vector
            def _(v):
                for f in P.q["v"]:
                    f(v)

            @block.scalar
            def _(s_):
                for f in P.q["s"]:
                    f(s_)

            @block.gpsimd
            def _(g):
                for f in P.q["g"]:
                    f(g)

            @block.sync
            def _(y):
                for f in P.q["y"]:
                    f(y)

    return nc



def _tile_T(blk):
    K = blk.shape[1]
    n = K // 128
    return np.ascontiguousarray(
        blk.T.reshape(n, 128, 128).transpose(1, 0, 2).reshape(128, n * 128))


def _host_prep(x, wqkv, wo, gw, wgu, wd):
    x = np.asarray(x, np.float32)
    wqkv = np.asarray(wqkv, np.float32)
    wo = np.asarray(wo, np.float32)
    gw = np.asarray(gw, np.float32)
    wgu = np.asarray(wgu, np.float32)
    wd = np.asarray(wd, np.float32)

    tiles = []
    for h in range(16):
        tiles.append(_tile_T(wqkv[4096 + h * 128:4096 + (h + 1) * 128, :]))
    for h in range(16):
        tiles.append(_tile_T(wqkv[2048 + h * 128:2048 + (h + 1) * 128, :]))
    for h in range(16):
        tiles.append(_tile_T(wqkv[h * 128:(h + 1) * 128, :]))
    wkvq_t = np.stack(tiles)
    wout_t = np.stack([_tile_T(wo[ob * 128:(ob + 1) * 128, :])
                       for ob in range(16)])
    gate_t = np.ascontiguousarray(
        gw.T.reshape(16, 128, 32).transpose(1, 0, 2).reshape(128, 512))
    wgu_t = np.empty((NEXP, 6, 128, 4096), np.float32)
    for e in range(NEXP):
        for j in range(6):
            tg = _tile_T(wgu[e, j * 128:(j + 1) * 128, :]).reshape(128, 16, 128)
            tu = _tile_T(wgu[e, 768 + j * 128:768 + (j + 1) * 128, :]
                         ).reshape(128, 16, 128)
            wgu_t[e, j] = np.concatenate([tg, tu], axis=2).reshape(128, 4096)
    wd_t = np.empty((NEXP, 128, 12288), np.float32)
    for e in range(NEXP):
        cols = [_tile_T(wd[e, oc * 128:(oc + 1) * 128, :]) for oc in range(16)]
        wd_t[e] = np.concatenate(cols, axis=1)

    cbf = np.zeros((128, CBF_W), np.float32)
    cbf[:, C_ID:C_ID + 128] = np.eye(128)
    RT = np.zeros((128, 128), np.float32)
    for i in range(64):
        RT[2 * i + 1, 2 * i] = -1.0
        RT[2 * i, 2 * i + 1] = 1.0
    cbf[:, C_RT:C_RT + 128] = RT
    l_idx = np.arange(128)[:, None]
    m_idx = np.arange(256)[None, :]
    cbf[:, C_M0:C_M0 + 256] = (m_idx <= l_idx).astype(np.float32)
    cbf[:, C_M1:C_M1 + 256] = (m_idx <= l_idx + 128).astype(np.float32)
    cbf[:, C_ONE] = 1.0
    m_idx2 = np.arange(256)[None, :]
    p_idx = np.arange(128)[:, None]
    cbf[:, C_MT0:C_MT0 + 256] = (m_idx2 >= p_idx).astype(np.float32)
    cbf[:, C_MT1:C_MT1 + 256] = (m_idx2 >= p_idx + 128).astype(np.float32)
    for e in range(NEXP):
        cbf[e, C_SEL + e * 128:C_SEL + (e + 1) * 128] = 1.0
    cfc = np.zeros((128, CF_W), np.float32)
    cfc[:, CF_ID:CF_ID + 128] = np.eye(128)
    cfc[:, CF_ONE] = 1.0
    cfc[0, CF_ROW:CF_ROW + 128] = 1.0
    cfc[0, CF_EPS] = EPS

    inv_freq = 1.0 / (10000.0 ** (np.arange(0, 128, 2, dtype=np.float32) / 128))
    pos = np.arange(S, dtype=np.float32)
    freqs = np.outer(pos, inv_freq)
    cos_full = np.cos(freqs).astype(np.float32)
    sin_full = np.sin(freqs).astype(np.float32)

    maps = []
    for core in range(NCORES):
        b, pp = divmod(core, 4)
        t0 = pp * T
        xs = np.ascontiguousarray(x[b, t0:t0 + T, :].T)
        cos_i = np.ascontiguousarray(
            np.concatenate([cos_full[t0:t0 + T], cos_full[t0:t0 + T]], axis=1).T)
        sin_i = np.ascontiguousarray(
            np.concatenate([sin_full[t0:t0 + T], sin_full[t0:t0 + T]], axis=1).T)
        pm = np.zeros((128, 8), np.float32)
        pm[:, b * 4:b * 4 + pp] = 1.0
        maps.append(dict(x=xs, cos=cos_i, sin=sin_i, pmask=pm, cbf=cbf, cf=cfc,
                         gate=gate_t, wkvq=wkvq_t, wout=wout_t, wgu=wgu_t,
                         wd=wd_t))
    return maps


def kernel(L0_attn_x, L0_attn_W_qkv, L0_attn_W_out, L0_ffn_gate_weight,
           L0_ffn_gate_up_proj, L0_ffn_down_proj):
    maps = _host_prep(L0_attn_x, L0_attn_W_qkv, L0_attn_W_out,
                      L0_ffn_gate_weight, L0_ffn_gate_up_proj,
                      L0_ffn_down_proj)
    nc = build_program()
    res = run_bass_kernel_spmd(nc, maps, core_ids=list(range(NCORES)))
    out = np.empty((NB, S, EMB), np.float32)
    for core in range(NCORES):
        b, pp = divmod(core, 4)
        out[b, pp * T:(pp + 1) * T, :] = np.asarray(res.results[core]["out"]).T
    return out


# version 15
# speedup vs baseline: 1.0357x; 1.0126x over previous
import numpy as np
import concourse.bass as bass
import concourse.mybir as mybir
from concourse.bass_utils import run_bass_kernel_spmd
from contextlib import ExitStack

F32 = mybir.dt.float32
BF16 = mybir.dt.bfloat16
AX = mybir.AxisListType.X
AF = mybir.ActivationFunctionType
OP = mybir.AluOpType

NCORES = 8
NB, S, EMB = 2, 4096, 2048
T = 1024
H, D = 16, 128
CH = 4
HCN = 16
NEXP = 32
EPS = 1e-5

C_ID = 0
C_RT = 128
C_M0 = 256
C_M1 = 512
C_ONE = 768
C_MT0 = 769
C_MT1 = 1025
C_SEL = 1281
CBF_W = C_SEL + NEXP * 128
CF_ID = 0
CF_ONE = 128
CF_ROW = 129
CF_EPS = 257
CF_W = 258


class Sem:
    def __init__(self, st, nc, name):
        self.h = st.enter_context(nc.semaphore(name))
        self.n = 0


class Prog:
    ENGS = ("t", "v", "s", "g", "y")

    def __init__(self):
        self.q = {e: [] for e in self.ENGS}

    def op(self, eng, build, inc=None, amt=1):
        if inc is None:
            self.q[eng].append(build)
            return None
        inc.n += amt
        self.q[eng].append(lambda E, b=build, s=inc.h, a=amt: b(E).then_inc(s, a))
        return inc.n

    def wait(self, eng, sem, val):
        if val is None:
            return
        self.q[eng].append(lambda E, s=sem.h, v=val: E.wait_ge(s, v))


def build_program(dbg=()):
    nc = bass.Bass("TRN2", target_bir_lowering=False,
                   detect_race_conditions=False)
    P = Prog()

    d_x = nc.dram_tensor("x", [EMB, T], F32, kind="ExternalInput").ap()
    d_cos = nc.dram_tensor("cos", [D, T], F32, kind="ExternalInput").ap()
    d_sin = nc.dram_tensor("sin", [D, T], F32, kind="ExternalInput").ap()
    d_pm = nc.dram_tensor("pmask", [128, 8], F32, kind="ExternalInput").ap()
    d_cbf = nc.dram_tensor("cbf", [128, CBF_W], F32, kind="ExternalInput").ap()
    d_cf = nc.dram_tensor("cf", [128, CF_W], F32, kind="ExternalInput").ap()
    d_gate = nc.dram_tensor("gate", [128, 512], F32, kind="ExternalInput").ap()
    d_wkvq = nc.dram_tensor("wkvq", [48, 128, 2048], F32, kind="ExternalInput").ap()
    d_wout = nc.dram_tensor("wout", [16, 128, 2048], F32, kind="ExternalInput").ap()
    d_wgu = nc.dram_tensor("wgu", [NEXP, 6, 128, 4096], F32, kind="ExternalInput").ap()
    d_wd = nc.dram_tensor("wd", [NEXP, 128, 12288], F32, kind="ExternalInput").ap()
    d_out = nc.dram_tensor("out", [EMB, T], F32, kind="ExternalOutput").ap()
    x_src = bass.AP(d_x.tensor, 0, [[T, 128], [128 * T, HCN], [1, T]])

    dbg_out = {}
    for name, shape in (("dbg_qf", [128, H * T]), ("dbg_kf", [128, H * T]),
                        ("dbg_vt", [128, H * T]), ("dbg_attn", [128, H * T])):
        if name in dbg:
            dbg_out[name] = nc.dram_tensor(name, shape, BF16, kind="ExternalOutput").ap()
    for name, shape, dt in (("dbg_scur", [128, H * D], F32),
                            ("dbg_x2", [EMB, T], F32),
                            ("dbg_ce", [NEXP, T], BF16),
                            ("dbg_den", [128, 128], F32),
                            ("dbg_am", [128, 640], BF16),
                            ("dbg_num", [128, 256], F32)):
        if name in dbg:
            dbg_out[name] = nc.dram_tensor(name, shape, dt, kind="ExternalOutput").ap()

    d_ccin = nc.dram_tensor("cc_in", [128, 2064], F32, kind="Internal")
    d_ccout = nc.dram_tensor("cc_out", [128 * NCORES, 2064], F32,
                             kind="Internal", addr_space="Shared")

    st = ExitStack()
    with st:
        def sb(name, shape, dt):
            return st.enter_context(nc.sbuf_tensor(name, shape, dt))

        p = [st.enter_context(nc.psum_tensor(f"p{i}", [128, 512], F32))
             for i in range(6)]
        pt = [st.enter_context(nc.psum_tensor(f"pt{i}", [128, 1024], BF16))
              for i in range(2)]

        sW = Sem(st, nc, "sW")
        sW3 = [Sem(st, nc, f"sW3_{i}") for i in range(3)]
        sGU2 = [Sem(st, nc, f"sGU2_{i}") for i in range(2)]
        sWD2 = [Sem(st, nc, f"sWD2_{i}") for i in range(2)]
        sPRE2 = [Sem(st, nc, f"sPRE2_{i}") for i in range(2)]
        sGU = Sem(st, nc, "sGU")
        sWD = Sem(st, nc, "sWD")
        sPRE = Sem(st, nc, "sPRE")
        sSTG = Sem(st, nc, "sSTG")
        sCC = Sem(st, nc, "sCC")
        sF = Sem(st, nc, "sF")
        sRES = Sem(st, nc, "sRES")
        sOUT = Sem(st, nc, "sOUT")
        cP = Sem(st, nc, "cP")
        cV = Sem(st, nc, "cV")
        cA = Sem(st, nc, "cA")

        cbf = sb("cbf_s", [128, CBF_W], BF16)
        cf = sb("cf_s", [128, CF_W], F32)
        gate_bf = sb("gate_s", [128, 512], BF16)
        pmask = sb("pmask_s", [128, 8], F32)
        wbufA = sb("wbufA", [128, 3 * 2048], BF16)
        qf = sb("qf", [128, H * T], BF16)
        fdv = sb("fdv", [128, 1], F32)
        fds = sb("fds", [128, 1], F32)
        S_cur = sb("S_cur", [128, H * D], F32)
        Z_cur = sb("Z_cur", [128, H], F32)

        def pefence(out_ap_fn):
            return P.op("t", lambda E, f=out_ap_fn: E.matmul(
                f(), cbf[:, C_ONE:C_ONE + 1], cbf[:, C_ONE:C_ONE + 1],
                start=True, stop=True), cP)

        def vfence():
            return P.op("v", lambda E: E.drain(), cV)

        def sfence():
            return P.op("s", lambda E: E.drain(), cA)

        P.op("y", lambda E: E.dma_start(out=cf[:, :], in_=d_cf), sF, 16)
        P.op("y", lambda E: E.dma_start(out=pmask[:, :], in_=d_pm), sF, 16)

        def g_load(dst_ap, src_ap, sem):
            return P.op("g", lambda E, d=dst_ap, s=src_ap: E.dma_start(out=d, in_=s),
                        sem, 16)

        m_wslot = {}
        m_qfin = {}
        nW_holder = [0]

        with ExitStack() as st2:
            def sb2(name, shape, dt):
                return st2.enter_context(nc.sbuf_tensor(name, shape, dt))

            cos_bf = sb2("cos_bf", [128, T], BF16)
            sin_bf = sb2("sin_bf", [128, T], BF16)
            kf = sb2("kf", [128, H * T], BF16)
            vt = sb2("vt", [128, H * T], BF16)
            S_run = sb2("S_run", [128, H * D], F32)
            Z_run = sb2("Z_run", [128, H], F32)
            kraw = [sb2(f"kraw{i}", [128, T], BF16) for i in range(2)]
            vraw = [sb2(f"vraw{i}", [128, 512], BF16) for i in range(2)]
            r1 = [sb2(f"r1_{i}", [128, 512], BF16) for i in range(2)]
            r2 = [sb2(f"r2_{i}", [128, 512], BF16) for i in range(2)]
            r3 = [sb2(f"r3_{i}", [128, 512], BF16) for i in range(2)]
            ktm = [sb2(f"ktm{i}", [128, 256], BF16) for i in range(2)]
            am_sb = [sb2(f"am{i}", [128, 640], BF16) for i in range(2)]
            ot = [sb2(f"ot{i}", [128, 256], BF16) for i in range(2)]
            sbf = [sb2(f"sbf{i}", [128, 128], BF16) for i in range(2)]
            zbf = [sb2(f"zbf{i}", [128, 1], BF16) for i in range(2)]
            den_c = sb2("den_c", [128, 32], F32)
            jnk = sb2("jnk", [128, 256], BF16)
            ztmp = sb2("ztmp", [128, 2], F32)
            pre_buf = [sb2("pre0", [128, 2064], F32)]
            ktmp = sb2("ktmp", [128, 2048], BF16)
            dbg_den_sb = sb2("dbg_den_sb", [128, 128], F32) if "dbg_den" in dbg_out else None
            dbg_am_sb = sb2("dbg_am_sb", [128, 640], BF16) if "dbg_am" in dbg_out else None
            dbg_num_sb = sb2("dbg_num_sb", [128, 256], F32) if "dbg_num" in dbg_out else None

            g_load(cbf[:, :], d_cbf, sW)
            g_load(cos_bf[:, :], d_cos, sW)
            g_load(sin_bf[:, :], d_sin, sW)
            g_load(gate_bf[:, :], d_gate, sW)

            P.op("v", lambda E: E.memset(S_run[:, :], 0.0))
            P.op("v", lambda E: E.memset(Z_run[:, :], 0.0))
            P.op("v", lambda E: E.memset(S_cur[:, :], 0.0))
            P.op("v", lambda E: E.memset(Z_cur[:, :], 0.0))
            m_memset = vfence()

            m_proj = {}
            m_kfin = {}
            m_vtev = {}
            m_Sadd = {}
            m_kact = {}

            with ExitStack() as st3:
                x_bf = st3.enter_context(
                    nc.sbuf_tensor("x_bf", [128, HCN * T], BF16))
                g_load(x_bf[:, :], x_src, sW)
                nW = 5

                def proj_group(ptile, wslot, half):
                    m = None
                    for hc in range(HCN):
                        def bld(E, hc=hc, ptile=ptile, wslot=wslot, half=half):
                            return E.matmul(
                                p[ptile][:, :],
                                wbufA[:, wslot * 2048 + hc * 128:
                                      wslot * 2048 + hc * 128 + 128],
                                x_bf[:, hc * T + half * 512:hc * T + half * 512 + 512],
                                start=(hc == 0), stop=(hc == HCN - 1))
                        if hc == HCN - 1:
                            m = P.op("t", bld, cP)
                        else:
                            P.op("t", bld)
                    return m

                def rope_elu(h, half, dst, me_prev_needed):
                    sl = half
                    g2 = half
                    mrh = P.op("t", lambda E, g2=g2, h=h, half=half:
                               E.matmul(p[2 + g2][:, :], cbf[:, C_RT:C_RT + 128],
                                        kraw[h % 2][:, half * 512:half * 512 + 512],
                                        start=True, stop=True), cP)
                    P.wait("v", cP, mrh)
                    P.op("v", lambda E, sl=sl, h=h, half=half:
                         E.tensor_mul(r1[sl][:, :],
                                      kraw[h % 2][:, half * 512:half * 512 + 512],
                                      cos_bf[:, half * 512:half * 512 + 512]))
                    P.op("v", lambda E, sl=sl, g2=g2, half=half:
                         E.tensor_mul(r3[sl][:, :], p[2 + g2][:, :],
                                      sin_bf[:, half * 512:half * 512 + 512]))
                    m_rope = P.op("v", lambda E, sl=sl:
                                  E.tensor_add(r2[sl][:, :], r1[sl][:, :],
                                               r3[sl][:, :]), cV)
                    P.wait("s", cV, m_rope)
                    mex = P.op("s", lambda E, sl=sl:
                               E.activation(r1[sl][:, :], r2[sl][:, :], AF.Exp), cA)
                    P.op("v", lambda E, sl=sl:
                         E.tensor_scalar_max(r3[sl][:, :], r2[sl][:, :], 0.0))
                    P.wait("v", cA, mex)
                    P.op("v", lambda E, sl=sl:
                         E.tensor_scalar_min(r1[sl][:, :], r1[sl][:, :], 1.0))
                    P.op("v", lambda E, sl=sl, dst=dst:
                         E.tensor_add(dst, r1[sl][:, :], r3[sl][:, :]))
                    return vfence(), mrh

                for ob in range(16):
                    h = ob
                    if ob >= 3:
                        P.wait("g", cP, m_wslot[ob - 3])
                    g_load(wbufA[:, (ob % 3) * 2048:(ob % 3) * 2048 + 2048],
                           d_wkvq[ob], sW3[ob % 3])
                    if ob == 0:
                        P.wait("t", sW, 16 * nW)
                    P.wait("t", sW3[ob % 3], sW3[ob % 3].n)
                    proj_group(0, ob % 3, 0)
                    m1 = proj_group(1, ob % 3, 1)
                    m_proj[(ob, 1)] = m1
                    P.wait("s", cP, m1)
                    for g2 in range(2):
                        P.op("s", lambda E, g2=g2:
                             E.copy(vraw[g2][:, :], p[g2][:, :]))
                    mv = sfence()
                    P.wait("t", cA, mv)
                    if ob >= 1:
                        P.wait("t", cV, m_vtev[ob - 1])
                    for g2 in range(2):
                        for tb in range(4):
                            P.op("t", lambda E, g2=g2, tb=tb:
                                 E.transpose(pt[g2][:, tb * 128:tb * 128 + 128],
                                             vraw[g2][:, tb * 128:tb * 128 + 128],
                                             cbf[:, C_ID:C_ID + 128]))
                    mtr = pefence(lambda: p[2][0:1, 0:1])
                    P.wait("v", cP, mtr)
                    for g2 in range(2):
                        P.op("v", lambda E, g2=g2, h=h:
                             E.tensor_copy(
                                 vt[:, h * T + g2 * 512:h * T + g2 * 512 + 512],
                                 pt[g2][:, 0:512]))
                    mvt = vfence()
                    m_vtev[ob] = mvt
                    m_wslot[ob] = m1
                m_vt_all = max(m_vtev.values())

                for ob in range(16, 32):
                    h = ob - 16
                    P.wait("g", cP, m_wslot[ob - 3])
                    g_load(wbufA[:, (ob % 3) * 2048:(ob % 3) * 2048 + 2048],
                           d_wkvq[ob], sW3[ob % 3])
                    P.wait("t", sW3[ob % 3], sW3[ob % 3].n)
                    proj_group(0, ob % 3, 0)
                    m1 = proj_group(1, ob % 3, 1)
                    m_proj[(ob, 1)] = m1
                    P.wait("s", cP, m1)
                    if h >= 2:
                        P.wait("s", cV, m_kfin[h - 2])
                    for g2 in range(2):
                        P.op("s", lambda E, h=h, g2=g2:
                             E.copy(kraw[h % 2][:, g2 * 512:g2 * 512 + 512],
                                    p[g2][:, :]))
                    me = sfence()
                    P.wait("t", cA, me)
                    mrh = None
                    for g2 in range(2):
                        mrh = P.op("t", lambda E, g2=g2, h=h:
                                   E.matmul(p[2 + g2][:, :], cbf[:, C_RT:C_RT + 128],
                                            kraw[h % 2][:, g2 * 512:g2 * 512 + 512],
                                            start=True, stop=True), cP)
                    P.wait("v", cP, mrh)
                    for g2 in range(2):
                        P.op("v", lambda E, g2=g2, h=h:
                             E.tensor_mul(r1[g2][:, :],
                                          kraw[h % 2][:, g2 * 512:g2 * 512 + 512],
                                          cos_bf[:, g2 * 512:g2 * 512 + 512]))
                    for g2 in range(2):
                        P.op("v", lambda E, g2=g2:
                             E.tensor_mul(r3[g2][:, :], p[2 + g2][:, :],
                                          sin_bf[:, g2 * 512:g2 * 512 + 512]))
                    for g2 in range(2):
                        P.op("v", lambda E, g2=g2:
                             E.tensor_add(r2[g2][:, :], r1[g2][:, :], r3[g2][:, :]))
                    mrope = vfence()
                    P.wait("s", cV, mrope)
                    for g2 in range(2):
                        P.op("s", lambda E, g2=g2:
                             E.activation(r1[g2][:, :], r2[g2][:, :], AF.Exp))
                    mex = sfence()
                    for g2 in range(2):
                        P.op("v", lambda E, g2=g2:
                             E.tensor_scalar_max(r3[g2][:, :], r2[g2][:, :], 0.0))
                    P.wait("v", cA, mex)
                    for g2 in range(2):
                        P.op("v", lambda E, g2=g2:
                             E.tensor_scalar_min(r1[g2][:, :], r1[g2][:, :], 1.0))
                    for g2 in range(2):
                        P.op("v", lambda E, g2=g2, h=h:
                             E.tensor_add(kf[:, h * T + g2 * 512:h * T + g2 * 512 + 512],
                                          r1[g2][:, :], r3[g2][:, :]))
                    mk = vfence()
                    m_kfin[h] = mk
                    mzr = P.op("v", lambda E, h=h:
                               E.reduce_sum(ztmp[:, 0:1],
                                            kf[:, h * T:h * T + T], AX), cV)
                    P.wait("v", cV, mzr)
                    P.op("v", lambda E, h=h:
                         E.tensor_add(Z_run[:, h:h + 1], Z_run[:, h:h + 1],
                                      ztmp[:, 0:1]), cV)
                    P.wait("t", cV, mk)
                    for c in range(CH):
                        tile = c // 2
                        coff = (c % 2) * 256
                        for mb in range(2):
                            P.op("t", lambda E, h=h, c=c, mb=mb, tile=tile, coff=coff:
                                 E.transpose(
                                     pt[tile][:, coff + mb * 128:coff + mb * 128 + 128],
                                     kf[:, h * T + c * 256 + mb * 128:
                                        h * T + c * 256 + mb * 128 + 128],
                                     cbf[:, C_ID:C_ID + 128]))
                    mtr = pefence(lambda: p[2][0:1, 0:1])
                    P.wait("s", cP, mtr)
                    for tile in range(2):
                        P.op("s", lambda E, tile=tile:
                             E.copy(ktmp[:, tile * 512:tile * 512 + 512],
                                    pt[tile][:, 0:512]))
                    mke = sfence()
                    P.wait("t", cA, mke)
                    P.wait("t", cV, m_vt_all)
                    for group in range(2):
                        msm = None
                        for cc in range(2):
                            c = group * 2 + cc
                            for mb in range(2):
                                def bmm(E, h=h, c=c, mb=mb, cc=cc):
                                    return E.matmul(
                                        p[4 + cc][:, 0:128],
                                        ktmp[:, c * 256 + mb * 128:
                                             c * 256 + mb * 128 + 128],
                                        vt[:, h * T + (c * 2 + mb) * 128:
                                           h * T + (c * 2 + mb) * 128 + 128],
                                        start=(mb == 0), stop=(mb == 1))
                                if cc == 1 and mb == 1:
                                    msm = P.op("t", bmm, cP)
                                else:
                                    P.op("t", bmm)
                        P.wait("v", cP, msm)
                        for cc in range(2):
                            P.op("v", lambda E, h=h, cc=cc:
                                 E.tensor_add(S_run[:, h * D:h * D + 128],
                                              S_run[:, h * D:h * D + 128],
                                              p[4 + cc][:, 0:128]))
                        msa = vfence()
                        if group == 0:
                            P.wait("t", cV, msa)
                        m_Sadd[(h, group)] = msa
                    m_wslot[ob] = m1

                m_states_done = vfence()
                P.wait("g", cV, m_states_done)
                P.op("g", lambda E: E.dma_start(out=d_ccin[:, 0:2048],
                                                in_=S_run[:, :]), sSTG, 16)
                P.op("g", lambda E: E.dma_start(out=d_ccin[:, 2048:2064],
                                                in_=Z_run[:, :]), sSTG, 16)
                P.wait("g", sSTG, 32)
                P.op("g", lambda E: E.collective_compute(
                    "AllGather", OP.bypass,
                    replica_groups=[list(range(NCORES))],
                    ins=[d_ccin.ap().opt()],
                    outs=[d_ccout.ap().opt()]), sCC, 1)

                def q_pair(pi):
                    for i2 in range(2):
                        ob = 32 + 2 * pi + i2
                        h = ob - 32
                        P.wait("g", cP, m_wslot[ob - 3])
                        g_load(wbufA[:, (ob % 3) * 2048:(ob % 3) * 2048 + 2048],
                               d_wkvq[ob], sW3[ob % 3])
                        P.wait("t", sW3[ob % 3], sW3[ob % 3].n)
                        for half in range(2):
                            g2 = half
                            m = proj_group(g2, ob % 3, half)
                            m_proj[(ob, half)] = m
                            P.wait("s", cP, m)
                            if h >= 2:
                                P.wait("s", cV, m_qfin[(h - 2, half)])
                            else:
                                P.wait("s", cV, m_kfin[15])
                            P.op("s", lambda E, h=h, g2=g2, half=half:
                                 E.copy(kraw[h % 2][:, half * 512:half * 512 + 512],
                                        p[g2][:, :]))
                            me = sfence()
                            m_kact[(h, half)] = me
                            P.wait("t", cA, me)
                            mq, _ = rope_elu(
                                h, half,
                                qf[:, h * T + half * 512:h * T + half * 512 + 512],
                                None)
                            m_qfin[(h, half)] = mq
                        m_wslot[ob] = m_proj[(ob, 1)]

                q_pair(0)
                q_pair(1)
                q_pair(2)
                q_pair(3)

                P.wait("g", sCC, 1)
                m_pre = {}
                for r in range(NCORES):
                    if r >= 1:
                        P.wait("g", cV, m_pre[r - 1])
                    P.op("g", lambda E, r=r: E.dma_start(
                        out=pre_buf[0][:, :],
                        in_=d_ccout.ap()[r * 128:(r + 1) * 128, :]), sPRE2[r % 2], 16)
                    P.wait("v", sPRE2[r % 2], 16 * (r // 2 + 1))
                    if r == 0:
                        P.wait("v", sF, 32)
                    P.op("v", lambda E, r=r: E.scalar_tensor_tensor(
                        S_cur[:, :], pre_buf[0][:, 0:2048], pmask[:, r:r + 1],
                        S_cur[:, :], OP.mult, OP.add))
                    m_pre[r] = P.op("v", lambda E, r=r: E.scalar_tensor_tensor(
                        Z_cur[:, :], pre_buf[0][:, 2048:2064], pmask[:, r:r + 1],
                        Z_cur[:, :], OP.mult, OP.add), cV)
                m_prefix_done = vfence()

                if "dbg_scur" in dbg_out:
                    P.wait("y", cV, m_prefix_done)
                    P.op("y", lambda E: E.dma_start(out=dbg_out["dbg_scur"],
                                                    in_=S_cur[:, :]), sOUT, 16)

                m_den = {}
                m_N = {}
                m_ot = {}
                m_fm = {}
                m_ke = {}
                m_kv = {}
                m_SC = {}
                seq = [(c, pi) for pi in range(8) for c in range(CH)]
                pos_of = {k: n for n, k in enumerate(seq)}

                def prev(key, k):
                    n = pos_of[key] - k
                    return seq[n] if n >= 0 else None

                def get(dic, key):
                    return dic[key] if key is not None and key in dic else None

                m_ktmp = {}

                def ktm_hoist(pi):
                    hs = (2 * pi, 2 * pi + 1)
                    P.wait("t", cA, get(m_fm, (3, pi - 1) if pi else None))
                    for i2, h in enumerate(hs):
                        for c4 in range(CH):
                            for mb in range(2):
                                P.op("t", lambda E, h=h, c4=c4, mb=mb, i2=i2:
                                     E.transpose(
                                         pt[i2][:, (c4 * 2 + mb) * 128:
                                                (c4 * 2 + mb) * 128 + 128],
                                         kf[:, h * T + c4 * 256 + mb * 128:
                                            h * T + c4 * 256 + mb * 128 + 128],
                                         cbf[:, C_ID:C_ID + 128]))
                    mhtr = pefence(lambda: p[5][0:1, 502:503])
                    P.wait("s", cP, mhtr)
                    for i2 in range(2):
                        P.op("s", lambda E, i2=i2:
                             E.copy(ktmp[:, i2 * 1024:i2 * 1024 + 1024],
                                    pt[i2][:, 0:1024]))
                    m_ktmp[pi] = sfence()

                def attn_pair(c, pi):
                    key = (c, pi)
                    hs = (2 * pi, 2 * pi + 1)
                    P.wait("v", cP, get(m_N, prev(key, 1)))
                    for i2, h in enumerate(hs):
                        P.op("v", lambda E, h=h, i2=i2:
                             E.tensor_copy(sbf[i2][:, :], S_cur[:, h * D:h * D + 128]))
                        P.op("v", lambda E, h=h, i2=i2:
                             E.tensor_copy(zbf[i2][:, :], Z_cur[:, h:h + 1]))
                    mcz = vfence()
                    P.wait("t", cV, mcz)
                    P.wait("t", cV, get(m_den, prev(key, 1)))
                    mA = None
                    for i2, h in enumerate(hs):
                        for lb in range(2):
                            ptile = i2 * 2 + lb
                            P.op("t", lambda E, h=h, c=c, lb=lb, ptile=ptile:
                                 E.matmul(p[ptile][:, 0:256],
                                          qf[:, h * T + c * 256 + lb * 128:
                                             h * T + c * 256 + lb * 128 + 128],
                                          kf[:, h * T + c * 256:h * T + c * 256 + 256],
                                          start=True, stop=True))
                            mA = P.op("t", lambda E, h=h, c=c, lb=lb, ptile=ptile:
                                      E.matmul(p[ptile][:, 256:512],
                                               kf[:, h * T + c * 256 + lb * 128:
                                                  h * T + c * 256 + lb * 128 + 128],
                                               qf[:, h * T + c * 256:
                                                  h * T + c * 256 + 256],
                                               start=True, stop=True), cP)
                    P.wait("v", cP, mA)
                    P.wait("v", cP, get(m_N, prev(key, 1)))
                    for i2, h in enumerate(hs):
                        for lb in range(2):
                            ptile = i2 * 2 + lb
                            col = h * 2 + lb
                            moff = C_M0 if lb == 0 else C_M1
                            P.op("v", lambda E, ptile=ptile, moff=moff, col=col:
                                 E.scalar_tensor_tensor(
                                     jnk[:, :], p[ptile][:, 0:256], 1.0,
                                     cbf[:, moff:moff + 256], OP.bypass, OP.mult,
                                     accum_out=den_c[:, col:col + 1]))
                    m_amv = None
                    for i2, h in enumerate(hs):
                        for mb in range(2):
                            ptile = i2 * 2 + mb
                            mtoff = C_MT0 if mb == 0 else C_MT1
                            P.op("v", lambda E, ptile=ptile, i2=i2, mtoff=mtoff, mb=mb:
                                 E.tensor_mul(am_sb[i2][:, mb * 320:mb * 320 + 256],
                                              p[ptile][:, 256:512],
                                              cbf[:, mtoff:mtoff + 256]))
                    m_amv = vfence()
                    if dbg_am_sb is not None and key == (0, 0):
                        P.op("v", lambda E: E.tensor_copy(dbg_am_sb[:, :],
                                                          am_sb[0][:, :]), cV)
                    P.wait("t", cV, m_amv)
                    P.wait("t", cV, get(m_SC, prev(key, 1)))
                    P.wait("t", cA, get(m_ot, prev(key, 1)))
                    mN = None
                    for i2, h in enumerate(hs):
                        np_ = 4 + i2
                        for lb in range(2):
                            for mb in range(2):
                                P.op("t", lambda E, h=h, c=c, lb=lb, mb=mb, np_=np_,
                                     i2=i2:
                                     E.matmul(p[np_][:, lb * 128:lb * 128 + 128],
                                              am_sb[i2][:, mb * 320 + lb * 128:
                                                        mb * 320 + lb * 128 + 128],
                                              vt[:, h * T + (c * 2 + mb) * 128:
                                                 h * T + (c * 2 + mb) * 128 + 128],
                                              start=(mb == 0), stop=False))
                            P.op("t", lambda E, h=h, c=c, lb=lb, np_=np_, i2=i2:
                                 E.matmul(p[np_][:, lb * 128:lb * 128 + 128],
                                          qf[:, h * T + c * 256 + lb * 128:
                                             h * T + c * 256 + lb * 128 + 128],
                                          sbf[i2][:, :],
                                          start=False, stop=True))
                            mN = P.op("t", lambda E, h=h, c=c, lb=lb, np_=np_, i2=i2:
                                      E.matmul(p[np_][:, 300 + lb:301 + lb],
                                               qf[:, h * T + c * 256 + lb * 128:
                                                  h * T + c * 256 + lb * 128 + 128],
                                               zbf[i2][:, :], start=True, stop=True),
                                      cP)
                    m_N[key] = mN
                    P.wait("v", cP, mN)
                    for i2, h in enumerate(hs):
                        for lb in range(2):
                            col = h * 2 + lb
                            P.op("v", lambda E, col=col, np_=4 + i2, lb=lb:
                                 E.tensor_add(den_c[:, col:col + 1],
                                              den_c[:, col:col + 1],
                                              p[np_][:, 300 + lb:301 + lb]))
                    P.op("v", lambda E: E.drain())
                    cols0 = 4 * pi
                    mmx = P.op("v", lambda E, cols0=cols0:
                               E.tensor_scalar_max(den_c[:, cols0:cols0 + 4],
                                                   den_c[:, cols0:cols0 + 4], 1e-6),
                               cV)
                    P.wait("v", cV, mmx)
                    P.op("v", lambda E, cols0=cols0:
                         E.reciprocal(den_c[:, cols0:cols0 + 4],
                                      den_c[:, cols0:cols0 + 4]))
                    mden = vfence()
                    m_den[key] = mden
                    P.wait("s", cP, mN)
                    P.wait("s", cV, mden)
                    if dbg_num_sb is not None and key == (0, 0):
                        P.op("v", lambda E: E.tensor_copy(dbg_num_sb[:, 0:128],
                                                          p[4][:, 0:128]), cV)
                        P.op("v", lambda E: E.tensor_copy(dbg_num_sb[:, 128:256],
                                                          p[4][:, 128:256]), cV)
                    for i2, h in enumerate(hs):
                        for lb in range(2):
                            col = h * 2 + lb
                            P.op("s", lambda E, np_=4 + i2, col=col, i2=i2, lb=lb:
                                 E.activation(ot[i2][:, lb * 128:lb * 128 + 128],
                                              p[np_][:, lb * 128:lb * 128 + 128],
                                              AF.Copy, scale=den_c[:, col:col + 1]))
                    mot = sfence()
                    m_ot[key] = mot
                    P.wait("t", cA, mot)
                    for i2 in range(2):
                        for lb in range(2):
                            P.op("t", lambda E, i2=i2, lb=lb:
                                 E.transpose(
                                     pt[i2][:, 512 + lb * 128:512 + lb * 128 + 128],
                                     ot[i2][:, lb * 128:lb * 128 + 128],
                                     cbf[:, C_ID:C_ID + 128]))
                    mtr2 = pefence(lambda: p[4][0:1, 500:501])
                    P.wait("s", cP, mtr2)
                    for i2, h in enumerate(hs):
                        P.op("s", lambda E, h=h, c=c, i2=i2:
                             E.copy(qf[:, h * T + c * 256:h * T + c * 256 + 256],
                                    pt[i2][:, 512:768]))
                    mfm = sfence()
                    m_fm[key] = mfm
                    P.wait("t", cA, m_ktmp[pi])
                    P.wait("t", cA, mot)
                    P.wait("t", cV, mden)
                    mkv = None
                    for i2, h in enumerate(hs):
                        for mb in range(2):
                            def bmm(E, h=h, c=c, mb=mb, i2=i2):
                                return E.matmul(
                                    p[4 + i2][:, 352:480],
                                    ktmp[:, i2 * 1024 + (c * 2 + mb) * 128:
                                         i2 * 1024 + (c * 2 + mb) * 128 + 128],
                                    vt[:, h * T + (c * 2 + mb) * 128:
                                       h * T + (c * 2 + mb) * 128 + 128],
                                    start=(mb == 0), stop=(mb == 1))
                            if i2 == 1 and mb == 1:
                                mkv = P.op("t", bmm, cP)
                            else:
                                P.op("t", bmm)
                    m_kv[key] = mkv
                    for i2, h in enumerate(hs):
                        P.op("v", lambda E, h=h, c=c, i2=i2:
                             E.reduce_sum(ztmp[:, i2:i2 + 1],
                                          kf[:, h * T + c * 256:h * T + c * 256 + 256],
                                          AX))
                    P.wait("v", cP, mkv)
                    for i2, h in enumerate(hs):
                        P.op("v", lambda E, h=h, i2=i2:
                             E.tensor_add(S_cur[:, h * D:h * D + 128],
                                          S_cur[:, h * D:h * D + 128],
                                          p[4 + i2][:, 352:480]))
                    for i2, h in enumerate(hs):
                        P.op("v", lambda E, h=h, i2=i2:
                             E.tensor_add(Z_cur[:, h:h + 1], Z_cur[:, h:h + 1],
                                          ztmp[:, i2:i2 + 1]))
                    msz = vfence()
                    m_SC[key] = msz

                P.wait("t", cV, m_prefix_done)
                P.wait("t", cV, max(m_kfin.values()))
                P.wait("t", cV, m_vt_all)
                for pi in range(8):
                    P.wait("t", cV, m_qfin[(2 * pi + 1, 1)])
                    if pi >= 1 and (2 * (pi + 3) + 1, 1) in m_qfin:
                        P.wait("t", cV, m_qfin[(2 * (pi + 3) + 1, 1)])
                        P.wait("t", cA, m_kact[(2 * (pi + 3) + 1, 1)])
                    ktm_hoist(pi)
                    for c in range(CH):
                        attn_pair(c, pi)
                    if pi + 4 < 8:
                        q_pair(pi + 4)

            m_attn_done = m_fm[(CH - 1, 7)]
            m_attn_dve = vfence()
            m_attn_act = sfence()

            for nm, sbv in (("dbg_den", dbg_den_sb), ("dbg_am", dbg_am_sb),
                            ("dbg_num", dbg_num_sb)):
                if nm in dbg_out and sbv is not None:
                    P.wait("y", cV, m_attn_dve)
                    P.op("y", lambda E, nm=nm, sbv=sbv: E.dma_start(
                        out=dbg_out[nm], in_=sbv[:, :]), sOUT, 16)
            if "dbg_attn" in dbg_out:
                P.wait("y", cA, m_attn_done)
                P.op("y", lambda E: E.dma_start(out=dbg_out["dbg_attn"],
                                                in_=qf[:, :]), sOUT, 16)

        with ExitStack() as st4:
            def sb4(name, shape, dt):
                return st4.enter_context(nc.sbuf_tensor(name, shape, dt))

            resid = sb4("resid", [128, HCN * T], F32)
            wgu_buf = sb4("wgu_buf", [128, 2 * 4096], BF16)
            wd_buf = sb4("wd_buf", [128, 2 * 1536], BF16)
            sq_sb = sb4("sq_sb", [128, 512], F32)
            stat1 = sb4("stat1", [1, 1024], F32)
            stat2 = sb4("stat2", [1, 1024], F32)
            nmbB = sb4("nmbB", [128, 1024], F32)
            rbB = sb4("rbB", [128, 1024], F32)
            lg_sb = sb4("lg_sb", [32, 1024], BF16)
            g_tm = sb4("g_tm", [128, 256], F32)
            p_sb = sb4("p_sb", [128, 256], F32)
            w32 = [sb4(f"w32_{i}", [128, 32], F32) for i in range(8)]
            red = sb4("red", [128, 8], F32)
            ce_fm = sb4("ce_fm", [NEXP, T], BF16)
            ce_b = [sb4(f"ce_b{i}", [128, T], BF16) for i in range(2)]
            h_buf = [sb4(f"h_buf{i}", [128, 6 * T], BF16) for i in range(2)]
            sg_sb = [sb4(f"sg{i}", [128, 512], BF16) for i in range(2)]
            usc_sb = [sb4(f"usc{i}", [128, 512], BF16) for i in range(2)]

            EMBf = float(EMB)

            P.wait("y", cV, m_attn_dve)
            P.wait("y", cA, m_attn_act)
            for i in range(4):
                src_i = bass.AP(d_x.tensor, i * 4 * 128 * T,
                                [[T, 128], [128 * T, 4], [1, T]])
                P.op("y", lambda E, i=i, src_i=src_i: E.dma_start(
                    out=resid[:, i * 4 * T:(i + 1) * 4 * T],
                    in_=src_i), sRES, 16)

            def bootstrap(step):
                if step == 0:
                    P.wait("g", cV, m_attn_dve)
                    P.wait("g", cA, m_attn_act)
                _i = step // 2
                if step % 2 == 0:
                    P.op("g", lambda E, _i=_i: E.dma_start(
                        out=wgu_buf[:, _i * 4096:_i * 4096 + 4096],
                        in_=d_wgu[0, _i]), sGU2[_i], 16)
                else:
                    P.op("g", lambda E, _i=_i: E.dma_start(
                        out=wd_buf[:, _i * 1536:_i * 1536 + 1536],
                        in_=d_wd[0][:, _i * 1536:_i * 1536 + 1536]), sWD2[_i], 16)

            m_resid = {}
            m_sqm = {}
            prev_sq_pe = None
            P.wait("t", cV, m_attn_dve)
            P.wait("t", cA, m_attn_act)
            for ob in range(16):
                if ob in (3, 6, 9, 12):
                    bootstrap(ob // 3 - 1)
                guard = m_wslot[ob - 3] if ob >= 3 else m_wslot[45 + ob]
                P.wait("g", cP, guard)
                g_load(wbufA[:, (ob % 3) * 2048:(ob % 3) * 2048 + 2048],
                       d_wout[ob], sW3[ob % 3])
                P.wait("t", sW3[ob % 3], sW3[ob % 3].n)
                for half in range(2):
                    g2 = half
                    if ob >= 1:
                        P.wait("t", cV, m_resid[(ob - 1, half)])
                    mwo = None
                    for ec in range(HCN):
                        def bld(E, ec=ec, g2=g2, ob=ob, half=half):
                            return E.matmul(
                                p[g2][:, :],
                                wbufA[:, (ob % 3) * 2048 + ec * 128:
                                      (ob % 3) * 2048 + ec * 128 + 128],
                                qf[:, ec * T + half * 512:ec * T + half * 512 + 512],
                                start=(ec == 0), stop=(ec == HCN - 1))
                        if ec == HCN - 1:
                            mwo = P.op("t", bld, cP)
                        else:
                            P.op("t", bld)
                    m_wslot[("wo", ob)] = mwo
                    P.wait("v", cP, mwo)
                    P.wait("v", sRES, 64)
                    P.op("v", lambda E, ob=ob, half=half, g2=g2:
                         E.tensor_add(
                             resid[:, ob * T + half * 512:ob * T + half * 512 + 512],
                             resid[:, ob * T + half * 512:ob * T + half * 512 + 512],
                             p[g2][:, :]))
                    mr = vfence()
                    m_resid[(ob, half)] = mr
                    P.wait("s", cV, mr)
                    P.wait("s", cP, prev_sq_pe)
                    P.op("s", lambda E, ob=ob, half=half:
                         E.activation(sq_sb[:, :],
                                      resid[:, ob * T + half * 512:
                                            ob * T + half * 512 + 512],
                                      AF.Square))
                    msq = sfence()
                    P.wait("t", cV, mr)
                    P.op("t", lambda E, ob=ob, half=half:
                         E.matmul(p[4 + half][0:1, :], cf[:, CF_ONE:CF_ONE + 1],
                                  resid[:, ob * T + half * 512:ob * T + half * 512 + 512],
                                  start=(ob == 0), stop=(ob == 15)))
                    P.wait("t", cA, msq)
                    msqm = P.op("t", lambda E, ob=ob, half=half:
                                E.matmul(p[2 + half][0:1, :], cf[:, CF_ONE:CF_ONE + 1],
                                         sq_sb[:, :],
                                         start=(ob == 0), stop=(ob == 15)), cP)
                    m_sqm[(ob, half)] = msqm
                    prev_sq_pe = msqm
                m_wslot[ob] = m_wslot[("wo", ob)]
            m_stats = max(m_sqm.values())

            def ln_finalize_apply(m_statmm, write_x2b, sum_b=4, nm_b=0):
                P.wait("v", cP, m_statmm)
                for half in range(2):
                    P.op("v", lambda E, half=half, sum_b=sum_b:
                         E.tensor_copy(stat1[:, half * 512:half * 512 + 512],
                                       p[sum_b + half][0:1, :]))
                    P.op("v", lambda E, half=half:
                         E.tensor_copy(stat2[:, half * 512:half * 512 + 512],
                                       p[2 + half][0:1, :]))
                P.op("v", lambda E: E.tensor_scalar_mul(stat1[:, :], stat1[:, :],
                                                        float(-1.0 / EMBf)))
                P.op("v", lambda E: E.tensor_scalar_mul(stat2[:, :], stat2[:, :],
                                                        float(1.0 / EMBf)))
                mrow = P.op("v", lambda E: E.tensor_mul(nmbB[0:1, 0:1024],
                                                        stat1[:, :], stat1[:, :]),
                            cV)
                P.wait("v", cV, mrow)
                mvar = P.op("v", lambda E: E.tensor_sub(stat2[:, :], stat2[:, :],
                                                        nmbB[0:1, 0:1024]), cV)
                P.wait("v", cV, mvar)
                P.wait("s", cV, mvar)
                P.op("s", lambda E: E.activation(stat2[:, :], stat2[:, :],
                                                 AF.Sqrt,
                                                 bias=cf[0:1, CF_EPS:CF_EPS + 1]))
                msd = sfence()
                P.wait("v", cA, msd)
                P.op("v", lambda E: E.reciprocal(stat2[:, :], stat2[:, :]))
                mrstd = vfence()
                P.wait("t", cV, mrstd)
                mbc = None
                for half in range(2):
                    P.op("t", lambda E, half=half, nm_b=nm_b:
                         E.matmul(p[nm_b + half][:, :], cf[0:1, CF_ROW:CF_ROW + 128],
                                  stat1[:, half * 512:half * 512 + 512],
                                  start=True, stop=True))
                    mbc = P.op("t", lambda E, half=half:
                               E.matmul(p[2 + half][:, :], cf[0:1, CF_ROW:CF_ROW + 128],
                                        stat2[:, half * 512:half * 512 + 512],
                                        start=True, stop=True), cP)
                P.wait("v", cP, mbc)
                for half in range(2):
                    P.op("v", lambda E, half=half, nm_b=nm_b:
                         E.tensor_copy(nmbB[:, half * 512:half * 512 + 512],
                                       p[nm_b + half][:, :]))
                    P.op("v", lambda E, half=half:
                         E.tensor_copy(rbB[:, half * 512:half * 512 + 512],
                                       p[2 + half][:, :]))
                m_apply = {}
                for ec in range(HCN):
                    mxb = None
                    for half in range(2):
                        lo = ec * T + half * 512
                        P.op("v", lambda E, lo=lo, half=half:
                             E.tensor_add(resid[:, lo:lo + 512], resid[:, lo:lo + 512],
                                          nmbB[:, half * 512:half * 512 + 512]))
                        P.op("v", lambda E, lo=lo, half=half:
                             E.tensor_mul(resid[:, lo:lo + 512],
                                          resid[:, lo:lo + 512],
                                          rbB[:, half * 512:half * 512 + 512]))
                        mx2 = vfence()
                        if write_x2b:
                            P.wait("s", cV, mx2)
                            P.op("s", lambda E, lo=lo:
                                 E.copy(qf[:, lo:lo + 512],
                                        resid[:, lo:lo + 512]))
                            mxb = sfence()
                        else:
                            mxb = mx2
                    m_apply[ec] = mxb
                return m_apply

            m_x2b = ln_finalize_apply(m_stats, True)
            m_ln1_done = max(m_x2b.values())

            if "dbg_x2" in dbg_out:
                P.wait("y", cA, m_ln1_done)
                P.op("y", lambda E: E.dma_start(
                    out=bass.AP(dbg_out["dbg_x2"].tensor, 0,
                                [[T, 128], [128 * T, HCN], [1, T]]),
                    in_=resid[:, :]), sOUT, 16)

            m_gu_mm = {}
            m_wd_mm = {}
            m_hband = {}
            m_sg = {}
            m_dn_ev = {}
            m_ceb = {}

            def load_gu(idx, guard):
                e_, j_ = divmod(idx, 6)
                P.wait("g", cP, guard)
                P.op("g", lambda E, e_=e_, j_=j_, idx=idx: E.dma_start(
                    out=wgu_buf[:, (idx % 2) * 4096:(idx % 2) * 4096 + 4096],
                    in_=d_wgu[e_, j_]), sGU2[idx % 2], 16)

            def load_wd(idx, guard):
                e_, q_ = divmod(idx, 8)
                P.wait("g", cP, guard)
                P.op("g", lambda E, e_=e_, q_=q_, idx=idx: E.dma_start(
                    out=wd_buf[:, (idx % 2) * 1536:(idx % 2) * 1536 + 1536],
                    in_=d_wd[e_][:, q_ * 1536:q_ * 1536 + 1536]), sWD2[idx % 2], 16)

            def emit_ce_half(e, half, extra_tV=None):
                ep = e % 2
                P.wait("t", cV, extra_tV)
                mcb = P.op("t", lambda E, e=e, half=half:
                           E.matmul(p[4 + half][:, :],
                                    cbf[0:32, C_SEL + e * 128:C_SEL + e * 128 + 128],
                                    ce_fm[0:32, half * 512:half * 512 + 512],
                                    start=True, stop=True), cP)
                P.wait("v", cP, mcb)
                P.op("v", lambda E, ep=ep, half=half:
                     E.tensor_copy(ce_b[ep][:, half * 512:half * 512 + 512],
                                   p[4 + half][:, :]))
                m_ceb[(e, half)] = vfence()

            def emit_gu_band_mms(e, j, half):
                ep = e % 2
                idx = e * 6 + j
                gt_, ut_ = 2 * half, 2 * half + 1
                P.wait("t", sGU2[idx % 2], 16 * (idx // 2 + 1))
                if e >= 2 and j == 0 and half == 0:
                    P.wait("t", cP, m_wd_mm[(e - 2, 7)])
                pkey = (e, j - 1, half) if j >= 1 else                        ((e - 1, 5, half) if e >= 1 else None)
                if pkey is not None:
                    P.wait("t", cV, m_hband[pkey])
                    P.wait("t", cA, m_sg[pkey])
                mgu = None
                for hc in range(HCN):
                    if e == 0 and j == 0:
                        P.wait("t", cA, m_x2b[hc])
                    P.op("t", lambda E, hc=hc, half=half, gt_=gt_, idx=idx:
                         E.matmul(p[gt_][:, :],
                                  wgu_buf[:, (idx % 2) * 4096 + hc * 256:
                                          (idx % 2) * 4096 + hc * 256 + 128],
                                  qf[:, hc * T + half * 512:
                                     hc * T + half * 512 + 512],
                                  start=(hc == 0), stop=(hc == HCN - 1)))
                    def bldu(E, hc=hc, half=half, ut_=ut_, idx=idx):
                        return E.matmul(
                            p[ut_][:, :],
                            wgu_buf[:, (idx % 2) * 4096 + hc * 256 + 128:
                                    (idx % 2) * 4096 + hc * 256 + 256],
                            qf[:, hc * T + half * 512:hc * T + half * 512 + 512],
                            start=(hc == 0), stop=(hc == HCN - 1))
                    if hc == HCN - 1:
                        mgu = P.op("t", bldu, cP)
                    else:
                        P.op("t", bldu)
                m_gu_mm[(e, j)] = mgu
                if half == 1 and e * 6 + j + 2 < NEXP * 6:
                    load_gu(e * 6 + j + 2, mgu)
                return mgu

            def emit_gu_evacs(e, j, half, mgu):
                ep = e % 2
                gt_, ut_ = 2 * half, 2 * half + 1
                P.wait("s", cP, mgu)
                P.op("s", lambda E, half=half, gt_=gt_:
                     E.activation(sg_sb[half][:, :], p[gt_][:, :], AF.Silu))
                msg = sfence()
                m_sg[(e, j, half)] = msg
                P.wait("v", cP, mgu)
                P.wait("v", cV, m_ceb[(e, half)])
                P.op("v", lambda E, half=half, ut_=ut_, ep=ep:
                     E.tensor_mul(usc_sb[half][:, :], p[ut_][:, :],
                                  ce_b[ep][:, half * 512:half * 512 + 512]))
                P.wait("v", cA, msg)
                P.op("v", lambda E, half=half, ep=ep, j=j:
                     E.tensor_mul(
                         h_buf[ep][:, j * T + half * 512:j * T + half * 512 + 512],
                         sg_sb[half][:, :], usc_sb[half][:, :]))
                m_hband[(e, j, half)] = vfence()

            def emit_gu_band(e, j, half):
                emit_gu_evacs(e, j, half, emit_gu_band_mms(e, j, half))

            mlg = None
            for half in range(2):
                for hc in range(HCN):
                    if half == 0:
                        P.wait("t", cA, m_x2b[hc])
                    def bld(E, hc=hc, half=half):
                        return E.matmul(
                            p[4 + half][0:32, :], gate_bf[:, hc * 32:hc * 32 + 32],
                            qf[:, hc * T + half * 512:hc * T + half * 512 + 512],
                            start=(hc == 0), stop=(hc == HCN - 1))
                    if hc == HCN - 1:
                        mlg = P.op("t", bld, cP)
                    else:
                        P.op("t", bld)
            P.wait("v", cP, mlg)
            P.op("v", lambda E: E.tensor_copy(lg_sb[:, 0:512], p[4][0:32, :]))
            P.op("v", lambda E: E.tensor_copy(lg_sb[:, 512:1024], p[5][0:32, :]))
            mlge = vfence()
            P.wait("t", cV, mlge)
            for tb in range(8):
                P.op("t", lambda E, tb=tb:
                     E.transpose(pt[0][0:128, tb * 32:tb * 32 + 32],
                                 lg_sb[0:32, tb * 128:tb * 128 + 128],
                                 cbf[0:32, C_ID:C_ID + 32]))
            mgt = pefence(lambda: p[1][0:1, 0:1])
            P.wait("v", cP, mgt)
            mgtm = P.op("v", lambda E: E.tensor_copy(g_tm[:, :], pt[0][:, 0:256]), cV)
            P.wait("v", cV, mgtm)

            m_cetr = {}
            mcv_prev = None
            for pair in range(4):
                tiles = (2 * pair, 2 * pair + 1)
                mm_ = [w32[0], w32[4]]
                wv_ = [w32[1], w32[5]]
                cnt_ = [w32[2], w32[6]]
                Gg_ = [w32[3], w32[7]]
                rr_ = [0, 4]

                def zip2(fn, inc=None):
                    out = None
                    for i_ in range(2):
                        b_ = fn(i_, tiles[i_])
                        if i_ == 1 and inc is not None:
                            out = P.op("v", b_, inc)
                        else:
                            P.op("v", b_)
                    return out

                zip2(lambda i, tb: (lambda E, i=i, tb=tb:
                     E.reduce_max(red[:, rr_[i]:rr_[i] + 1],
                                  g_tm[:, tb * 32:tb * 32 + 32], AX,
                                  negate=True)))
                mneg = vfence()
                P.wait("s", cV, mneg)
                for i_ in range(2):
                    P.op("s", lambda E, i=i_, tb=tiles[i_]:
                         E.activation(mm_[i][:, :],
                                      g_tm[:, tb * 32:tb * 32 + 32],
                                      AF.Exp, bias=red[:, rr_[i]:rr_[i] + 1]))
                mexp = sfence()
                P.wait("v", cA, mexp)
                zip2(lambda i, tb: (lambda E, i=i:
                     E.reduce_sum(red[:, rr_[i] + 1:rr_[i] + 2], mm_[i][:, :], AX)))
                zip2(lambda i, tb: (lambda E, i=i:
                     E.reciprocal(red[:, rr_[i] + 1:rr_[i] + 2],
                                  red[:, rr_[i] + 1:rr_[i] + 2])))
                zip2(lambda i, tb: (lambda E, i=i, tb=tb:
                     E.tensor_single_scalar(p_sb[:, tb * 32:tb * 32 + 32],
                                            mm_[i][:, :],
                                            red[:, rr_[i] + 1:rr_[i] + 2],
                                            OP.mult)))
                for e32 in range(NEXP):
                    zip2(lambda i, tb: (lambda E, i=i, tb=tb, e32=e32:
                         E.tensor_scalar(
                             mm_[i][:, :], p_sb[:, tb * 32:tb * 32 + 32],
                             p_sb[:, tb * 32 + e32:tb * 32 + e32 + 1], 0.0,
                             OP.is_ge, OP.add,
                             accum_out=cnt_[i][:, e32:e32 + 1])))
                    zip2(lambda i, tb: (lambda E, i=i, tb=tb, e32=e32:
                         E.scalar_tensor_tensor(
                             wv_[i][:, :], p_sb[:, tb * 32:tb * 32 + 32],
                             p_sb[:, tb * 32 + e32:tb * 32 + e32 + 1],
                             p_sb[:, tb * 32:tb * 32 + 32],
                             OP.is_ge, OP.mult,
                             accum_out=Gg_[i][:, e32:e32 + 1])))
                zip2(lambda i, tb: (lambda E, i=i:
                     E.tensor_single_scalar(mm_[i][:, :], cnt_[i][:, :], 8.0,
                                            OP.is_le)))
                zip2(lambda i, tb: (lambda E, i=i, tb=tb:
                     E.tensor_mul(wv_[i][:, :], mm_[i][:, :],
                                  p_sb[:, tb * 32:tb * 32 + 32])))
                zip2(lambda i, tb: (lambda E, i=i:
                     E.reduce_sum(red[:, rr_[i] + 2:rr_[i] + 3], wv_[i][:, :], AX)))
                zip2(lambda i, tb: (lambda E, i=i:
                     E.tensor_scalar_mul(red[:, rr_[i] + 2:rr_[i] + 3],
                                         red[:, rr_[i] + 2:rr_[i] + 3], 0.8)))
                zip2(lambda i, tb: (lambda E, i=i, tb=tb:
                     E.tensor_sub(Gg_[i][:, :], Gg_[i][:, :],
                                  p_sb[:, tb * 32:tb * 32 + 32])))
                zip2(lambda i, tb: (lambda E, i=i:
                     E.tensor_single_scalar(Gg_[i][:, :], Gg_[i][:, :],
                                            red[:, rr_[i] + 2:rr_[i] + 3],
                                            OP.is_lt)))
                zip2(lambda i, tb: (lambda E, i=i:
                     E.tensor_mul(mm_[i][:, :], mm_[i][:, :], Gg_[i][:, :])))
                zip2(lambda i, tb: (lambda E, i=i, tb=tb:
                     E.tensor_mul(wv_[i][:, :], mm_[i][:, :],
                                  p_sb[:, tb * 32:tb * 32 + 32])))
                zip2(lambda i, tb: (lambda E, i=i:
                     E.reduce_sum(red[:, rr_[i] + 3:rr_[i] + 4], wv_[i][:, :], AX)))
                zip2(lambda i, tb: (lambda E, i=i:
                     E.tensor_scalar_max(red[:, rr_[i] + 3:rr_[i] + 4],
                                         red[:, rr_[i] + 3:rr_[i] + 4], 1e-12)))
                zip2(lambda i, tb: (lambda E, i=i:
                     E.reciprocal(red[:, rr_[i] + 3:rr_[i] + 4],
                                  red[:, rr_[i] + 3:rr_[i] + 4])))
                zip2(lambda i, tb: (lambda E, i=i:
                     E.tensor_single_scalar(wv_[i][:, :], wv_[i][:, :],
                                            red[:, rr_[i] + 3:rr_[i] + 4],
                                            OP.mult)))
                if pair == 0:
                    mgu00 = emit_gu_band_mms(0, 0, 0)
                elif pair == 1:
                    mgu01 = emit_gu_band_mms(0, 0, 1)
                elif pair == 2:
                    emit_gu_evacs(0, 0, 0, mgu00)
                    mgu10 = emit_gu_band_mms(0, 1, 0)
                elif pair == 3:
                    mgu11 = None
                mce = vfence()
                P.wait("t", cV, mce)
                for i_ in range(2):
                    tb = tiles[i_]
                    P.wait("t", cV, mcv_prev)
                    P.op("t", lambda E, i=i_:
                         E.transpose(p[4][0:32, 0:128], wv_[i][:, :],
                                     cf[:, CF_ID:CF_ID + 128]))
                    mct = pefence(lambda: p[5][0:1, 400:401])
                    P.wait("v", cP, mct)
                    P.op("v", lambda E, tb=tb:
                         E.tensor_copy(ce_fm[0:32, tb * 128:tb * 128 + 128],
                                       p[4][0:32, 0:128]))
                    mcv = vfence()
                    m_cetr[tb] = mcv
                    mcv_prev = mcv
                if pair == 1:
                    emit_ce_half(0, 0, extra_tV=m_cetr[3])
                    mcv_prev = m_ceb[(0, 0)]
                elif pair == 3:
                    emit_ce_half(0, 1, extra_tV=m_cetr[7])
                    emit_gu_evacs(0, 0, 1, mgu01)
                    emit_gu_evacs(0, 1, 0, mgu10)
            m_ce_done = m_cetr[7]

            if "dbg_ce" in dbg_out:
                P.wait("y", cV, m_ce_done)
                P.op("y", lambda E: E.dma_start(out=dbg_out["dbg_ce"],
                                                in_=ce_fm[:, :]), sOUT, 16)

            for e in range(NEXP):
                ep = e % 2
                if e >= 1:
                    for half in range(2):
                        emit_ce_half(e, half,
                                     extra_tV=m_dn_ev[(e - 1, 15, 1)]
                                     if half == 0 else None)
                for j in range(6):
                    for half in range(2):
                        if e == 0 and (j == 0 or (j == 1 and half == 0)):
                            continue
                        emit_gu_band(e, j, half)
                m_h_done = max(m_hband[(e, j, hh)] for j in range(6)
                               for hh in range(2))

                for oc in range(16):
                    qq = oc // 2
                    idx = e * 8 + qq
                    for half in range(2):
                        dt_ = 4 + half
                        P.wait("t", sWD2[idx % 2], 16 * (idx // 2 + 1))
                        P.wait("t", cV, m_h_done)
                        pkey = (e, oc - 1, half) if oc >= 1 else \
                               ((e - 1, 15, half) if e >= 1 else None)
                        if pkey is not None and pkey in m_dn_ev:
                            P.wait("t", cV, m_dn_ev[pkey])
                        woff = (idx % 2) * 1536 + (oc % 2) * 768
                        mdn = None
                        for ic in range(6):
                            def bld(E, ic=ic, woff=woff, half=half, dt_=dt_, ep=ep):
                                return E.matmul(
                                    p[dt_][:, :],
                                    wd_buf[:, woff + ic * 128:woff + ic * 128 + 128],
                                    h_buf[ep][:, ic * T + half * 512:
                                              ic * T + half * 512 + 512],
                                    start=(ic == 0), stop=(ic == 5))
                            if ic == 5:
                                mdn = P.op("t", bld, cP)
                            else:
                                P.op("t", bld)
                        if half == 1 and oc % 2 == 1:
                            m_wd_mm[(e, qq)] = mdn
                            if e * 8 + qq + 2 < NEXP * 8:
                                load_wd(e * 8 + qq + 2, mdn)
                        P.wait("v", cP, mdn)
                        if e == NEXP - 1:
                            P.op("v", lambda E, oc=oc, half=half, dt_=dt_:
                                 E.tensor_add(
                                     resid[:, oc * T + half * 512:
                                           oc * T + half * 512 + 512],
                                     resid[:, oc * T + half * 512:
                                           oc * T + half * 512 + 512],
                                     p[dt_][:, :]))
                            mde = vfence()
                        else:
                            mde = P.op("v", lambda E, oc=oc, half=half, dt_=dt_:
                                       E.tensor_add(
                                           resid[:, oc * T + half * 512:
                                                 oc * T + half * 512 + 512],
                                           resid[:, oc * T + half * 512:
                                                 oc * T + half * 512 + 512],
                                           p[dt_][:, :]), cV)
                        m_dn_ev[(e, oc, half)] = mde

            m_dn_all = m_dn_ev[(NEXP - 1, 15, 1)]
            m_sg_all = m_sg[(NEXP - 1, 5, 1)]

            prev_sq_pe = None
            m_sqm2 = {}
            P.wait("t", cV, max(m_hband[(NEXP - 1, 5, hh)] for hh in range(2)))
            P.wait("t", cA, m_sg_all)
            for ec in range(HCN):
                for half in range(2):
                    mlast = m_dn_ev[(NEXP - 1, ec, half)]
                    P.wait("s", cV, mlast)
                    P.wait("s", cP, prev_sq_pe)
                    P.op("s", lambda E, ec=ec, half=half:
                         E.activation(sq_sb[:, :],
                                      resid[:, ec * T + half * 512:
                                            ec * T + half * 512 + 512],
                                      AF.Square))
                    msqa = sfence()
                    P.wait("t", cV, mlast)
                    P.op("t", lambda E, ec=ec, half=half:
                         E.matmul(p[half][0:1, :], cf[:, CF_ONE:CF_ONE + 1],
                                  resid[:, ec * T + half * 512:ec * T + half * 512 + 512],
                                  start=(ec == 0), stop=(ec == 15)))
                    P.wait("t", cA, msqa)
                    msq2 = P.op("t", lambda E, half=half, ec=ec:
                                E.matmul(p[2 + half][0:1, :], cf[:, CF_ONE:CF_ONE + 1],
                                         sq_sb[:, :],
                                         start=(ec == 0), stop=(ec == 15)), cP)
                    m_sqm2[(ec, half)] = msq2
                    prev_sq_pe = msq2
            m_stats2 = max(m_sqm2.values())
            m_out = ln_finalize_apply(m_stats2, False, sum_b=0, nm_b=4)
            for ec in range(HCN):
                P.wait("y", cV, m_out[ec])
                P.op("y", lambda E, ec=ec: E.dma_start(
                    out=d_out[ec * 128:(ec + 1) * 128, :],
                    in_=resid[:, ec * T:(ec + 1) * T]), sOUT, 16)
            P.wait("y", sOUT, sOUT.n)

        with nc.Block() as block:
            @block.tensor
            def _(t):
                for f in P.q["t"]:
                    f(t)

            @block.vector
            def _(v):
                for f in P.q["v"]:
                    f(v)

            @block.scalar
            def _(s_):
                for f in P.q["s"]:
                    f(s_)

            @block.gpsimd
            def _(g):
                for f in P.q["g"]:
                    f(g)

            @block.sync
            def _(y):
                for f in P.q["y"]:
                    f(y)

    return nc



def _tile_T(blk):
    K = blk.shape[1]
    n = K // 128
    return np.ascontiguousarray(
        blk.T.reshape(n, 128, 128).transpose(1, 0, 2).reshape(128, n * 128))


def _host_prep(x, wqkv, wo, gw, wgu, wd):
    x = np.asarray(x, np.float32)
    wqkv = np.asarray(wqkv, np.float32)
    wo = np.asarray(wo, np.float32)
    gw = np.asarray(gw, np.float32)
    wgu = np.asarray(wgu, np.float32)
    wd = np.asarray(wd, np.float32)

    tiles = []
    for h in range(16):
        tiles.append(_tile_T(wqkv[4096 + h * 128:4096 + (h + 1) * 128, :]))
    for h in range(16):
        tiles.append(_tile_T(wqkv[2048 + h * 128:2048 + (h + 1) * 128, :]))
    for h in range(16):
        tiles.append(_tile_T(wqkv[h * 128:(h + 1) * 128, :]))
    wkvq_t = np.stack(tiles)
    wout_t = np.stack([_tile_T(wo[ob * 128:(ob + 1) * 128, :])
                       for ob in range(16)])
    gate_t = np.ascontiguousarray(
        gw.T.reshape(16, 128, 32).transpose(1, 0, 2).reshape(128, 512))
    wgu_t = np.empty((NEXP, 6, 128, 4096), np.float32)
    for e in range(NEXP):
        for j in range(6):
            tg = _tile_T(wgu[e, j * 128:(j + 1) * 128, :]).reshape(128, 16, 128)
            tu = _tile_T(wgu[e, 768 + j * 128:768 + (j + 1) * 128, :]
                         ).reshape(128, 16, 128)
            wgu_t[e, j] = np.concatenate([tg, tu], axis=2).reshape(128, 4096)
    wd_t = np.empty((NEXP, 128, 12288), np.float32)
    for e in range(NEXP):
        cols = [_tile_T(wd[e, oc * 128:(oc + 1) * 128, :]) for oc in range(16)]
        wd_t[e] = np.concatenate(cols, axis=1)

    cbf = np.zeros((128, CBF_W), np.float32)
    cbf[:, C_ID:C_ID + 128] = np.eye(128)
    RT = np.zeros((128, 128), np.float32)
    for i in range(64):
        RT[2 * i + 1, 2 * i] = -1.0
        RT[2 * i, 2 * i + 1] = 1.0
    cbf[:, C_RT:C_RT + 128] = RT
    l_idx = np.arange(128)[:, None]
    m_idx = np.arange(256)[None, :]
    cbf[:, C_M0:C_M0 + 256] = (m_idx <= l_idx).astype(np.float32)
    cbf[:, C_M1:C_M1 + 256] = (m_idx <= l_idx + 128).astype(np.float32)
    cbf[:, C_ONE] = 1.0
    m_idx2 = np.arange(256)[None, :]
    p_idx = np.arange(128)[:, None]
    cbf[:, C_MT0:C_MT0 + 256] = (m_idx2 >= p_idx).astype(np.float32)
    cbf[:, C_MT1:C_MT1 + 256] = (m_idx2 >= p_idx + 128).astype(np.float32)
    for e in range(NEXP):
        cbf[e, C_SEL + e * 128:C_SEL + (e + 1) * 128] = 1.0
    cfc = np.zeros((128, CF_W), np.float32)
    cfc[:, CF_ID:CF_ID + 128] = np.eye(128)
    cfc[:, CF_ONE] = 1.0
    cfc[0, CF_ROW:CF_ROW + 128] = 1.0
    cfc[0, CF_EPS] = EPS

    inv_freq = 1.0 / (10000.0 ** (np.arange(0, 128, 2, dtype=np.float32) / 128))
    pos = np.arange(S, dtype=np.float32)
    freqs = np.outer(pos, inv_freq)
    cos_full = np.cos(freqs).astype(np.float32)
    sin_full = np.sin(freqs).astype(np.float32)

    maps = []
    for core in range(NCORES):
        b, pp = divmod(core, 4)
        t0 = pp * T
        xs = np.ascontiguousarray(x[b, t0:t0 + T, :].T)
        cos_i = np.ascontiguousarray(
            np.concatenate([cos_full[t0:t0 + T], cos_full[t0:t0 + T]], axis=1).T)
        sin_i = np.ascontiguousarray(
            np.concatenate([sin_full[t0:t0 + T], sin_full[t0:t0 + T]], axis=1).T)
        pm = np.zeros((128, 8), np.float32)
        pm[:, b * 4:b * 4 + pp] = 1.0
        maps.append(dict(x=xs, cos=cos_i, sin=sin_i, pmask=pm, cbf=cbf, cf=cfc,
                         gate=gate_t, wkvq=wkvq_t, wout=wout_t, wgu=wgu_t,
                         wd=wd_t))
    return maps


def kernel(L0_attn_x, L0_attn_W_qkv, L0_attn_W_out, L0_ffn_gate_weight,
           L0_ffn_gate_up_proj, L0_ffn_down_proj):
    maps = _host_prep(L0_attn_x, L0_attn_W_qkv, L0_attn_W_out,
                      L0_ffn_gate_weight, L0_ffn_gate_up_proj,
                      L0_ffn_down_proj)
    nc = build_program()
    res = run_bass_kernel_spmd(nc, maps, core_ids=list(range(NCORES)))
    out = np.empty((NB, S, EMB), np.float32)
    for core in range(NCORES):
        b, pp = divmod(core, 4)
        out[b, pp * T:(pp + 1) * T, :] = np.asarray(res.results[core]["out"]).T
    return out
